# version 01
# baseline (speedup 1.0000x reference)
import numpy as np
import contextlib
import concourse.bass as bass
import concourse.mybir as mybir
from concourse.bass_utils import run_bass_kernel_spmd
from concourse.bass_core_rust import IndirectOffsetOnAxis

F32 = mybir.dt.float32
BF16 = mybir.dt.bfloat16
U32 = mybir.dt.uint32
I8 = mybir.dt.int8
AF = mybir.ActivationFunctionType
ALU = mybir.AluOpType
AXX = mybir.AxisListType.X

NC, S, T, D = 8, 2048, 16384, 1024
XT, AT, XZT = 20, 18, 17
XCOLS, ACOLS, XZCOLS = 2560, 2304, 2176
MAGIC = 12582912.0
NEG = -1e30

DEBUG = []



def _host_stage(inputs):
    g = lambda k: np.asarray(inputs[k], dtype=np.float32)
    x = np.ascontiguousarray(g("L0_attn_x").reshape(T, D))
    rep = lambda v: np.ascontiguousarray(np.tile(np.asarray(v, np.float32).reshape(1, -1), (128, 1)))
    shared = {
        "w_qkv": g("L0_attn_wqkv_l"), "w_ol": g("L0_attn_wo_l"),
        "w_qh": g("L0_attn_wq_h"), "w_kvh": g("L0_attn_wkv_h"), "w_oh": g("L0_attn_wo_h"),
        "gam_h": np.ascontiguousarray(g("L0_attn_gamma_h").reshape(8, 128).T),
        "rtq_rep": rep(g("L0_attn_rt_q")), "rtkv_rep": rep(g("L0_attn_rt_kv")),
        "lngb": np.concatenate([rep(g("L0_attn_ln_g")), rep(g("L0_attn_ln_b"))], 1),
        "w1T": np.ascontiguousarray(g("L0_ffn_w1").T),
        "w2T": np.ascontiguousarray(g("L0_ffn_w2").T),
        "cw0r": np.ascontiguousarray(g("L0_ffn_conv_w").reshape(32, 128, 3).transpose(1, 0, 2).reshape(128, 96)),
        "cb0r": np.ascontiguousarray(g("L0_ffn_conv_b").reshape(32, 128).T),
        "w_in_z": np.ascontiguousarray(g("L1_attn_W_in")[:, 0:2048]),
        "w_in_fm": None,
        "cw1r": np.ascontiguousarray(g("L1_attn_conv_w").reshape(17, 128, 4).transpose(1, 0, 2).reshape(128, 68)),
        "cb1r": np.ascontiguousarray(g("L1_attn_conv_b").reshape(17, 128).T),
        "a_log": g("L1_attn_A_log").reshape(32, 1), "d_par": g("L1_attn_D_param").reshape(32, 1),
        "dt_b": g("L1_attn_dt_bias").reshape(32, 1), "nrm_w": rep(g("L1_attn_norm_w")),
        "w_out": g("L1_attn_W_out"),
        "moe_w1": np.ascontiguousarray(g("L1_ffn_W1").reshape(8 * 1024, 4096)),
        "moe_b1": g("L1_ffn_b1"),
        "moe_w2": np.ascontiguousarray(g("L1_ffn_W2").reshape(8 * 4096, 1024)),
        "moe_b2": g("L1_ffn_b2"),
        "w_r": g("L1_ffn_Wr"), "b_r": g("L1_ffn_br").reshape(1, 8),
        "gamma": rep(g("L1_ffn_gamma")), "beta": rep(g("L1_ffn_beta")),
        "xfull": x,
        "iota_r": np.tile(np.arange(1024, dtype=np.float32)[None, :], (128, 1)),
    }
    w_in = g("L1_attn_W_in")
    bc = w_in[:, 2048:4224].reshape(8, 128, 17, 128).transpose(1, 0, 2, 3).reshape(128, 8 * 17 * 128)
    dtp = np.zeros((128, 8, 128), np.float32)
    dtp[:, :, 0:32] = w_in[:, 4224:4256].reshape(8, 128, 32).transpose(1, 0, 2)
    shared["w_in_fm"] = np.ascontiguousarray(np.concatenate([bc, dtp.reshape(128, 1024)], 1))
    c2 = np.zeros((128, 768), np.float32)
    c2[:, 0:128] = (np.arange(128)[:, None] < np.arange(128)[None, :])
    c2[:, 256:384] = np.where(np.arange(128)[:, None] > np.arange(128)[None, :], NEG, 0.0)
    c2[:, 384:512] = np.eye(128)
    c2[:, 512] = 1.0
    c2[0, 640:768] = 1.0
    cb = np.zeros((128, 1024), np.float32)
    cb[0, 0:64] = 1.0; cb[1, 64:128] = 1.0; cb[2, 0:128] = 1.0
    cb[:, 128:256] = np.eye(128)
    cb[:, 256] = 1.0
    cb[0, 257:385] = 1.0
    cb[:, 385:513] = np.tile(np.arange(128, dtype=np.float32)[None, :], (128, 1))
    cb[:, 513] = np.arange(128) * 128.0
    cb[:, 514:642] = (np.arange(128)[:, None] <= np.arange(128)[None, :])
    cb[:, 642:770] = 1.0
    shared.update({"cst2c": c2, "cstbc": cb,
                   "hsel": np.repeat(np.eye(32, dtype=np.float32), 64, 1)})

    per_core = []
    for c in range(NC):
        start = c * S
        xrows = np.clip(np.arange(start - 256, start - 256 + XCOLS), 0, T - 1)
        maskV = np.zeros((AT, 4, 256), np.float32)
        for t in range(AT):
            maskV[t, 0, 192:] = NEG
            maskV[t, 1, :64] = NEG
            gcols = start - 192 + 128 * t + np.arange(256)
            maskV[t, 2, (gcols < 0) | (gcols >= T)] = NEG
        tokrow = np.tile((start - 128 + np.arange(ACOLS, dtype=np.float32))[None, :], (128, 1))
        PP, FF = np.meshgrid(np.arange(128), np.arange(128), indexing="ij")
        gtok = (PP // 16) * 2048 + (FF % 16) * 128 + (PP % 16) * 8 + FF // 16
        ghi = (PP // 16) * 16 + (FF % 16)
        glo = (PP % 16) * 8 + FF // 16
        hmask = np.ones((128, 2), np.float32)
        if c == 0:
            hmask[:, 0] = 0.0
        if c == NC - 1:
            hmask[:, 1] = 0.0
        xzmask = np.zeros((128, 1), np.float32) if c == 0 else np.ones((128, 1), np.float32)
        m8 = np.zeros((1, 8), np.float32); m8[0, c] = 1.0
        pm8 = np.zeros((128, 8), np.float32); pm8[:, :c] = 1.0
        per_core.append({"xloc": np.ascontiguousarray(x[xrows]),
                         "maskV": np.ascontiguousarray(maskV), "tokrow": tokrow,
                         "gtok": gtok.astype(np.float32), "ghi": ghi.astype(np.float32),
                         "glo": glo.astype(np.float32), "hmask": hmask,
                         "xzmask": xzmask, "m8": m8, "pm8": pm8})
    return shared, per_core



class Prog:
    def __init__(self):
        self.phases = []

    def dma(self, eng, dst, src, slow=False):
        self.phases.append((eng, "dma", [(dst, src, slow)]))

    def dmas(self, eng, pairs, slow=False):
        self.phases.append((eng, "dma", [(d, s, slow) for d, s in pairs]))

    def op(self, eng, fn):
        self.phases.append((eng, "op", fn))

    def op16(self, eng, fn):
        self.phases.append((eng, "op16", fn))

    def emit(self, blk, sem):
        starts, c = [], 0
        for eng, kind, payload in self.phases:
            starts.append(c)
            c += 16 * len(payload) if kind == "dma" else (16 if kind == "op16" else 1)
        total = c

        def run(engobj, engname):
            for (eng, kind, payload), s0 in zip(self.phases, starts):
                if eng != engname:
                    continue
                engobj.wait_ge(sem, s0)
                if kind == "dma":
                    for dst, src, slow in payload:
                        engobj.dma_start(dst, src, allow_slow_non_contiguous=slow).then_inc(sem, 16)
                elif kind == "op16":
                    payload(engobj).then_inc(sem, 16)
                else:
                    last = payload(engobj)
                    (engobj.nop() if last is None else last).then_inc(sem, 1)

        @blk.sync
        def _(e): run(e, "sync")
        @blk.vector
        def _(e): run(e, "vector")
        @blk.scalar
        def _(e): run(e, "scalar")
        @blk.tensor
        def _(e): run(e, "tensor")
        @blk.gpsimd
        def _(e):
            run(e, "gpsimd")
            e.wait_ge(sem, total)


IN_SHAPES = [
    ("w_qkv", (1024, 1536)), ("w_ol", (512, 1024)), ("w_qh", (1024, 512)),
    ("w_kvh", (1024, 1024)), ("w_oh", (512, 1024)), ("gam_h", (128, 8)),
    ("rtq_rep", (128, D)), ("rtkv_rep", (128, D)), ("lngb", (128, 2048)),
    ("w1T", (1024, 4096)), ("w2T", (4096, 1024)), ("cw0r", (128, 96)), ("cb0r", (128, 32)),
    ("w_in_z", (1024, 2048)), ("w_in_fm", (128, 18432)), ("cw1r", (128, 68)), ("cb1r", (128, 17)),
    ("a_log", (32, 1)), ("d_par", (32, 1)), ("dt_b", (32, 1)), ("nrm_w", (128, 2048)),
    ("w_out", (2048, 1024)), ("moe_w1", (8 * 1024, 4096)), ("moe_b1", (8, 4096)),
    ("moe_w2", (8 * 4096, 1024)), ("moe_b2", (8, 1024)), ("w_r", (1024, 8)), ("b_r", (1, 8)),
    ("gamma", (128, D)), ("beta", (128, D)), ("xfull", (T, D)),
    ("iota_r", (128, 1024)), ("cst2c", (128, 768)), ("cstbc", (128, 1024)),
    ("hsel", (32, 2048)),
    ("xloc", (XCOLS, D)), ("maskV", (AT, 4, 256)), ("tokrow", (128, ACOLS)),
    ("gtok", (128, 128)), ("ghi", (128, 128)), ("glo", (128, 128)),
    ("hmask", (128, 2)), ("xzmask", (128, 1)), ("m8", (1, 8)), ("pm8", (128, 8)),
]


def build_program(debug=None):
    nc = bass.Bass("TRN2", target_bir_lowering=False)
    IN = {n: nc.dram_tensor(n, list(sh), F32, kind="ExternalInput").ap() for n, sh in IN_SHAPES}
    out = nc.dram_tensor("out", [S, D], F32, kind="ExternalOutput").ap()
    dbg_aps = {}
    dr = lambda n, sh, dt=F32: nc.dram_tensor(n, list(sh), dt).ap()
    DSC = {
        "y1": dr("d_y1", (ACOLS, D)),
        "z": dr("d_z", (XZCOLS, 2048), BF16),
        "hvt": dr("d_hvt", (1024, 1024), BF16),
        "hq": dr("d_hq", (AT, 128, 32, 128), I8),
        "y2": dr("d_y2", (ACOLS, D)),
        "invs": dr("d_invs", (2305, 1)),
        "Y": dr("d_Y", (S, 2048), BF16),
        "cc": dr("d_cc", (16, 64, 128), BF16),
        "fac": dr("d_fac", (16, 128, 32)),
        "y3": dr("d_y3", (S, D)),
        "ln3F": dr("d_ln3F", (16, 8, 128, 128), BF16),
        "acc": dr("d_acc", (S, D)),
    }
    CC = {
        "sc_in": nc.dram_tensor("cc_sc_in", [1, 2 * S], F32),
        "sc_out": nc.dram_tensor("cc_sc_out", [NC, 2 * S], F32),
        "mb_in": nc.dram_tensor("cc_mb_in", [1, 32 * 64 * 64 + 64], F32),
        "mb_out": nc.dram_tensor("cc_mb_out", [NC, 32 * 64 * 64 + 64], F32),
        "mu_in": nc.dram_tensor("cc_mu_in", [1, D], F32),
        "mu_out": nc.dram_tensor("cc_mu_out", [1, D], F32),
    }
    ctx = contextlib.ExitStack()
    with ctx:
        sb = lambda n, sh, dt=F32: ctx.enter_context(nc.sbuf_tensor(n, list(sh), dt))
        ps = lambda n, sh, dt=F32: ctx.enter_context(nc.psum_tensor(n, list(sh), dt))
        B = {
            "r": [sb(f"r{i}", (128, D)) for i in range(XT)],
            "bq": [sb(f"bq{i}", (128, 2304), BF16) for i in range(4)],
            "bk": [sb(f"bk{i}", (128, 2432), BF16) for i in range(4)],
            "bv": [sb(f"bv{i}", (128, 2560), BF16) for i in range(4)],
            "bx": [sb(f"bx{i}", (128, 2560), BF16) for i in range(2)],
            "wb": [sb(f"wb{i}", (128, 4096), BF16) for i in range(3)],
            "st": sb("st", (128, 512)),
            "st2": sb("st2", (128, 512)),
            "cst": sb("cst", (128, 1024)),
            "cst2": sb("cst2", (128, 768)),
            "cstb": sb("cstb", (128, 1024), BF16),
            "repa": sb("repa", (128, 1024)),
            "repb": sb("repb", (128, 1024)),
            "idx": sb("idx", (128, 32), U32),
            "smal": sb("smal", (128, 256)),
            "pb": [ps(f"pb{i}", (128, 512)) for i in range(6)],
            "pbt": ps("pbt", (128, 512), BF16),
        }
        sem = ctx.enter_context(nc.semaphore("S"))
        with nc.Block() as blk:
            P = Prog()
            EM = Emitter(nc, P, IN, out, DSC, CC, B, debug or [], dbg_aps)
            EM.forward()
            P.emit(blk, sem)
    return nc, dbg_aps


class Emitter:
    def __init__(self, nc, P, IN, out, DSC, CC, B, debug, dbg_aps):
        self.nc, self.P, self.IN, self.out = nc, P, IN, out
        self.DSC, self.CC = DSC, CC
        self.debug, self.dbg_aps = debug, dbg_aps
        for k2, v2 in B.items():
            setattr(self, k2, v2)

    def dump(self, name, src_ap, shape, dt=F32):
        if name in self.debug:
            d = self.nc.dram_tensor("dbg_" + name, list(shape), dt, kind="ExternalOutput").ap()
            self.dbg_aps[name] = d
            self.P.dma("scalar", d, src_ap)

    def v(self, fn): self.P.op("vector", fn)
    def s(self, fn): self.P.op("scalar", fn)
    def t(self, fn): self.P.op("tensor", fn)
    def gp(self, fn): self.P.op("gpsimd", fn)

    def rsqrt(self, dst, src):
        eps = self.smal[:, 255:256]
        self.s(lambda e: e.activation(dst, src, AF.Sqrt, bias=eps))
        self.v(lambda e: e.reciprocal(dst, dst))

    @property
    def ones_row(self): return self.cst2[0:1, 640:768]
    @property
    def ones_col(self): return self.cst2[:, 512:513]
    @property
    def onesb_col(self): return self.cstb[:, 256:257]
    @property
    def onesb_row(self): return self.cstb[0:1, 257:385]
    @property
    def eyeb(self): return self.cstb[:, 128:256]

    def ptot(self, cnt_cols, out_cols, n):
        p = self.pb[5]
        self.t(lambda e: e.matmul(p[0:1, 0:n], self.ones_col, cnt_cols, start=True, stop=True))
        self.v(lambda e: e.tensor_copy(self.smal[0:1, 240:240 + n], p[0:1, 0:n]))
        self.t(lambda e: e.matmul(p[:, 4:4 + n], self.ones_row, self.smal[0:1, 240:240 + n], start=True, stop=True))
        self.v(lambda e: e.tensor_copy(out_cols, p[:, 4:4 + n]))

    def forward(self):
        import os
        lim = int(os.environ.get("KSTAGE", "99"))
        stages = [self.setup, self.stageA_ln_scores, self.stageB_routing,
                  self.stageC_heavy, self.stageD_qkv, self.stageE_light,
                  self.stageF_l0ffn, self.stageG_mamba]
        for i, fn in enumerate(stages):
            if i <= lim:
                fn()
        self.stageH_moe()

    def setup(self):
        P, IN = self.P, self.IN
        P.dmas("sync", [(self.cst[:, :], IN["iota_r"]), (self.cst2[:, :], IN["cst2c"])])
        P.op16("gpsimd", lambda e: e.dma_start(self.cstb[:, :], IN["cstbc"]))
        P.dmas("sync", [(self.r[i][:, :], IN["xloc"][128 * i:128 * (i + 1), :]) for i in range(XT)])
        self.v(lambda e: (e.memset(self.smal[:, 255:256], 1e-5),
                          e.memset(self.smal[:, 254:255], 0.0))[-1])

    def stageA_ln_scores(self):
        P, IN = self.P, self.IN
        st, st2 = self.st, self.st2
        P.dmas("sync", [(self.repa[:, :], IN["rtq_rep"]), (self.repb[:, :], IN["rtkv_rep"])])
        scr = self.bq[0][:, 0:2048].bitcast(F32)
        for i in range(2, 18):
            self.v(lambda e, i=i: e.tensor_tensor(scr[:, 0:1024], self.r[i][:, :], self.repa[:, :], op=ALU.mult))
            self.v(lambda e, i=i: e.reduce_sum(st2[:, 128 + i - 2:129 + i - 2], scr[:, 0:1024], axis=AXX))
            self.v(lambda e, i=i: e.tensor_tensor(scr[:, 0:1024], self.r[i][:, :], self.repb[:, :], op=ALU.mult))
            self.v(lambda e, i=i: e.reduce_sum(st2[:, 160 + i - 2:161 + i - 2], scr[:, 0:1024], axis=AXX))
        self.v(lambda e: e.reduce_sum(st2[0:1, 6:7], self.repa[0:1, :], axis=AXX))
        self.v(lambda e: e.reduce_sum(st2[0:1, 7:8], self.repb[0:1, :], axis=AXX))
        p5 = self.pb[5]
        self.t(lambda e: e.matmul(p5[:, 0:2], self.ones_row, st2[0:1, 6:8], start=True, stop=True))
        self.v(lambda e: e.tensor_copy(st2[:, 4:6], p5[:, 0:2]))

        for i in range(XT):
            ri, c0 = self.r[i], i
            m0 = st[:, c0:c0 + 1]
            r0 = st[:, 20 + c0:21 + c0]
            al = st[:, 40 + c0:41 + c0]
            be = st[:, 60 + c0:61 + c0]
            self.v(lambda e, ri=ri, m0=m0: e.reduce_sum(m0, ri[:, :], axis=AXX))
            self.v(lambda e, m0=m0: e.tensor_scalar(m0, m0, 1.0 / D, None, op0=ALU.mult))
            self.v(lambda e, ri=ri, m0=m0: e.tensor_scalar(self.repa[:, :], ri[:, :], m0, None, op0=ALU.subtract))
            self.s(lambda e: e.activation(self.repb[:, :], self.repa[:, :], AF.Square, accum_out=st2[:, 0:1]))
            self.v(lambda e: e.tensor_scalar(st2[:, 0:1], st2[:, 0:1], 1.0 / D, None, op0=ALU.mult))
            self.rsqrt(r0, st2[:, 0:1])
            if 2 <= i < 18:
                for rr, (xc, sc) in enumerate([(128, 64), (160, 96)]):
                    xd = st2[:, xc + i - 2:xc + 1 + i - 2]
                    so = st2[:, sc + i - 2:sc + 1 + i - 2]
                    self.v(lambda e, m0=m0, rr=rr: e.tensor_tensor(st2[:, 1:2], m0, st2[:, 4 + rr:5 + rr], op=ALU.mult))
                    self.v(lambda e, xd=xd: e.tensor_tensor(st2[:, 1:2], xd, st2[:, 1:2], op=ALU.subtract))
                    self.v(lambda e, so=so, r0=r0: e.tensor_tensor(so, st2[:, 1:2], r0, op=ALU.mult))
            self.v(lambda e, r0=r0: e.tensor_scalar(self.repa[:, :], self.repa[:, :], r0, None, op0=ALU.mult))
            self.v(lambda e: e.reduce_sum(st2[:, 1:2], self.repa[:, :], axis=AXX))
            self.v(lambda e: e.tensor_scalar(st2[:, 1:2], st2[:, 1:2], 1.0 / D, None, op0=ALU.mult))
            self.v(lambda e: e.tensor_scalar(self.repb[:, :], self.repa[:, :], st2[:, 1:2], None, op0=ALU.subtract))
            self.s(lambda e: e.activation(self.repb[:, :], self.repb[:, :], AF.Square, accum_out=st2[:, 2:3]))
            self.v(lambda e: e.tensor_scalar(st2[:, 2:3], st2[:, 2:3], 1.0 / D, None, op0=ALU.mult))
            self.rsqrt(st2[:, 3:4], st2[:, 2:3])
            self.v(lambda e, r0=r0, al=al: e.tensor_tensor(al, r0, st2[:, 3:4], op=ALU.mult))
            self.v(lambda e, m0=m0, r0=r0, be=be: e.tensor_tensor(be, m0, r0, op=ALU.mult))
            self.v(lambda e, be=be: e.tensor_tensor(be, be, st2[:, 1:2], op=ALU.add))
            self.v(lambda e, be=be: e.tensor_tensor(be, be, st2[:, 3:4], op=ALU.mult))
            self.v(lambda e, be=be: e.tensor_scalar(be, be, -1.0, None, op0=ALU.mult))
        self.dump("st", st[:, :], (128, 512))
        self.dump("st2", st2[:, :], (128, 512))
        self.dump("scq", st2[:, 64:80], (128, 16))
        self.dump("sckv", st2[:, 96:112], (128, 16))

    def stageB_routing(self):
        P, IN, CC = self.P, self.IN, self.CC
        st2, smal, idx = self.st2, self.smal, self.idx
        P.dma("sync", CC["sc_in"].ap()[0:1, 0:2048].rearrange("a (p c) -> (a p) c", p=128), st2[:, 64:80])
        P.dma("sync", CC["sc_in"].ap()[0:1, 2048:4096].rearrange("a (p c) -> (a p) c", p=128), st2[:, 96:112])
        self.gp(lambda e: e.collective_compute(
            "AllGather", ALU.bypass, replica_groups=[list(range(NC))],
            ins=[CC["sc_in"].ap().opt()], outs=[CC["sc_out"].ap().opt()]))
        P.dmas("sync", [(self.repa[16 * c:16 * (c + 1), 0:128],
                         CC["sc_out"].ap()[c:c + 1, 0:2048].rearrange("a (p c) -> (a p) c", p=16))
                        for c in range(NC)])
        P.dmas("sync", [(self.repb[16 * c:16 * (c + 1), 0:128],
                         CC["sc_out"].ap()[c:c + 1, 2048:4096].rearrange("a (p c) -> (a p) c", p=16))
                        for c in range(NC)])
        self.dump("scall", self.repa[:, 0:128], (128, 128))
        def binit(e):
            e.memset(smal[:, 0:1], -4096.0)
            e.memset(smal[:, 1:2], 4096.0)
            e.memset(smal[:, 2:3], -4096.0)
            return e.memset(smal[:, 3:4], 4096.0)
        self.v(binit)
        scr = self.bq[1][:, 0:512].bitcast(F32)
        for it in range(40):
            def half1(e):
                e.tensor_tensor(smal[:, 4:5], smal[:, 0:1], smal[:, 1:2], op=ALU.add)
                return e.tensor_tensor(smal[:, 5:6], smal[:, 2:3], smal[:, 3:4], op=ALU.add)
            self.v(half1)
            def half2(e):
                e.tensor_scalar(smal[:, 4:5], smal[:, 4:5], 0.5, None, op0=ALU.mult)
                return e.tensor_scalar(smal[:, 5:6], smal[:, 5:6], 0.5, None, op0=ALU.mult)
            self.v(half2)
            def cmps(e):
                e.tensor_scalar(scr[:, 0:128], self.repa[:, 0:128], smal[:, 4:5], None, op0=ALU.is_gt)
                return e.tensor_scalar(scr[:, 128:256], self.repb[:, 0:128], smal[:, 5:6], None, op0=ALU.is_gt)
            self.v(cmps)
            def reds(e):
                e.reduce_sum(smal[:, 8:9], scr[:, 0:128], axis=AXX)
                return e.reduce_sum(smal[:, 9:10], scr[:, 128:256], axis=AXX)
            self.v(reds)
            self.ptot(smal[:, 8:10], smal[:, 10:12], 2)
            def ges(e):
                e.tensor_scalar(smal[:, 12:13], smal[:, 10:11], 1024.0, None, op0=ALU.is_ge)
                return e.tensor_scalar(smal[:, 13:14], smal[:, 11:12], 1024.0, None, op0=ALU.is_ge)
            self.v(ges)
            def d1(e):
                e.tensor_tensor(smal[:, 14:15], smal[:, 4:5], smal[:, 0:1], op=ALU.subtract)
                e.tensor_tensor(smal[:, 15:16], smal[:, 1:2], smal[:, 4:5], op=ALU.subtract)
                e.tensor_tensor(smal[:, 16:17], smal[:, 5:6], smal[:, 2:3], op=ALU.subtract)
                return e.tensor_tensor(smal[:, 17:18], smal[:, 3:4], smal[:, 5:6], op=ALU.subtract)
            self.v(d1)
            def d2(e):
                e.tensor_tensor(smal[:, 14:15], smal[:, 14:15], smal[:, 12:13], op=ALU.mult)
                e.tensor_tensor(smal[:, 15:16], smal[:, 15:16], smal[:, 12:13], op=ALU.mult)
                e.tensor_tensor(smal[:, 16:17], smal[:, 16:17], smal[:, 13:14], op=ALU.mult)
                return e.tensor_tensor(smal[:, 17:18], smal[:, 17:18], smal[:, 13:14], op=ALU.mult)
            self.v(d2)
            def d3(e):
                e.tensor_tensor(smal[:, 0:1], smal[:, 0:1], smal[:, 14:15], op=ALU.add)
                e.tensor_tensor(smal[:, 1:2], smal[:, 4:5], smal[:, 15:16], op=ALU.add)
                e.tensor_tensor(smal[:, 2:3], smal[:, 2:3], smal[:, 16:17], op=ALU.add)
                return e.tensor_tensor(smal[:, 3:4], smal[:, 5:6], smal[:, 17:18], op=ALU.add)
            self.v(d3)
        self.dump("thr", smal[:, 0:4], (128, 4))
        self.compact(self.repa, smal[:, 0:1], 0, st2[:, 192:200])
        self.compact(self.repb, smal[:, 2:3], 8, None)
        self.dump("idx", idx[:, 0:16], (128, 16), U32)

    def compact(self, sc, locol, idx_off, valcols):
        P, IN = self.P, self.IN
        smal, st2, idx = self.smal, self.st2, self.idx
        mk = self.bq[2][:, 0:512].bitcast(F32)
        pf = self.bq[2][:, 512:1024].bitcast(F32)
        gr = self.bq[3][:, 0:512].bitcast(F32)
        ghib = self.bq[3][:, 512:768]
        glob_ = self.bq[3][:, 768:1024]
        sfb = self.bq[1][:, 0:2048]
        P.op16("gpsimd", lambda e: e.dma_start(ghib[:, 0:128], IN["ghi"]))
        P.op16("gpsimd", lambda e: e.dma_start(glob_[:, 0:128], IN["glo"]))
        self.v(lambda e: e.tensor_scalar(mk[:, 0:128], sc[:, 0:128], locol, None, op0=ALU.is_gt))
        self.v(lambda e: e.tensor_copy(pf[:, 0:128], mk[:, 0:128]))
        for sh in [1, 2, 4, 8, 16, 32, 64]:
            self.v(lambda e, sh=sh: e.tensor_tensor(pf[:, sh:128], pf[:, sh:128], pf[:, 0:128 - sh], op=ALU.add))
        p5 = self.pb[5]
        self.t(lambda e: e.matmul(p5[:, 8:9], self.cst2[:, 0:128], pf[:, 127:128], start=True, stop=True))
        self.v(lambda e: e.tensor_copy(smal[:, 16:17], p5[:, 8:9]))
        self.v(lambda e: e.tensor_tensor(gr[:, 0:128], pf[:, 0:128], mk[:, 0:128], op=ALU.subtract))
        self.v(lambda e: e.tensor_scalar(gr[:, 0:128], gr[:, 0:128], smal[:, 16:17], None, op0=ALU.add))
        self.v(lambda e: e.tensor_tensor(gr[:, 0:128], gr[:, 0:128], mk[:, 0:128], op=ALU.mult))
        self.v(lambda e: e.tensor_scalar(mk[:, 0:128], mk[:, 0:128], 1.0, None, op0=ALU.subtract))
        self.v(lambda e: e.tensor_tensor(gr[:, 0:128], gr[:, 0:128], mk[:, 0:128], op=ALU.add))
        for f in range(128):
            self.v(lambda e, f=f: e.tensor_scalar(sfb[:, 0:1024], self.cst[:, :], gr[:, f:f + 1], None, op0=ALU.is_equal))
            def mms(e, f=f):
                st_ = (f == 0)
                e.matmul(self.pb[0][0:1, 0:512], ghib[:, f:f + 1], sfb[:, 0:512], start=st_, stop=(f == 127))
                e.matmul(self.pb[1][0:1, 0:512], ghib[:, f:f + 1], sfb[:, 512:1024], start=st_, stop=(f == 127))
                e.matmul(self.pb[2][0:1, 0:512], glob_[:, f:f + 1], sfb[:, 0:512], start=st_, stop=(f == 127))
                return e.matmul(self.pb[3][0:1, 0:512], glob_[:, f:f + 1], sfb[:, 512:1024], start=st_, stop=(f == 127))
            self.t(mms)
        def comb1(e):
            e.tensor_scalar(self.repa[0:1, 0:512], self.pb[0][0:1, 0:512], 128.0, None, op0=ALU.mult)
            return e.tensor_scalar(self.repa[0:1, 512:1024], self.pb[1][0:1, 0:512], 128.0, None, op0=ALU.mult)
        self.v(comb1)
        def comb2(e):
            e.tensor_tensor(self.repa[0:1, 0:512], self.repa[0:1, 0:512], self.pb[2][0:1, 0:512], op=ALU.add)
            return e.tensor_tensor(self.repa[0:1, 512:1024], self.repa[0:1, 512:1024], self.pb[3][0:1, 0:512], op=ALU.add)
        self.v(comb2)
        one11 = self.cst2[0:1, 512:513]
        for j in range(8):
            self.t(lambda e, j=j: e.matmul(p5[:, 20 + j:21 + j], self.repa[0:1, 128 * j:128 * (j + 1)], one11, start=True, stop=True))
        def tocols(e):
            e.tensor_copy(idx[:, idx_off:idx_off + 8], p5[:, 20:28])
            if valcols is not None:
                e.tensor_copy(valcols, p5[:, 20:28])
            return None
        self.v(tocols)

    def stageC_heavy(self):
        P, IN = self.P, self.IN
        st2, smal, idx = self.st2, self.smal, self.idx
        P.dma("sync", smal[:, 16:24], IN["gam_h"])
        zero = smal[:, 254:255]

        def gather_side(idx_off, dst_bufs):
            for j in range(8):
                P.op16("gpsimd", lambda e, j=j: e.indirect_dma_start(
                    dst_bufs[j // 2][:, (j % 2) * 1024:(j % 2) * 1024 + 1024], None,
                    IN["xfull"], IndirectOffsetOnAxis(ap=idx[:, idx_off + j:idx_off + j + 1], axis=0)))

        def u_side(bufs):
            junk = self.repb[:, :].bitcast(BF16)[:, 0:1024]
            for j in range(8):
                sl = bufs[j // 2][:, (j % 2) * 1024:(j % 2) * 1024 + 1024]
                self.v(lambda e, sl=sl: e.reduce_sum(smal[:, 32:33], sl, axis=AXX))
                self.v(lambda e: e.tensor_scalar(smal[:, 32:33], smal[:, 32:33], 1.0 / D, None, op0=ALU.mult))
                self.s(lambda e, sl=sl: e.activation(junk, sl, AF.Square, accum_out=smal[:, 33:34]))
                def var1(e):
                    e.tensor_scalar(smal[:, 33:34], smal[:, 33:34], 1.0 / D, None, op0=ALU.mult)
                    return e.tensor_tensor(smal[:, 34:35], smal[:, 32:33], smal[:, 32:33], op=ALU.mult)
                self.v(var1)
                self.v(lambda e: e.tensor_tensor(smal[:, 33:34], smal[:, 33:34], smal[:, 34:35], op=ALU.subtract))
                self.s(lambda e: e.activation(smal[:, 33:34], smal[:, 33:34], AF.Sqrt, bias=zero))
                self.v(lambda e: e.reciprocal(smal[:, 33:34], smal[:, 33:34]))
                self.v(lambda e, sl=sl: e.tensor_scalar(sl, sl, smal[:, 32:33], smal[:, 33:34], op0=ALU.subtract, op1=ALU.mult))

        def transpose_side(src_bufs):
            for kk in range(8):
                for grp in range(2):
                    def trs(e, kk=kk, grp=grp):
                        last = None
                        for jj in range(4):
                            j = 4 * grp + jj
                            src = src_bufs[j // 2][:, (j % 2) * 1024 + 128 * kk:(j % 2) * 1024 + 128 * kk + 128]
                            last = e.transpose(self.pbt[:, 128 * jj:128 * jj + 128], src, self.eyeb)
                        return last
                    self.t(trs)
                    dst = self.bk[kk // 2][:, (kk % 2) * 1024 + 512 * grp:(kk % 2) * 1024 + 512 * grp + 512]
                    self.v(lambda e, dst=dst: e.tensor_copy(dst, self.pbt[:, 0:512]))

        gather_side(0, self.bq)
        u_side(self.bq)
        transpose_side(self.bq)
        P.op16("gpsimd", lambda e: e.dma_start(
            self.wb[0][:, :].rearrange("p (k n) -> p k n", k=8),
            IN["w_qh"].rearrange("(k p) n -> p k n", p=128)))
        def gscale0(e):
            last = None
            for kk in range(8):
                last = e.tensor_scalar(self.wb[0][:, 512 * kk:512 * kk + 512],
                                       self.wb[0][:, 512 * kk:512 * kk + 512],
                                       smal[:, 16 + kk:17 + kk], None, op0=ALU.mult)
            return last
        self.v(gscale0)
        for m4 in range(4):
            for nh in range(2):
                def mms(e, m4=m4, nh=nh):
                    last = None
                    for kk in range(8):
                        last = e.matmul(self.pb[0][:, 0:512],
                                        self.wb[0][:, 512 * kk + 128 * m4:512 * kk + 128 * m4 + 128],
                                        self.bk[kk // 2][:, (kk % 2) * 1024 + 512 * nh:(kk % 2) * 1024 + 512 * nh + 512],
                                        start=(kk == 0), stop=(kk == 7))
                    return last
                self.t(mms)
                dst = self.bx[m4 // 2][:, (m4 % 2) * 1024 + 512 * nh:(m4 % 2) * 1024 + 512 * nh + 512]
                self.v(lambda e, dst=dst: e.tensor_scalar(dst, self.pb[0][:, 0:512], 0.125, None, op0=ALU.mult))
        gather_side(8, self.bq)
        u_side(self.bq)
        transpose_side(self.bq)
        P.op16("gpsimd", lambda e: e.dma_start(
            self.wb[1][:, :].rearrange("p (k n) -> p k n", k=4),
            IN["w_kvh"][0:512, :].rearrange("(k p) n -> p k n", p=128)))
        P.op16("gpsimd", lambda e: e.dma_start(
            self.wb[2][:, :].rearrange("p (k n) -> p k n", k=4),
            IN["w_kvh"][512:1024, :].rearrange("(k p) n -> p k n", p=128)))
        def gscale12(e):
            last = None
            for kk in range(8):
                w = self.wb[1 + kk // 4][:, 1024 * (kk % 4):1024 * (kk % 4) + 1024]
                last = e.tensor_scalar(w, w, smal[:, 16 + kk:17 + kk], None, op0=ALU.mult)
            return last
        self.v(gscale12)
        for m4 in range(4):
            for nh in range(2):
                def mmk(e, m4=m4, nh=nh):
                    last = None
                    for kk in range(8):
                        w = self.wb[1 + kk // 4][:, 1024 * (kk % 4) + 128 * m4:1024 * (kk % 4) + 128 * m4 + 128]
                        last = e.matmul(self.pb[0][:, 0:512], w,
                                        self.bk[kk // 2][:, (kk % 2) * 1024 + 512 * nh:(kk % 2) * 1024 + 512 * nh + 512],
                                        start=(kk == 0), stop=(kk == 7))
                    return last
                self.t(mmk)
                dst = self.bq[m4 // 2][:, (m4 % 2) * 1024 + 512 * nh:(m4 % 2) * 1024 + 512 * nh + 512]
                self.v(lambda e, dst=dst: e.tensor_copy(dst, self.pb[0][:, 0:512]))
        for tt in range(8):
            def mmv(e, tt=tt):
                last = None
                for kk in range(8):
                    lhs = self.bk[kk // 2][:, (kk % 2) * 1024 + 128 * tt:(kk % 2) * 1024 + 128 * tt + 128]
                    w = self.wb[1 + kk // 4][:, 1024 * (kk % 4) + 512:1024 * (kk % 4) + 1024]
                    last = e.matmul(self.pb[0][:, 0:512], lhs, w, start=(kk == 0), stop=(kk == 7))
                return last
            self.t(mmv)
            dst = self.bq[2 + tt // 4][:, (tt % 4) * 512:(tt % 4) * 512 + 512]
            self.v(lambda e, dst=dst: e.tensor_copy(dst, self.pb[0][:, 0:512]))
        self.dump("hk", self.bq[0][:, 0:2048], (128, 2048), BF16)
        self.dump("hq", self.bx[0][:, 0:2048], (128, 2048), BF16)
        self._heavy_heads()
        P.op16("gpsimd", lambda e: e.dma_start(
            self.wb[2][:, :].rearrange("p (k n) -> p k n", k=4),
            IN["w_oh"].rearrange("(k p) n -> p k n", p=128)))
        stg = self.repb[:, :].bitcast(BF16)[:, 0:1024]
        for qm in range(8):
            def mmo(e, qm=qm):
                last = None
                for kt in range(4):
                    for nh in range(2):
                        last = e.matmul(self.pb[nh][:, 0:512],
                                        self.wb[1][:, 1024 * kt + 128 * qm:1024 * kt + 128 * qm + 128],
                                        self.wb[2][:, 1024 * kt + 512 * nh:1024 * kt + 512 * nh + 512],
                                        start=(kt == 0), stop=(kt == 3))
                return last
            self.t(mmo)
            def evo(e):
                e.tensor_copy(stg[:, 0:512], self.pb[0][:, 0:512])
                return e.tensor_copy(stg[:, 512:1024], self.pb[1][:, 0:512])
            self.v(evo)
            P.dma("sync", self.DSC["hvt"][128 * qm:128 * qm + 128, :], stg)
        self.dump("hvt_s", stg, (128, 1024), BF16)

    def _heavy_heads(self):
        for h in range(8):
            m4 = h // 2
            po = (h % 2) * 64
            for mt in range(8):
                def mmsim(e, h=h, mt=mt, m4=m4, po=po):
                    last = None
                    for nh in range(2):
                        lhs = self.bq[m4 // 2][po:po + 64, (m4 % 2) * 1024 + 128 * mt:(m4 % 2) * 1024 + 128 * mt + 128]
                        rhs = self.bx[m4 // 2][po:po + 64, (m4 % 2) * 1024 + 512 * nh:(m4 % 2) * 1024 + 512 * nh + 512]
                        last = e.matmul(self.pb[nh][:, 0:512], lhs, rhs, start=True, stop=True)
                    return last
                self.t(mmsim)
                def expev(e, mt=mt):
                    dst = self.bv[mt // 2][:, (mt % 2) * 1024:(mt % 2) * 1024 + 1024]
                    e.activation(dst[:, 0:512], self.pb[0][:, 0:512], AF.Exp)
                    return e.activation(dst[:, 512:1024], self.pb[1][:, 0:512], AF.Exp)
                self.s(expev)
            def mmden(e, h=h):
                last = None
                for mt in range(8):
                    src = self.bv[mt // 2][:, (mt % 2) * 1024:(mt % 2) * 1024 + 1024]
                    last = e.matmul(self.pb[2][0:1, 0:512], self.onesb_col, src[:, 0:512],
                                    start=(mt == 0), stop=(mt == 7))
                    last = e.matmul(self.pb[3][0:1, 0:512], self.onesb_col, src[:, 512:1024],
                                    start=(mt == 0), stop=(mt == 7))
                return last
            self.t(mmden)
            def recip0(e):
                e.tensor_copy(self.repa[0:1, 0:512], self.pb[2][0:1, 0:512])
                return e.tensor_copy(self.repa[0:1, 512:1024], self.pb[3][0:1, 0:512])
            self.v(recip0)
            self.v(lambda e: e.tensor_scalar(self.repa[0:1, 0:1024], self.repa[0:1, 0:1024], 1e-30, None, op0=ALU.max))
            self.v(lambda e: e.reciprocal(self.repa[0:1, 0:1024], self.repa[0:1, 0:1024]))
            for nh in range(2):
                def mmho(e, h=h, nh=nh):
                    for kt in range(8):
                        lhs = self.bq[2 + kt // 4][:, (kt % 4) * 512 + 64 * h:(kt % 4) * 512 + 64 * h + 64]
                        rhs = self.bv[kt // 2][:, (kt % 2) * 1024 + 512 * nh:(kt % 2) * 1024 + 512 * nh + 512]
                        e.matmul(self.pb[0][0:64, 0:512], lhs, rhs, start=(kt == 0), stop=(kt == 7))
                    return e.matmul(self.pb[1][:, 0:512], self.ones_row,
                                    self.repa[0:1, 512 * nh:512 * nh + 512], start=True, stop=True)
                self.t(mmho)
                self.v(lambda e: e.tensor_copy(self.repb[:, 0:512], self.pb[1][:, 0:512]))
                def evho(e, h=h, nh=nh):
                    dst = self.wb[1][(h % 2) * 64:(h % 2) * 64 + 64,
                                     1024 * (h // 2) + 512 * nh:1024 * (h // 2) + 512 * nh + 512]
                    return e.tensor_tensor(dst, self.pb[0][0:64, 0:512], self.repb[0:64, 0:512], op=ALU.mult)
                self.v(evho)

    def stageD_qkv(self):
        P, IN = self.P, self.IN
        st = self.st
        for wi, c0 in ((0, 0), (1, 512), (2, 1024)):
            P.op16("gpsimd", lambda e, wi=wi, c0=c0: e.dma_start(
                self.wb[wi][:, :].rearrange("p (k n) -> p k n", k=8),
                IN["w_qkv"][:, c0:c0 + 512].rearrange("(k p) n -> p k n", p=128)))
        P.dmas("sync", [(self.repa[:, :], IN["lngb"][:, 0:1024]),
                        (self.repb[:, :], IN["lngb"][:, 1024:2048])])
        stg32 = None
        for c in range(10):
            fmb = self.bx[c % 2]
            for half in range(2):
                i = 2 * c + half
                al = st[:, 40 + i:41 + i]
                be = st[:, 60 + i:61 + i]
                def xl(e, i=i, al=al, be=be, half=half, fmb=fmb):
                    last = None
                    for q4 in range(2):
                        stg = (self.bx[0] if q4 == 0 else self.bx[1])[:, 2048:2560]
                        sl = slice(512 * q4, 512 * q4 + 512)
                        e.tensor_scalar(stg, self.r[i][:, sl], al, be, op0=ALU.mult, op1=ALU.add)
                        e.tensor_tensor(stg, stg, self.repa[:, sl], op=ALU.mult)
                        last = e.tensor_tensor(stg, stg, self.repb[:, sl], op=ALU.add)
                    return last
                self.v(xl)
                for grp in range(2):
                    def trs(e, grp=grp):
                        last = None
                        for q in range(4):
                            kk = 4 * grp + q
                            stg = (self.bx[0] if kk < 4 else self.bx[1])[:, 2048 + 128 * (kk % 4):2048 + 128 * (kk % 4) + 128]
                            last = e.transpose(self.pbt[:, 128 * q:128 * q + 128], stg, self.eyeb)
                        return last
                    self.t(trs)
                    self.v(lambda e, grp=grp, half=half, fmb=fmb: e.tensor_copy(
                        fmb[:, 1024 * grp + 128 * half:1024 * grp + 128 * half + 1024]
                        .rearrange("p (k n) -> p k n", n=128)[:, ::2, :],
                        self.pbt[:, 0:512].rearrange("p (k n) -> p k n", n=128)))
            self._qkv_chunk(c, fmb)
        def qs(e):
            last = None
            for m4 in range(4):
                last = e.tensor_scalar(self.bq[m4][:, :], self.bq[m4][:, :], 0.125, None, op0=ALU.mult)
            return last
        self.v(qs)
        self.dump("qfm", self.bq[0][:, :], (128, 2304), BF16)
        self.dump("vsh9", self.bv[1][:, 2048:2560], (128, 512), BF16)
        self.dump("vsh8", self.bv[1][:, 1536:2048], (128, 512), BF16)
        self.dump("kfm", self.bk[0][:, :], (128, 2432), BF16)
        self.dump("vsh", self.bv[0][:, :], (128, 2560), BF16)

    def _qkv_chunk(self, c, fmb):
        for dst_bufs, wbi, off, width in ((self.bq, 0, 128, 2304), (self.bk, 1, 64, 2432)):
            xlo = 256 * c
            lo = max(xlo, off)
            hi = min(xlo + 256, off + width)
            if lo >= hi:
                continue
            for m4 in range(4):
                def mmq(e, m4=m4, fmb=fmb, wbi=wbi, lo=lo, hi=hi, xlo=xlo):
                    last = None
                    for kk in range(8):
                        last = e.matmul(self.pb[0][:, 0:256],
                                        self.wb[wbi][:, 512 * kk + 128 * m4:512 * kk + 128 * m4 + 128],
                                        fmb[:, 256 * kk:256 * kk + 256], start=(kk == 0), stop=(kk == 7))
                    return last
                self.t(mmq)
                self.v(lambda e, m4=m4, dst_bufs=dst_bufs, lo=lo, hi=hi, xlo=xlo, off=off:
                       e.tensor_copy(dst_bufs[m4][:, lo - off:hi - off],
                                     self.pb[0][:, lo - xlo:hi - xlo]))
        for u in (2 * c - 1, 2 * c):
            if u < 0 or u >= 19:
                continue
            prev = self.bx[(c - 1) % 2]
            cur = fmb
            def mmv(e, u=u, c=c, prev=prev, cur=cur):
                last = None
                for kk in range(8):
                    w = self.wb[2][:, 512 * kk:512 * kk + 512]
                    if u == 2 * c:
                        lhs = cur[:, 256 * kk + 64:256 * kk + 64 + 128]
                        last = e.matmul(self.pb[1][:, 0:512], lhs, w, start=(kk == 0), stop=(kk == 7))
                    else:
                        lhs0 = prev[:, 256 * kk + 192:256 * kk + 256]
                        lhs1 = cur[:, 256 * kk:256 * kk + 64]
                        e.matmul(self.pb[1][0:64, 0:512], lhs0, w, start=(kk == 0), stop=False)
                        last = e.matmul(self.pb[2][0:64, 0:512], lhs1, w, start=(kk == 0), stop=(kk == 7))
                return last
            self.t(mmv)
            if u % 2 == 0:
                self.v(lambda e, u=u: e.tensor_copy(
                    self.bv[u // 5][:, 512 * (u % 5):512 * (u % 5) + 512], self.pb[1][:, 0:512]))
            else:
                def evodd(e, u=u):
                    e.tensor_copy(self.bv[u // 5][0:64, 512 * (u % 5):512 * (u % 5) + 512],
                                  self.pb[1][0:64, 0:512])
                    return e.tensor_copy(self.bx[0][0:64, 2048:2560], self.pb[2][0:64, 0:512])
                self.v(evodd)
                self.P.dma("sync", self.bv[u // 5][64:128, 512 * (u % 5):512 * (u % 5) + 512],
                           self.bx[0][0:64, 2048:2560])

    def stageE_light(self):
        import os
        P, IN = self.P, self.IN
        ntiles = int(os.environ.get("KE_TILES", str(AT)))
        parts = os.environ.get("KE_PARTS", "mask,tok,pt,heads,light,scat,evac").split(",")
        self.ke_parts = parts
        P.op16("gpsimd", lambda e: e.dma_start(
            self.wb[0][:, :].rearrange("p (k n) -> p k n", k=4),
            IN["w_ol"].rearrange("(k p) n -> p k n", p=128)))
        for half in range(2):
            P.op16("gpsimd", lambda e, half=half: e.dma_start(
                self.wb[1][:, :].rearrange("p (k n) -> p k n", k=4),
                self.DSC["hvt"][512 * half:512 * half + 512, :].rearrange("(k p) n -> p k n", p=128)))
            for t in range(ntiles):
                self._light_tile(t, half)
        if "y1" in self.debug:
            d = self.nc.dram_tensor("dbg_y1", [ACOLS, D], F32, kind="ExternalOutput").ap()
            self.dbg_aps["y1"] = d
            self.P.dmas("scalar", [(d[128 * t:128 * t + 128, :], self.r[t + 1][:, :]) for t in range(AT)])

    def _light_tile(self, t, half):
        P, IN = self.P, self.IN
        parts = self.ke_parts
        mvb = self.bx[0][0:4, 0:256]
        if "mask" in parts:
            P.op16("gpsimd", lambda e, t=t: e.dma_start(mvb, IN["maskV"][t:t + 1, :, :].rearrange("a m n -> (a m) n")))
        if "tok" in parts:
            P.dma("sync", self.repb[:, 0:128], IN["tokrow"][:, 128 * t:128 * t + 128])
        if "pt" not in parts:
            return
        def ptiles(e, t=t, half=half):
            last = None
            for q in range(4):
                qt = 4 * half + q
                last = e.tensor_scalar(self.bx[1][:, 512 + 128 * q:512 + 128 * q + 128],
                                       self.repb[:, 0:128], self.st2[:, 192 + qt:193 + qt],
                                       None, op0=ALU.is_equal)
            return last
        self.v(ptiles)
        if "heads" not in parts:
            return
        if half == 0:
            for h in range(8):
                if "light" in parts:
                    self._light_head(t, h)
            if t == 8:
                self.dump("lit8", self.bx[1][:, 0:512], (128, 512), BF16)
            def mmlo(e, t=t):
                last = None
                for kk4 in range(4):
                    lhs = self.bx[1][(0):(128), 128 * kk4:128 * kk4 + 128]
                    for nh in range(2):
                        last = e.matmul(self.pb[4 + nh][:, 0:512], lhs,
                                        self.wb[0][:, 1024 * kk4 + 512 * nh:1024 * kk4 + 512 * nh + 512],
                                        start=(kk4 == 0), stop=False)
                return last
            self.t(mmlo)
        if "scat" not in parts:
            return
        def mmsc(e, t=t, half=half):
            last = None
            for q in range(4):
                lhs = self.bx[1][:, 512 + 128 * q:512 + 128 * q + 128]
                for nh in range(2):
                    last = e.matmul(self.pb[4 + nh][:, 0:512], lhs,
                                    self.wb[1][:, 1024 * q + 512 * nh:1024 * q + 512 * nh + 512],
                                    start=(half == 1 and q == 0), stop=(q == 3))
            return last
        self.t(mmsc)
        def evac(e, t=t):
            rt = self.r[t + 1]
            e.tensor_tensor(rt[:, 0:512], self.pb[4][:, 0:512], rt[:, 0:512], op=ALU.add)
            return e.tensor_tensor(rt[:, 512:1024], self.pb[5][:, 0:512], rt[:, 512:1024], op=ALU.add)
        self.v(evac)

    def _light_head(self, t, h):
        m4 = h // 2
        po = (h % 2) * 64
        def mmsim(e, t=t, h=h, m4=m4, po=po):
            qw = self.bq[m4][po:po + 64, 128 * t:128 * t + 128]
            last = None
            for kc in range(2):
                kw = self.bk[m4][po:po + 64, 128 * t + 128 * kc:128 * t + 128 * kc + 128]
                e.matmul(self.pb[2][:, 128 * kc:128 * kc + 128], kw, qw,
                         start=(kc == 0), stop=False)
                last = e.matmul(self.pb[2][:, 128 * kc:128 * kc + 128],
                                self.bx[0][0:4, 128 * kc:128 * kc + 128],
                                self.cstb[0:4, 0:128], start=False, stop=(kc == 1))
            return last
        self.t(mmsim)
        ex = self.bx[0][:, 512:768]
        self.s(lambda e: e.activation(ex, self.pb[2][:, 0:256], AF.Exp))
        if t == 8 and h == 0:
            self.dump("ext8", ex, (128, 256), BF16)
        def mmden(e):
            e.matmul(self.pb[3][0:1, 0:128], self.onesb_col, ex[:, 0:128], start=True, stop=False)
            return e.matmul(self.pb[3][0:1, 0:128], self.onesb_col, ex[:, 128:256], start=False, stop=True)
        self.t(mmden)
        self.v(lambda e: e.tensor_scalar(self.smal[0:1, 32:160], self.pb[3][0:1, 0:128], 1e-30, None, op0=ALU.max))
        self.v(lambda e: e.reciprocal(self.smal[0:1, 32:160], self.smal[0:1, 32:160]))
        self.t(lambda e: e.matmul(self.pb[2][:, 0:128], self.ones_row, self.smal[0:1, 32:160], start=True, stop=True))
        def mmav(e, t=t, h=h):
            last = None
            for kc in range(2):
                u = t + kc
                lhs = self.bv[u // 5][:, 512 * (u % 5) + 64 * h:512 * (u % 5) + 64 * h + 64]
                last = e.matmul(self.pb[3][0:64, 0:128], lhs, ex[:, 128 * kc:128 * kc + 128],
                                start=(kc == 0), stop=(kc == 1))
            return last
        self.t(mmav)
        if t == 8 and h == 0:
            self.v(lambda e: e.tensor_copy(self.repb[0:1, 384:512], self.smal[0:1, 32:160]))
            self.dump("rec8", self.repb[0:1, 384:512], (1, 128))
            self.v(lambda e: e.tensor_copy(self.repb[:, 384:512], self.pb[3][:, 0:128]))
            self.dump("av8", self.repb[:, 384:512], (128, 128))
        self.v(lambda e: e.tensor_copy(self.repb[:, 128:256], self.pb[2][:, 0:128]))
        def ev(e, h=h, po=po, m4=m4):
            dst = self.bx[1][po:po + 64, 128 * m4:128 * m4 + 128]
            return e.tensor_tensor(dst, self.pb[3][0:64, 0:128], self.repb[0:64, 128:256], op=ALU.mult)
        self.v(ev)

    def stageF_l0ffn(self):
        P, IN = self.P, self.IN
        smal = self.smal
        P.dmas("scalar", [(self.DSC["y1"][128 * t:128 * t + 128, :], self.r[t + 1][:, :]) for t in range(AT)])
        self.quant_weights(IN["w1T"], 8, smal[:, 43:44], smal[:, 44:45])
        self.dump("w1q0", self.r[0][:, :].bitcast(BF16), (128, 2048), BF16)
        self.dump("sw1", smal[:, 43:45], (128, 2), F32)
        P.dmas("sync", [(smal[:, 64:160], IN["cw0r"]), (smal[:, 160:192], IN["cb0r"])])
        def diags(e):
            last = None
            for f in range(32):
                for tap in range(3):
                    last = e.tensor_scalar(self._diag(f, tap), self.eyeb,
                                           smal[:, 64 + 3 * f + tap:65 + 3 * f + tap], None, op0=ALU.mult)
            return last
        self.v(diags)
        def zinit(e):
            for wbb in (self.wb[0], self.wb[1]):
                e.memset(wbb[:, 0:2080].rearrange("p (f n) -> p f n", n=130)[:, :, 0:2], 0.0)
            return e.memset(smal[0:1, 49:50], 0.0)
        self.v(zinit)
        P.dma("sync", self.DSC["invs"][2304:2305, :], smal[0:1, 49:50])
        for c in range(AT):
            self._ffn_chunk(c)
        self.quant_weights(IN["w2T"], 32, smal[:, 45:46], smal[:, 46:47])
        for t in range(AT):
            self._bit2_tile(t)
        self.dump("y1full", self.DSC["y1"], (ACOLS, D))
        self.dump("y2full", self.DSC["y2"], (ACOLS, D))

    def _diag(self, f, tap):
        j = 3 * f + tap
        if j < 68:
            return self.bq[j // 17][:, 128 + 128 * (j % 17):256 + 128 * (j % 17)]
        j -= 68
        return self.bk[j // 18][:, 128 + 128 * (j % 18):256 + 128 * (j % 18)]

    def quant_weights(self, wap, ktiles, sw_col, invsw_col):
        P, smal = self.P, self.smal
        N = 4096 if ktiles == 8 else 1024
        nch = N // 1024
        stg = self.bq[2][:, 0:2048].bitcast(F32)
        stg2 = self.bq[3][:, 0:2048].bitcast(F32)
        self.v(lambda e: e.memset(smal[:, 40:41], 0.0))
        for kt in range(ktiles):
            for ch in range(nch):
                P.dma("sync", stg[:, 0:1024], wap[128 * kt:128 * kt + 128, 1024 * ch:1024 * ch + 1024])
                self.v(lambda e: e.reduce_sum(smal[:, 41:42], stg[:, 0:1024], axis=AXX, apply_absolute_value=True))
                self.v(lambda e: e.tensor_tensor(smal[:, 40:41], smal[:, 40:41], smal[:, 41:42], op=ALU.add))
        self.ptot(smal[:, 40:41], smal[:, 42:43], 1)
        self.v(lambda e, kt=ktiles: e.tensor_scalar(invsw_col, smal[:, 42:43], 1.0 / (128.0 * kt * N), 1e-5,
                                                    op0=ALU.mult, op1=ALU.max))
        self.v(lambda e: e.reciprocal(sw_col, invsw_col))
        for kt in range(ktiles):
            for ch in range(nch):
                P.dma("sync", stg[:, 0:1024], wap[128 * kt:128 * kt + 128, 1024 * ch:1024 * ch + 1024])
                if ktiles == 8:
                    dst = self.r[2 * kt + ch // 2][:, :].bitcast(BF16)[:, 1024 * (ch % 2):1024 * (ch % 2) + 1024]
                else:
                    dst = self.r[kt // 2][:, :].bitcast(BF16)[:, 1024 * (kt % 2):1024 * (kt % 2) + 1024]
                def q3(e, dst=dst):
                    e.tensor_scalar(stg2[:, 0:1024], stg[:, 0:1024], sw_col, MAGIC, op0=ALU.mult, op1=ALU.add)
                    e.tensor_scalar(stg2[:, 0:1024], stg2[:, 0:1024], MAGIC, 1.0, op0=ALU.subtract, op1=ALU.min)
                    return e.tensor_scalar(dst, stg2[:, 0:1024], -1.0, None, op0=ALU.max)
                self.v(q3)

    def _ffn_chunk(self, c):
        P, IN = self.P, self.IN
        smal = self.smal
        yt, lnf = self.repa, self.repb
        m0, r0, v0 = smal[:, 49:50], smal[:, 50:51], smal[:, 51:52]
        mx, scx, invx = smal[:, 52:53], smal[:, 53:54], smal[:, 54:55]
        P.dma("sync", yt[:, :], self.DSC["y1"][128 * c:128 * c + 128, :])
        self.v(lambda e: e.reduce_sum(m0, yt[:, :], axis=AXX))
        self.v(lambda e: e.tensor_scalar(m0, m0, 1.0 / D, None, op0=ALU.mult))
        self.v(lambda e: e.tensor_scalar(lnf[:, :], yt[:, :], m0, None, op0=ALU.subtract))
        self.s(lambda e: e.activation(yt[:, :], lnf[:, :], AF.Square, accum_out=v0))
        self.v(lambda e: e.tensor_scalar(v0, v0, 1.0 / D, None, op0=ALU.mult))
        self.rsqrt(r0, v0)
        self.v(lambda e: e.tensor_scalar(lnf[:, :], lnf[:, :], r0, None, op0=ALU.mult))
        self.v(lambda e: e.reduce_max(mx, lnf[:, :], axis=AXX, apply_absolute_value=True))
        self.v(lambda e: e.tensor_scalar(mx, mx, 1e-5, None, op0=ALU.max))
        self.v(lambda e: e.tensor_scalar(invx, mx, 1.0 / 127.0, None, op0=ALU.mult))
        self.v(lambda e: e.reciprocal(scx, invx))
        xqs = self.bx[1][:, 0:2048].bitcast(F32)
        xqb = self.bx[0][:, 0:1024]
        self.v(lambda e: e.tensor_scalar(xqs[:, :], lnf[:, :], scx, MAGIC, op0=ALU.mult, op1=ALU.add))
        self.v(lambda e: e.tensor_scalar(xqs[:, :], xqs[:, :], MAGIC, None, op0=ALU.subtract))
        self.v(lambda e: e.tensor_scalar(xqb[:, :], xqs[:, :], invx, None, op0=ALU.mult))
        if c == 0 or c == AT - 1:
            hm = IN["hmask"]
            P.dma("sync", smal[:, 56:58], hm)
            col = smal[:, 56:57] if c == 0 else smal[:, 57:58]
            self.v(lambda e, col=col: e.tensor_scalar(xqb[:, :], xqb[:, :], col, None, op0=ALU.mult))
        for grp in range(2):
            def trs(e, grp=grp):
                last = None
                for q in range(4):
                    kk = 4 * grp + q
                    last = e.transpose(self.pbt[:, 128 * q:128 * q + 128],
                                      xqb[:, 128 * kk:128 * kk + 128], self.eyeb)
                return last
            self.t(trs)
            def cpy(e, grp=grp):
                last = None
                for q in range(4):
                    kk = 4 * grp + q
                    dstb = self.bq[kk] if kk < 4 else self.bk[kk - 4]
                    last = e.tensor_copy(dstb[:, 0:128], self.pbt[:, 128 * q:128 * q + 128])
                return last
            self.v(cpy)
        for half in range(2):
            hraw = self.wb[half]
            def mm1(e, half=half):
                last = None
                for g in range(4):
                    for u in range(4):
                        f = 16 * half + 4 * g + u
                        for kk in range(8):
                            lhs = self.r[2 * kk + f // 16][:, :].bitcast(BF16)[:, 128 * (f % 16):128 * (f % 16) + 128]
                            rhs = (self.bq[kk] if kk < 4 else self.bk[kk - 4])[:, 0:128]
                            last = e.matmul(self.pb[g][:, 128 * u:128 * u + 128], lhs, rhs,
                                            start=(kk == 0), stop=(kk == 7))
                return last
            self.t(mm1)
            def ev1(e, half=half, hraw=hraw):
                last = None
                for g in range(4):
                    last = e.tensor_copy(
                        hraw[:, 0:2080].rearrange("p (f n) -> p f n", n=130)[:, 4 * g:4 * g + 4, 2:130],
                        self.pb[g][:, :].rearrange("p (f n) -> p f n", n=128))
                return last
            self.v(ev1)
        for half in range(2):
            hraw = self.wb[half]
            def mmc(e, half=half, hraw=hraw):
                last = None
                for g in range(4):
                    for u in range(4):
                        fl = 4 * g + u
                        for tap in range(3):
                            last = e.matmul(self.pb[g][:, 128 * u:128 * u + 128],
                                            self._diag(16 * half + fl, tap),
                                            hraw[:, 130 * fl + tap:130 * fl + tap + 128],
                                            start=(tap == 0), stop=(tap == 2))
                return last
            self.t(mmc)
            def sil(e, half=half):
                last = None
                for g in range(4):
                    for u in range(4):
                        f = 16 * half + 4 * g + u
                        dst = self.bv[f // 8][:, :].bitcast(F32)[:, 128 * (f % 8):128 * (f % 8) + 128]
                        last = e.activation(dst, self.pb[g][:, 128 * u:128 * u + 128], AF.Silu,
                                            bias=smal[:, 160 + f:161 + f], scale=smal[:, 44:45])
                return last
            self.s(sil)
        def cset(e):
            e.tensor_copy(self.wb[0][:, 0:2080].rearrange("p (f n) -> p f n", n=130)[:, :, 0:2],
                          self.wb[0][:, 0:2080].rearrange("p (f n) -> p f n", n=130)[:, :, 128:130])
            return e.tensor_copy(self.wb[1][:, 0:2080].rearrange("p (f n) -> p f n", n=130)[:, :, 0:2],
                                 self.wb[1][:, 0:2080].rearrange("p (f n) -> p f n", n=130)[:, :, 128:130])
        self.v(cset)
        w2f = self.wb[2][:, :].bitcast(F32)
        def ab01(e):
            e.activation(w2f[:, 0:1024], self.bv[0][:, :].bitcast(F32)[:, 0:1024], AF.Abs)
            return e.activation(w2f[:, 1024:2048], self.bv[1][:, :].bitcast(F32)[:, 0:1024], AF.Abs)
        self.s(ab01)
        for i in (None, 2, 3):
            if i is not None:
                self.s(lambda e, i=i: e.activation(w2f[:, 1024:2048],
                                                   self.bv[i][:, :].bitcast(F32)[:, 0:1024], AF.Abs))
            self.v(lambda e: e.tensor_tensor(w2f[:, 0:1024], w2f[:, 0:1024], w2f[:, 1024:2048], op=ALU.max))
        self.v(lambda e: e.tensor_tensor(w2f[:, 0:512], w2f[:, 0:512], w2f[:, 512:1024], op=ALU.max))
        self.v(lambda e: e.tensor_tensor(w2f[:, 0:256], w2f[:, 0:256], w2f[:, 256:512], op=ALU.max))
        self.v(lambda e: e.tensor_tensor(w2f[:, 0:128], w2f[:, 0:128], w2f[:, 128:256], op=ALU.max))
        eyef = self.cst2[:, 384:512]
        self.t(lambda e: e.transpose(self.pb[4][:, 0:128], w2f[:, 0:128], eyef))
        hmx, invt, sht = smal[:, 52:53], smal[:, 53:54], smal[:, 54:55]
        self.v(lambda e: e.reduce_max(hmx, self.pb[4][:, 0:128], axis=AXX))
        self.v(lambda e: e.tensor_scalar(hmx, hmx, 1e-5, None, op0=ALU.max))
        self.v(lambda e: e.tensor_scalar(invt, hmx, 1.0 / 127.0, None, op0=ALU.mult))
        self.v(lambda e: e.reciprocal(sht, invt))
        P.dma("sync", self.DSC["invs"][128 * c:128 * c + 128, :], invt)
        self.t(lambda e: e.transpose(self.pb[4][0:1, 128:256], sht, eyef))
        self.v(lambda e: e.tensor_copy(self.st2[0:1, 0:128], self.pb[4][0:1, 128:256]))
        self.t(lambda e: e.matmul(self.pb[4][:, 256:384], self.ones_row, self.st2[0:1, 0:128], start=True, stop=True))
        hqi = self.wb[2][:, :].bitcast(I8)
        def qtt(e):
            last = None
            for f in range(32):
                hsf = self.bv[f // 8][:, :].bitcast(F32)[:, 128 * (f % 8):128 * (f % 8) + 128]
                last = e.tensor_tensor(hqi[:, 128 * f:128 * f + 128], hsf, self.pb[4][:, 256:384], op=ALU.mult)
            return last
        self.v(qtt)
        P.dma("sync", self.DSC["hq"][c:c + 1, :, :, :].rearrange("a p f n -> (a p) (f n)"), hqi[:, 0:4096])
        if c == 8:
            self.dump("xq8", xqb, (128, 1024), BF16)
            self.dump("hraw8a", self.wb[0][:, 0:2080], (128, 2080), BF16)
            self.dump("hs8", self.bv[0][:, :].bitcast(F32)[:, 0:1024], (128, 1024))
            self.v(lambda e: e.tensor_copy(self.st2[:, 300:304] if False else self.repa[:, 0:128], self.pb[4][:, 256:384]))
            self.dump("shb8", self.repa[:, 0:128], (128, 128))

    def _bit2_tile(self, t):
        P, smal = self.P, self.smal
        prs = []
        for half in range(2):
            for sl in range(2):
                cc = t + sl
                dst = self.wb[half][:, :].rearrange("p (f n) -> p f n", n=256)[:, :, 128 * sl:128 * sl + 128]
                if cc < AT:
                    src = self.DSC["hq"][cc:cc + 1, :, 16 * half:16 * half + 16, :].rearrange("a p f n -> (a p) f n")
                    prs.append((dst, src))
                else:
                    self.v(lambda e, dst=dst: e.memset(dst, 0.0))
        P.dmas("gpsimd", prs, slow=True)
        def mm2(e, t=t):
            last = None
            for ft in range(32):
                lhs = self.wb[ft // 16][:, 256 * (ft % 16) + 1:256 * (ft % 16) + 129]
                for nh in range(2):
                    rhs = self.r[ft // 2][:, :].bitcast(BF16)[:, 1024 * (ft % 2) + 512 * nh:1024 * (ft % 2) + 512 * nh + 512]
                    last = e.matmul(self.pb[4 + nh][:, 0:512], lhs, rhs, start=(ft == 0), stop=(ft == 31))
            return last
        self.t(mm2)
        P.dma("sync", smal[:, 47:48], self.DSC["invs"][128 * t + 1:128 * t + 129, :])
        self.v(lambda e: e.tensor_tensor(smal[:, 48:49], smal[:, 47:48], smal[:, 46:47], op=ALU.mult))
        P.dma("sync", self.repa[:, :], self.DSC["y1"][128 * t:128 * t + 128, :])
        def ev2(e):
            e.tensor_scalar(self.repb[:, 0:512], self.pb[4][:, 0:512], smal[:, 48:49], None, op0=ALU.mult)
            return e.tensor_scalar(self.repb[:, 512:1024], self.pb[5][:, 0:512], smal[:, 48:49], None, op0=ALU.mult)
        self.v(ev2)
        self.v(lambda e: e.tensor_tensor(self.repb[:, :], self.repb[:, :], self.repa[:, :], op=ALU.add))
        P.dma("sync", self.DSC["y2"][128 * t:128 * t + 128, :], self.repb[:, :])

    def stageG_mamba(self):
        P, IN, smal, st2 = self.P, self.IN, self.smal, self.st2
        eyef = self.cst2[:, 384:512]
        P.dmas("sync", [(smal[:, 64:132], IN["cw1r"]), (smal[:, 132:149], IN["cb1r"]),
                        (smal[0:32, 58:59], IN["dt_b"]), (smal[0:32, 59:60], IN["a_log"]),
                        (smal[0:32, 60:61], IN["d_par"]), (smal[:, 61:62], IN["xzmask"]),
                        (smal[:, 213:221], IN["pm8"])])
        def ginit(e):
            e.memset(smal[:, 253:254], -1.0)
            e.memset(smal[:, 252:253], 1.0)
            e.memset(self.wb[0][:, 0:2227].rearrange("p (f n) -> p f n", n=131)[:, :, 0:3], 0.0)
            e.memset(self.r[19][:, :], 0.0)
            e.memset(self.bk[3][:, :].bitcast(F32)[0:64, 64:1088], 0.0)
            e.memset(st2[:, 256:288], 1.0)
            return e.memset(st2[:, 288:320], 0.0)
        self.v(ginit)
        self.t(lambda e: e.transpose(self.pb[4][0:1, 0:32], smal[0:32, 59:60], eyef[0:32, 0:32]))
        self.t(lambda e: e.transpose(self.pb[4][0:1, 32:64], smal[0:32, 60:61], eyef[0:32, 0:32]))
        self.v(lambda e: e.tensor_copy(smal[0:1, 149:213], self.pb[4][0:1, 0:64]))
        self.t(lambda e: e.matmul(self.pb[4][:, 64:96], self.ones_row, smal[0:1, 149:181], start=True, stop=True))
        self.t(lambda e: e.matmul(self.pb[4][:, 96:128], self.ones_row, smal[0:1, 181:213], start=True, stop=True))
        self.s(lambda e: e.activation(st2[:, 160:192], self.pb[4][:, 64:96], AF.Exp))
        self.v(lambda e: (e.tensor_scalar(st2[:, 160:192], st2[:, 160:192], -1.0, None, op0=ALU.mult),
                          e.tensor_copy(st2[:, 192:224], self.pb[4][:, 96:128]))[-1])
        def gdiags(e):
            last = None
            for f in range(17):
                for tap in range(4):
                    last = e.tensor_scalar(self._gdiag(f, tap), self.eyeb,
                                           smal[:, 64 + 4 * f + tap:65 + 4 * f + tap], None, op0=ALU.mult)
            return last
        self.v(gdiags)
        P.dmas("gpsimd", [(self.r[kk][:, :].bitcast(BF16), IN["w_in_z"][128 * kk:128 * kk + 128, :])
                          for kk in range(8)])
        P.dmas("gpsimd", [(self.r[8 + ss][:, :].bitcast(BF16), IN["w_in_fm"][:, 2048 * ss:2048 * ss + 2048])
                          for ss in range(9)])
        for i in range(XZT):
            self._g_prologue(i)
            if i >= 1:
                self._g_ssd(i - 1)
        self._g_prefix()
        P.dmas("gpsimd", [(self.r[kt // 2][:, :].bitcast(BF16)[:, 1024 * (kt % 2):1024 * (kt % 2) + 1024],
                           IN["w_out"][128 * kt:128 * kt + 128, :]) for kt in range(16)])
        P.op16("gpsimd", lambda e: e.dma_start(self.wb[2][:, 0:2048], IN["nrm_w"]))
        def hcb(e):
            last = None
            for h in range(32):
                last = e.tensor_copy(self.wb[1][0:64, 64 * h:64 * h + 64], self._sst(h))
            return last
        self.v(hcb)
        for ch in range(16):
            self._g_pass2(ch)
        self.dump("y3full", self.DSC["y3"], (S, D))

    def _gdiag(self, f, tap):
        j = 4 * f + tap
        return self.bq[j // 17][:, 128 + 128 * (j % 17):256 + 128 * (j % 17)]

    def _winfm(self, b):
        return self.r[8 + b // 16][:, :].bitcast(BF16)[:, 128 * (b % 16):128 * (b % 16) + 128]

    def _ln2F(self, kk):
        return (self.bq[kk] if kk < 4 else self.bk[kk - 4])[:, 0:128]

    def _sst(self, h):
        if h < 16:
            return self.r[19][0:64, 64 * h:64 * h + 64]
        return self.bk[3][:, :].bitcast(F32)[0:64, 64 + 64 * (h - 16):128 + 64 * (h - 16)]

    def _stg(self, h):
        return self.bv[h // 16][:, :].bitcast(F32)[0:64, 64 * (h % 16):64 * (h % 16) + 64]

    def _g_prologue(self, i):
        P, IN, smal, st2 = self.P, self.IN, self.smal, self.st2
        yt, lnf = self.repa, self.repb
        m0, r0, v0 = smal[:, 49:50], smal[:, 50:51], smal[:, 51:52]
        P.dma("sync", yt[:, :], self.DSC["y2"][128 * i:128 * i + 128, :])
        self.v(lambda e: e.reduce_sum(m0, yt[:, :], axis=AXX))
        self.v(lambda e: e.tensor_scalar(m0, m0, 1.0 / D, None, op0=ALU.mult))
        self.v(lambda e: e.tensor_scalar(lnf[:, :], yt[:, :], m0, None, op0=ALU.subtract))
        self.s(lambda e: e.activation(yt[:, :], lnf[:, :], AF.Square, accum_out=v0))
        self.v(lambda e: e.tensor_scalar(v0, v0, 1.0 / D, None, op0=ALU.mult))
        self.rsqrt(r0, v0)
        lnb = self.bx[0][:, 0:1024]
        self.v(lambda e: e.tensor_scalar(lnf[:, :], lnf[:, :], r0, None, op0=ALU.mult))
        if i == 0:
            self.v(lambda e: e.tensor_scalar(lnb[:, :], lnf[:, :], smal[:, 61:62], None, op0=ALU.mult))
        else:
            self.v(lambda e: e.tensor_copy(lnb[:, :], lnf[:, :]))
        for grp in range(2):
            def trs(e, grp=grp):
                last = None
                for q in range(4):
                    kk = 4 * grp + q
                    last = e.transpose(self.pbt[:, 128 * q:128 * q + 128],
                                      lnb[:, 128 * kk:128 * kk + 128], self.eyeb)
                return last
            self.t(trs)
            def cpy(e, grp=grp):
                last = None
                for q in range(4):
                    kk = 4 * grp + q
                    last = e.tensor_copy(self._ln2F(kk), self.pbt[:, 128 * q:128 * q + 128])
                return last
            self.v(cpy)
        if i >= 1:
            def mmz(e):
                last = None
                for b in range(4):
                    for kk in range(8):
                        last = e.matmul(self.pb[b][:, 0:512], self._ln2F(kk),
                                        self.r[kk][:, :].bitcast(BF16)[:, 512 * b:512 * b + 512],
                                        start=(kk == 0), stop=(kk == 7))
                return last
            self.t(mmz)
            def evz(e):
                last = None
                for b in range(4):
                    last = e.tensor_copy(self.bx[1][:, 512 * b:512 * b + 512], self.pb[b][:, 0:512])
                return last
            self.v(evz)
        def mmxa(e):
            last = None
            for g in range(4):
                for u in range(4):
                    f = 4 * g + u
                    for kk in range(8):
                        last = e.matmul(self.pb[g][:, 128 * u:128 * u + 128],
                                        self._winfm(17 * kk + f), self._ln2F(kk),
                                        start=(kk == 0), stop=(kk == 7))
            return last
        self.t(mmxa)
        hbc3 = self.wb[0][:, 0:2227].rearrange("p (f n) -> p f n", n=131)
        def evxa(e):
            last = None
            for g in range(4):
                last = e.tensor_copy(hbc3[:, 4 * g:4 * g + 4, 3:131],
                                     self.pb[g][:, :].rearrange("p (f n) -> p f n", n=128))
            return last
        self.v(evxa)
        def mmxb(e):
            last = None
            for kk in range(8):
                last = e.matmul(self.pb[4][:, 0:128], self._winfm(17 * kk + 16), self._ln2F(kk),
                                start=(kk == 0), stop=(kk == 7))
            for kk in range(8):
                last = e.matmul(self.pb[4][:, 128:256], self._winfm(136 + kk), self._ln2F(kk),
                                start=(kk == 0), stop=(kk == 7))
            return last
        self.t(mmxb)
        def evxb(e, i=i):
            e.tensor_copy(self.wb[0][:, 131 * 16 + 3:131 * 16 + 131], self.pb[4][:, 0:128])
            return e.tensor_copy(st2[0:32, 384:512], self.pb[4][0:32, 128:256])
        self.v(evxb)
        def mmca(e):
            last = None
            for g in range(4):
                for u in range(4):
                    f = 4 * g + u
                    for tap in range(4):
                        last = e.matmul(self.pb[g][:, 128 * u:128 * u + 128], self._gdiag(f, tap),
                                        self.wb[0][:, 131 * f + tap:131 * f + tap + 128],
                                        start=(tap == 0), stop=(tap == 3))
            return last
        self.t(mmca)
        def sila(e):
            last = None
            for g in range(4):
                for u in range(4):
                    f = 4 * g + u
                    last = e.activation(self.bk[1][:, 128 * f:128 * f + 128],
                                        self.pb[g][:, 128 * u:128 * u + 128], AF.Silu,
                                        bias=smal[:, 132 + f:133 + f])
            return last
        self.s(sila)
        def mmcb(e):
            last = None
            for tap in range(4):
                last = e.matmul(self.pb[4][:, 0:128], self._gdiag(16, tap),
                                self.wb[0][:, 131 * 16 + tap:131 * 16 + tap + 128],
                                start=(tap == 0), stop=(tap == 3))
            return last
        self.t(mmcb)
        self.s(lambda e: e.activation(self.bk[1][:, 2048:2176], self.pb[4][:, 0:128], AF.Silu,
                                      bias=smal[:, 148:149]))
        self.v(lambda e: e.tensor_copy(hbc3[:, :, 0:3], hbc3[:, :, 128:131]))
        if i >= 1:
            dtF = st2[0:32, 384:512]
            scr = self.bx[0][:, :].bitcast(F32)
            self.v(lambda e: e.tensor_scalar(dtF, dtF, smal[0:32, 58:59], None, op0=ALU.add))
            self.s(lambda e: e.activation(scr[0:32, 0:128], dtF, AF.Abs))
            self.s(lambda e: e.activation(scr[0:32, 128:256], scr[0:32, 0:128], AF.Exp,
                                          scale=smal[0:32, 253:254]))
            self.s(lambda e: e.activation(scr[0:32, 256:384], scr[0:32, 128:256], AF.Ln,
                                          bias=smal[0:32, 252:253]))
            self.v(lambda e: e.tensor_scalar(scr[0:32, 384:512], dtF, 0.0, None, op0=ALU.max))
            self.v(lambda e: e.tensor_tensor(scr[0:32, 0:128], scr[0:32, 256:384], scr[0:32, 384:512], op=ALU.add))
            self.t(lambda e: e.transpose(self.pb[4][:, 256:288], scr[0:32, 0:128],
                                         self.cst2[0:32, 384:416]))
            self.v(lambda e: e.tensor_copy(self.st[:, 0:32], self.pb[4][:, 256:288]))

    def _g_ssd(self, ch):
        P, smal, st2 = self.P, self.smal, self.st2
        eyef = self.cst2[:, 384:512]
        tri01 = self.cstb[:, 514:642]
        ones128 = self.cstb[:, 642:770]
        for grp in range(4):
            def trs(e, grp=grp):
                last = None
                for q in range(4):
                    f = 4 * grp + q
                    last = e.transpose(self.pbt[:, 128 * q:128 * q + 128],
                                      self.bk[1][:, 128 * f:128 * f + 128], self.eyeb)
                return last
            self.t(trs)
            def cpy(e, grp=grp):
                last = None
                for q in range(4):
                    f = 4 * grp + q
                    last = e.tensor_copy(self.bv[1][:, 128 * f:128 * f + 128],
                                         self.pbt[:, 128 * q:128 * q + 128])
                return last
            self.v(cpy)
        def xp(e):
            last = None
            for h in range(32):
                last = e.tensor_scalar(self.bv[0][:, 64 * h:64 * h + 64],
                                       self.bv[1][:, 64 * h:64 * h + 64],
                                       self.st[:, h:h + 1], None, op0=ALU.mult)
            return last
        self.v(xp)
        self.v(lambda e: e.tensor_tensor(st2[:, 224:256], self.st[:, 0:32], st2[:, 160:192], op=ALU.mult))
        self.v(lambda e: e.tensor_copy(self.wb[1][:, 1536:1568], st2[:, 224:256]))
        self.v(lambda e: e.tensor_tensor(self.wb[1][:, 1568:1600], st2[:, 224:256],
                                         self.wb[1][:, 1536:1568], op=ALU.subtract))
        def mmcum(e):
            e.matmul(self.pb[5][:, 0:32], tri01, self.wb[1][:, 1536:1568], start=True, stop=False)
            e.matmul(self.pb[5][:, 0:32], tri01, self.wb[1][:, 1568:1600], start=False, stop=True)
            e.matmul(self.pb[5][:, 32:64], ones128, self.wb[1][:, 1536:1568], start=True, stop=False)
            return e.matmul(self.pb[5][:, 32:64], ones128, self.wb[1][:, 1568:1600], start=False, stop=True)
        self.t(mmcum)
        self.v(lambda e: (e.tensor_copy(st2[:, 0:32], self.pb[5][:, 0:32]),
                          e.tensor_copy(st2[:, 128:160], self.pb[5][:, 32:64]))[-1])
        self.v(lambda e: e.tensor_tensor(st2[:, 352:384], st2[:, 128:160], st2[:, 0:32], op=ALU.subtract))
        def exps(e):
            e.activation(st2[:, 32:64], st2[:, 352:384], AF.Exp)
            e.activation(st2[:, 64:96], st2[:, 0:32], AF.Exp)
            return e.activation(st2[:, 96:128], st2[:, 128:160], AF.Exp)
        self.s(exps)
        self.v(lambda e: e.tensor_tensor(st2[:, 320:352], st2[:, 64:96], st2[:, 256:288], op=ALU.mult))
        P.dma("sync", self.DSC["fac"][ch:ch + 1, :, :].rearrange("a p c -> (a p) c"), st2[:, 320:352])
        self.v(lambda e: (e.tensor_tensor(st2[:, 256:288], st2[:, 256:288], st2[:, 96:128], op=ALU.mult),
                          e.tensor_tensor(st2[:, 288:320], st2[:, 288:320], st2[:, 128:160], op=ALU.add))[-1])
        P.dma("sync", self.bk[1][0:64, 2176:2304], self.bk[1][64:128, 2048:2176])
        self.t(lambda e: e.transpose(self.pbt[:, 0:64], self.bk[1][0:64, 2048:2176], self.eyeb[0:64, 0:64]))
        self.v(lambda e: e.tensor_copy(self.bx[1][:, 2048:2112], self.pbt[:, 0:64]))
        self.t(lambda e: e.matmul(self.pb[5][:, 64:192], self.bk[1][0:64, 2048:2176],
                                  self.bk[1][0:64, 2176:2304], start=True, stop=True))
        self.v(lambda e: e.tensor_tensor(self.pb[5][:, 192:320], self.pb[5][:, 64:192], tri01, op=ALU.mult))
        Ef = self.bx[0][:, :].bitcast(F32)
        for w in range(4):
            def trows(e, w=w):
                last = None
                for hh in range(8):
                    bank = self.pb[4] if hh < 4 else self.pb[0]
                    last = e.transpose(bank[0:1, 128 * (hh % 4):128 * (hh % 4) + 128],
                                       st2[:, 8 * w + hh:8 * w + hh + 1], eyef)
                return last
            self.t(trows)
            def rows(e):
                last = None
                for hh in range(8):
                    bank = self.pb[4] if hh < 4 else self.pb[0]
                    src = bank[0:1, 128 * (hh % 4):128 * (hh % 4) + 128]
                    e.tensor_copy(self.r[17][0:1, 128 * hh:128 * hh + 128], src)
                    last = e.tensor_scalar(self.r[18][0:1, 128 * hh:128 * hh + 128],
                                           src, -1.0, None, op0=ALU.mult)
                return last
            self.v(rows)
            def mmdiff(e):
                last = None
                for hh in range(8):
                    reg = self.pb[2 + hh // 4][:, 128 * (hh % 4):128 * (hh % 4) + 128]
                    e.matmul(reg, self.ones_row, self.r[17][0:1, 128 * hh:128 * hh + 128],
                             start=True, stop=False)
                    last = e.matmul(reg, self.r[18][0:1, 128 * hh:128 * hh + 128], self.ones_row,
                                    start=False, stop=True)
                return last
            self.t(mmdiff)
            def clamp(e):
                last = None
                for hh in range(8):
                    reg = self.pb[2 + hh // 4][:, 128 * (hh % 4):128 * (hh % 4) + 128]
                    last = e.tensor_scalar(reg, reg, 0.0, None, op0=ALU.min)
                return last
            self.v(clamp)
            def expd(e):
                last = None
                for hh in range(8):
                    reg = self.pb[2 + hh // 4][:, 128 * (hh % 4):128 * (hh % 4) + 128]
                    last = e.activation(Ef[:, 128 * hh:128 * hh + 128], reg, AF.Exp)
                return last
            self.s(expd)
            def mt(e):
                last = None
                for hh in range(8):
                    last = e.tensor_tensor(self.wb[1][:, 128 * hh:128 * hh + 128],
                                           Ef[:, 128 * hh:128 * hh + 128],
                                           self.pb[5][:, 192:320], op=ALU.mult)
                return last
            self.v(mt)
            def mmyi(e, w=w):
                last = None
                for hh in range(8):
                    h = 8 * w + hh
                    last = e.matmul(self.pb[0][:, 64 * hh:64 * hh + 64],
                                    self.wb[1][:, 128 * hh:128 * hh + 128],
                                    self.bv[0][:, 64 * h:64 * h + 64], start=True, stop=True)
                return last
            self.t(mmyi)
            def stb(e, w=w):
                last = None
                for hh in range(8):
                    h = 8 * w + hh
                    last = e.tensor_copy(self.wb[1][0:64, 1600 + 64 * hh:1664 + 64 * hh], self._sst(h))
                return last
            self.v(stb)
            def mmyint(e, w=w):
                last = None
                for hh in range(8):
                    last = e.matmul(self.pb[1][:, 64 * hh:64 * hh + 64],
                                    self.bk[1][0:64, 2176:2304],
                                    self.wb[1][0:64, 1600 + 64 * hh:1664 + 64 * hh], start=True, stop=True)
                return last
            self.t(mmyint)
            def bs(e, w=w):
                last = None
                for hh in range(8):
                    h = 8 * w + hh
                    last = e.tensor_scalar(self.wb[1][:, 1024 + 64 * hh:1088 + 64 * hh],
                                           self.bx[1][:, 2048:2112], st2[:, 32 + h:33 + h],
                                           None, op0=ALU.mult)
                return last
            self.v(bs)
            def mmtc(e, w=w):
                last = None
                for hh in range(8):
                    h = 8 * w + hh
                    last = e.matmul(self.pb[4][0:64, 64 * hh:64 * hh + 64],
                                    self.wb[1][:, 1024 + 64 * hh:1088 + 64 * hh],
                                    self.bv[0][:, 64 * h:64 * h + 64], start=True, stop=True)
                return last
            self.t(mmtc)
            def ydst(h):
                return self.bv[2 + h // 16][:, :].bitcast(F32)[:, 64 * (h % 16):64 * (h % 16) + 64]
            def ev1(e, w=w):
                last = None
                for hh in range(8):
                    h = 8 * w + hh
                    last = e.tensor_scalar(ydst(h), self.pb[1][:, 64 * hh:64 * hh + 64],
                                           st2[:, 64 + h:65 + h], None, op0=ALU.mult)
                return last
            self.v(ev1)
            def ev2(e, w=w):
                last = None
                for hh in range(8):
                    h = 8 * w + hh
                    last = e.tensor_tensor(ydst(h), ydst(h), self.pb[0][:, 64 * hh:64 * hh + 64], op=ALU.add)
                return last
            self.v(ev2)
            def ev3(e, w=w):
                last = None
                for hh in range(8):
                    h = 8 * w + hh
                    last = e.scalar_tensor_tensor(ydst(h), self.bv[1][:, 64 * h:64 * h + 64],
                                                  st2[:, 192 + h:193 + h], ydst(h),
                                                  op0=ALU.mult, op1=ALU.add)
                return last
            self.v(ev3)
            def sup1(e, w=w):
                last = None
                for hh in range(8):
                    h = 8 * w + hh
                    last = e.tensor_scalar(self._sst(h), self._sst(h), st2[0:64, 96 + h:97 + h],
                                           None, op0=ALU.mult)
                return last
            self.v(sup1)
            def sup2(e, w=w):
                last = None
                for hh in range(8):
                    h = 8 * w + hh
                    last = e.tensor_tensor(self._sst(h), self._sst(h),
                                           self.pb[4][0:64, 64 * hh:64 * hh + 64], op=ALU.add)
                return last
            self.v(sup2)
        if ch == 7:
            self.dump("xbc8", self.bk[1][:, 0:2176], (128, 2176), BF16)
            self.dump("dt8", self.st[:, 0:32], (128, 32))
            self.dump("z8", self.bx[1][:, 0:2048], (128, 2048), BF16)
            self.dump("yloc8a", self.bv[2][:, :].bitcast(F32)[:, 0:1024], (128, 1024))
            self.dump("yloc8b", self.bv[3][:, :].bitcast(F32)[:, 0:1024], (128, 1024))
        P.dmas("gpsimd", [(self.DSC["Y"][128 * ch:128 * ch + 128, 0:1024], self.bv[2][:, :].bitcast(F32)[:, 0:1024]),
                          (self.DSC["Y"][128 * ch:128 * ch + 128, 1024:2048], self.bv[3][:, :].bitcast(F32)[:, 0:1024])])
        P.dma("sync", self.DSC["z"][128 * ch:128 * ch + 128, :], self.bx[1][:, 0:2048])
        P.dma("sync", self.DSC["cc"][ch:ch + 1, :, :].rearrange("a p n -> (a p) n"),
              self.bk[1][64:128, 2048:2176])

    def _g_prefix(self):
        P, CC, smal, st2 = self.P, self.CC, self.smal, self.st2
        self.dump("sloc_a", self.r[19][0:64, 0:1024], (64, 1024))
        self.dump("sloc_b", self.bk[3][:, :].bitcast(F32)[0:64, 64:1088], (64, 1024))
        P.dmas("sync", [(CC["mb_in"].ap()[0:1, 4096 * h:4096 * h + 4096].rearrange("a (s d) -> (a s) d", s=64),
                         self._sst(h)) for h in range(32)])
        self.s(lambda e: e.activation(st2[:, 320:352], st2[:, 288:320], AF.Exp))
        self.dump("gamv", st2[:, 320:352], (128, 32))
        P.dma("sync", CC["mb_in"].ap()[0:1, 131072:131104], st2[0:1, 320:352])
        self.gp(lambda e: e.collective_compute(
            "AllGather", ALU.bypass, replica_groups=[list(range(NC))],
            ins=[CC["mb_in"].ap().opt()], outs=[CC["mb_out"].ap().opt()]))
        def hz(e):
            last = None
            for h in range(32):
                last = e.memset(self._sst(h), 0.0)
            return last
        self.v(hz)
        for cc in range(7):
            P.dmas("sync", [(self._stg(h),
                             CC["mb_out"].ap()[cc:cc + 1, 4096 * h:4096 * h + 4096].rearrange("a (s d) -> (a s) d", s=64))
                            for h in range(32)] +
                           [(smal[0:1, 181:213], CC["mb_out"].ap()[cc:cc + 1, 131072:131104])])
            self.t(lambda e: e.matmul(self.pb[5][:, 320:352], self.ones_row, smal[0:1, 181:213],
                                      start=True, stop=True))
            self.v(lambda e: e.tensor_scalar(st2[:, 352:384], self.pb[5][:, 320:352], 1.0, None, op0=ALU.subtract))
            self.v(lambda e, cc=cc: e.tensor_scalar(st2[:, 352:384], st2[:, 352:384],
                                                    smal[:, 213 + cc:214 + cc], None, op0=ALU.mult))
            self.v(lambda e: e.tensor_scalar(st2[:, 352:384], st2[:, 352:384], 1.0, None, op0=ALU.add))
            def hsc(e):
                last = None
                for h in range(32):
                    last = e.tensor_scalar(self._sst(h), self._sst(h), st2[0:64, 352 + h:353 + h],
                                           None, op0=ALU.mult)
                return last
            self.v(hsc)
            def hup(e, cc=cc):
                last = None
                for h in range(32):
                    last = e.scalar_tensor_tensor(self._sst(h), self._stg(h),
                                                  smal[0:64, 213 + cc:214 + cc], self._sst(h),
                                                  op0=ALU.mult, op1=ALU.add)
                return last
            self.v(hup)

    def _g_pass2(self, ch):
        P, smal, st2 = self.P, self.smal, self.st2
        P.dmas("gpsimd", [(self.bv[2][:, :].bitcast(F32)[:, 0:1024], self.DSC["Y"][128 * ch:128 * ch + 128, 0:1024]),
                          (self.bv[3][:, :].bitcast(F32)[:, 0:1024], self.DSC["Y"][128 * ch:128 * ch + 128, 1024:2048])])
        P.dmas("sync", [(self.bx[1][:, 0:2048], self.DSC["z"][128 * ch:128 * ch + 128, :]),
                        (self.bk[1][0:64, 2048:2176], self.DSC["cc"][ch:ch + 1, :, :].rearrange("a p n -> (a p) n")),
                        (st2[:, 0:32], self.DSC["fac"][ch:ch + 1, :, :].rearrange("a p c -> (a p) c"))])
        def mmcor(e):
            last = None
            for h in range(32):
                last = e.matmul(self.pb[h // 8][:, 64 * (h % 8):64 * (h % 8) + 64],
                                self.bk[1][0:64, 2048:2176], self.wb[1][0:64, 64 * h:64 * h + 64],
                                start=True, stop=True)
            return last
        self.t(mmcor)
        def ydst(h):
            return self.bv[2 + h // 16][:, :].bitcast(F32)[:, 64 * (h % 16):64 * (h % 16) + 64]
        def corr(e):
            last = None
            for h in range(32):
                last = e.scalar_tensor_tensor(ydst(h), self.pb[h // 8][:, 64 * (h % 8):64 * (h % 8) + 64],
                                              st2[:, h:h + 1], ydst(h), op0=ALU.mult, op1=ALU.add)
            return last
        self.v(corr)
        sz = self.bx[0][:, 0:2048]
        self.s(lambda e: e.activation(sz, self.bx[1][:, 0:2048], AF.Silu))
        self.v(lambda e: (e.tensor_tensor(self.bv[2][:, :].bitcast(F32)[:, 0:1024],
                                          self.bv[2][:, :].bitcast(F32)[:, 0:1024], sz[:, 0:1024], op=ALU.mult),
                          e.tensor_tensor(self.bv[3][:, :].bitcast(F32)[:, 0:1024],
                                          self.bv[3][:, :].bitcast(F32)[:, 0:1024], sz[:, 1024:2048], op=ALU.mult))[-1])
        va, vb, rs = smal[:, 49:50], smal[:, 50:51], smal[:, 51:52]
        def sq(e):
            e.activation(self.repa[:, :], self.bv[2][:, :].bitcast(F32)[:, 0:1024], AF.Square, accum_out=va)
            return e.activation(self.repa[:, :], self.bv[3][:, :].bitcast(F32)[:, 0:1024], AF.Square, accum_out=vb)
        self.s(sq)
        self.v(lambda e: e.tensor_tensor(va, va, vb, op=ALU.add))
        self.v(lambda e: e.tensor_scalar(va, va, 1.0 / 2048.0, None, op0=ALU.mult))
        self.rsqrt(rs, va)
        self.v(lambda e: (e.tensor_scalar(self.bv[2][:, :].bitcast(F32)[:, 0:1024],
                                          self.bv[2][:, :].bitcast(F32)[:, 0:1024], rs, None, op0=ALU.mult),
                          e.tensor_scalar(self.bv[3][:, :].bitcast(F32)[:, 0:1024],
                                          self.bv[3][:, :].bitcast(F32)[:, 0:1024], rs, None, op0=ALU.mult))[-1])
        yn = self.bx[0][:, 0:2048]
        self.v(lambda e: (e.tensor_tensor(yn[:, 0:1024], self.bv[2][:, :].bitcast(F32)[:, 0:1024],
                                          self.wb[2][:, 0:1024], op=ALU.mult),
                          e.tensor_tensor(yn[:, 1024:2048], self.bv[3][:, :].bitcast(F32)[:, 0:1024],
                                          self.wb[2][:, 1024:2048], op=ALU.mult))[-1])
        for grp in range(4):
            def trs(e, grp=grp):
                last = None
                for q in range(4):
                    kt = 4 * grp + q
                    last = e.transpose(self.pbt[:, 128 * q:128 * q + 128],
                                      yn[:, 128 * kt:128 * kt + 128], self.eyeb)
                return last
            self.t(trs)
            def cpy(e, grp=grp):
                last = None
                for q in range(4):
                    kt = 4 * grp + q
                    last = e.tensor_copy(self.bk[0][:, 128 * kt:128 * kt + 128],
                                         self.pbt[:, 128 * q:128 * q + 128])
                return last
            self.v(cpy)
        def mmo3(e):
            last = None
            for nh in range(2):
                for kt in range(16):
                    last = e.matmul(self.pb[4 + nh][:, 0:512], self.bk[0][:, 128 * kt:128 * kt + 128],
                                    self.r[kt // 2][:, :].bitcast(BF16)[:, 1024 * (kt % 2) + 512 * nh:1024 * (kt % 2) + 512 * nh + 512],
                                    start=(kt == 0), stop=(kt == 15))
            return last
        self.t(mmo3)
        P.dma("sync", self.repa[:, :], self.DSC["y2"][128 * (ch + 1):128 * (ch + 1) + 128, :])
        self.v(lambda e: (e.tensor_tensor(self.repb[:, 0:512], self.pb[4][:, 0:512], self.repa[:, 0:512], op=ALU.add),
                          e.tensor_tensor(self.repb[:, 512:1024], self.pb[5][:, 0:512], self.repa[:, 512:1024], op=ALU.add))[-1])
        P.dma("sync", self.DSC["y3"][128 * ch:128 * ch + 128, :], self.repb[:, :])


    def stageH_moe(self):
        P, IN, CC, smal, st2 = self.P, self.IN, self.CC, self.smal, self.st2
        eyef = self.cst2[:, 384:512]
        for t in range(16):
            yt, lnf = self.repa, self.repb
            m0, r0, v0 = smal[:, 49:50], smal[:, 50:51], smal[:, 51:52]
            P.dma("sync", yt[:, :], self.DSC["y3"][128 * t:128 * t + 128, :])
            self.v(lambda e: e.reduce_sum(m0, yt[:, :], axis=AXX))
            self.v(lambda e: e.tensor_scalar(m0, m0, 1.0 / D, None, op0=ALU.mult))
            self.v(lambda e: e.tensor_scalar(lnf[:, :], yt[:, :], m0, None, op0=ALU.subtract))
            self.s(lambda e: e.activation(yt[:, :], lnf[:, :], AF.Square, accum_out=v0))
            self.v(lambda e: e.tensor_scalar(v0, v0, 1.0 / D, None, op0=ALU.mult))
            self.rsqrt(r0, v0)
            self.v(lambda e: e.tensor_scalar(lnf[:, :], lnf[:, :], r0, None, op0=ALU.mult))
            P.dma("scalar", self.DSC["acc"][128 * t:128 * t + 128, :], lnf[:, :])
            def mmsum(e, t=t):
                e.matmul(self.pb[4][0:1, 0:512], self.ones_col, lnf[:, 0:512],
                         start=(t == 0), stop=(t == 15))
                return e.matmul(self.pb[5][0:1, 0:512], self.ones_col, lnf[:, 512:1024],
                                start=(t == 0), stop=(t == 15))
            self.t(mmsum)
            lnb = self.bx[0][:, 0:1024]
            self.v(lambda e: e.tensor_copy(lnb[:, :], lnf[:, :]))
            for grp in range(2):
                def trs(e, grp=grp):
                    last = None
                    for q in range(4):
                        kk = 4 * grp + q
                        last = e.transpose(self.pbt[:, 128 * q:128 * q + 128],
                                          lnb[:, 128 * kk:128 * kk + 128], self.eyeb)
                    return last
                self.t(trs)
                def cpy(e, grp=grp):
                    last = None
                    for q in range(4):
                        kk = 4 * grp + q
                        last = e.tensor_copy(self.bx[1][:, 128 * kk:128 * kk + 128],
                                             self.pbt[:, 128 * q:128 * q + 128])
                    return last
                self.v(cpy)
            P.dma("sync", self.DSC["ln3F"][t:t + 1, :, :, :].rearrange("a k p n -> (a p) k n"),
                  self.bx[1][:, 0:1024].rearrange("p (k n) -> p k n", n=128))
        self.v(lambda e: (e.tensor_copy(self.repa[0:1, 0:512], self.pb[4][0:1, 0:512]),
                          e.tensor_copy(self.repa[0:1, 512:1024], self.pb[5][0:1, 0:512]))[-1])
        P.dma("sync", CC["mu_in"].ap()[0:1, :], self.repa[0:1, :])
        self.gp(lambda e: e.collective_compute(
            "AllReduce", ALU.add, replica_groups=[list(range(NC))],
            ins=[CC["mu_in"].ap().opt()], outs=[CC["mu_out"].ap().opt()]))
        P.dma("sync", smal[:, 96:104], CC["mu_out"].ap()[0:1, :].rearrange("a (k p) -> (a p) k", p=128), slow=True)
        self.v(lambda e: e.tensor_scalar(smal[:, 96:104], smal[:, 96:104], 1.0 / T, None, op0=ALU.mult))
        P.dma("sync", smal[:, 104:168].rearrange("p (k e) -> p k e", e=8),
              IN["w_r"].rearrange("(k p) e -> p k e", p=128), slow=True)
        def mmlg(e):
            last = None
            for kt in range(8):
                last = e.matmul(self.pb[0][0:8, 0:1], smal[:, 104 + 8 * kt:112 + 8 * kt],
                                smal[:, 96 + kt:97 + kt], start=(kt == 0), stop=(kt == 7))
            return last
        self.t(mmlg)
        P.dma("sync", smal[0:8, 168:169], IN["b_r"].rearrange("a e -> e a"), slow=True)
        self.v(lambda e: e.tensor_tensor(smal[0:8, 169:170], self.pb[0][0:8, 0:1], smal[0:8, 168:169], op=ALU.add))
        self.t(lambda e: e.transpose(self.pb[0][0:1, 4:12], smal[0:8, 169:170], eyef[0:8, 0:8]))
        self.v(lambda e: e.tensor_copy(smal[0:1, 170:178], self.pb[0][0:1, 4:12]))
        self.s(lambda e: e.activation(smal[0:1, 178:186], smal[0:1, 170:178], AF.Exp))
        self.v(lambda e: e.reduce_sum(smal[0:1, 186:187], smal[0:1, 178:186], axis=AXX))
        self.v(lambda e: e.reciprocal(smal[0:1, 187:188], smal[0:1, 186:187]))
        self.v(lambda e: e.tensor_scalar(smal[0:1, 188:196], smal[0:1, 178:186], smal[0:1, 187:188], None, op0=ALU.mult))
        self.v(lambda e: e.reduce_max(smal[0:1, 196:197], smal[0:1, 188:196], axis=AXX))
        self.v(lambda e: e.memset(smal[0:1, 197:198], NEG))
        self.v(lambda e: e.tensor_scalar(smal[0:1, 198:206], smal[0:1, 188:196], smal[0:1, 196:197], None, op0=ALU.is_equal))
        self.v(lambda e: e.scalar_tensor_tensor(smal[0:1, 206:214], smal[0:1, 198:206], smal[0:1, 197:198],
                                                smal[0:1, 188:196], op0=ALU.mult, op1=ALU.add))
        self.v(lambda e: e.reduce_max(smal[0:1, 214:215], smal[0:1, 206:214], axis=AXX))
        self.v(lambda e: e.tensor_scalar(smal[0:1, 215:223], smal[0:1, 206:214], smal[0:1, 214:215], None, op0=ALU.is_equal))
        self.v(lambda e: (e.tensor_copy(smal[0:1, 223:224], smal[0:1, 196:197]),
                          e.tensor_copy(smal[0:1, 224:225], smal[0:1, 214:215]))[-1])
        self.s(lambda e: e.activation(smal[0:1, 225:227], smal[0:1, 223:225], AF.Exp))
        self.v(lambda e: e.reduce_sum(smal[0:1, 227:228], smal[0:1, 225:227], axis=AXX))
        self.v(lambda e: e.reciprocal(smal[0:1, 228:229], smal[0:1, 227:228]))
        self.v(lambda e: e.tensor_scalar(smal[0:1, 229:231], smal[0:1, 225:227], smal[0:1, 228:229], None, op0=ALU.mult))
        self.v(lambda e: (e.tensor_tensor(smal[0:1, 231:239], smal[0:1, 198:206], self.cst[0:1, 0:8], op=ALU.mult),
                          e.tensor_tensor(smal[0:1, 239:247], smal[0:1, 215:223], self.cst[0:1, 0:8], op=ALU.mult))[-1])
        self.v(lambda e: (e.reduce_sum(smal[0:1, 92:93], smal[0:1, 231:239], axis=AXX),
                          e.reduce_sum(smal[0:1, 93:94], smal[0:1, 239:247], axis=AXX))[-1])
        def bcasts(e):
            e.matmul(self.pb[0][:, 0:1], self.ones_row, smal[0:1, 92:93], start=True, stop=True)
            e.matmul(self.pb[0][:, 1:2], self.ones_row, smal[0:1, 93:94], start=True, stop=True)
            e.matmul(self.pb[0][:, 2:3], self.ones_row, smal[0:1, 229:230], start=True, stop=True)
            return e.matmul(self.pb[0][:, 3:4], self.ones_row, smal[0:1, 230:231], start=True, stop=True)
        self.t(bcasts)
        self.v(lambda e: e.tensor_copy(smal[:, 231:235], self.pb[0][:, 0:4]))
        def selcols(e):
            e.transpose(self.pb[0][0:8, 16:17], smal[0:1, 198:206], eyef[0:1, 0:1])
            return e.transpose(self.pb[0][0:8, 17:18], smal[0:1, 215:223], eyef[0:1, 0:1])
        self.t(selcols)
        self.v(lambda e: e.tensor_copy(smal[0:8, 238:240], self.pb[0][0:8, 16:18]))
        self.v(lambda e: e.tensor_copy(self.bx[1][0:8, 2548:2550], smal[0:8, 238:240]))
        self.v(lambda e: e.tensor_copy(smal[:, 236:237], self.cstb[:, 513:514]))
        self.v(lambda e: e.tensor_scalar(smal[:, 236:237], smal[:, 236:237], 1.0 / 128.0, None, op0=ALU.mult))
        self.dump("router", smal[0:1, 170:247], (1, 77))
        for ee in range(2):
            self._moe_expert(ee, t_sel=238 + ee, t_e=231 + ee, t_wt=233 + ee)
        ga = self.bq[0][:, :].bitcast(F32)[:, 0:1024]
        be = self.bq[1][:, :].bitcast(F32)[:, 0:1024]
        P.dmas("sync", [(ga, IN["gamma"]), (be, IN["beta"])])
        for t in range(16):
            at, lnf = self.repa, self.repb
            m0, r0, v0 = smal[:, 49:50], smal[:, 50:51], smal[:, 51:52]
            P.dma("sync", at[:, :], self.DSC["acc"][128 * t:128 * t + 128, :])
            self.v(lambda e: e.reduce_sum(m0, at[:, :], axis=AXX))
            self.v(lambda e: e.tensor_scalar(m0, m0, 1.0 / D, None, op0=ALU.mult))
            self.v(lambda e: e.tensor_scalar(lnf[:, :], at[:, :], m0, None, op0=ALU.subtract))
            self.s(lambda e: e.activation(at[:, :], lnf[:, :], AF.Square, accum_out=v0))
            self.v(lambda e: e.tensor_scalar(v0, v0, 1.0 / D, None, op0=ALU.mult))
            self.rsqrt(r0, v0)
            self.v(lambda e: e.tensor_scalar(lnf[:, :], lnf[:, :], r0, None, op0=ALU.mult))
            self.v(lambda e: e.tensor_tensor(lnf[:, :], lnf[:, :], ga, op=ALU.mult))
            self.v(lambda e: e.tensor_tensor(lnf[:, :], lnf[:, :], be, op=ALU.add))
            P.dma("sync", at[:, :], self.DSC["y3"][128 * t:128 * t + 128, :])
            self.v(lambda e: e.tensor_tensor(lnf[:, :], lnf[:, :], at[:, :], op=ALU.add))
            P.dma("sync", self.out[128 * t:128 * t + 128, :], lnf[:, :])

    def _w2slot(self, kt2):
        bufs = ([(self.bq[i], o) for i in range(4) for o in (0, 1024)]
                + [(self.bk[i], o) for i in range(4) for o in (0, 1024)]
                + [(self.bv[i], o) for i in range(4) for o in (0, 1024)]
                + [(self.wb[0], o) for o in (0, 1024, 2048, 3072)]
                + [(self.r[16][:, :].bitcast(BF16), o) for o in (0, 1024)]
                + [(self.r[17][:, :].bitcast(BF16), o) for o in (0, 1024)])
        b, o = bufs[kt2]
        return b[:, o:o + 1024]

    def _moe_expert(self, ee, t_sel, t_e, t_wt):
        P, IN, smal, idx = self.P, self.IN, self.smal, self.idx
        selT = self.bk[0][0:8, 2048:2176]
        self.t(lambda e: e.matmul(self.pb[1][0:8, 0:128], smal[0:1, (198 if ee == 0 else 215):(206 if ee == 0 else 223)],
                                  self.ones_row, start=True, stop=True))
        P.op16("gpsimd", lambda e: e.dma_start(self.wb[1][0:8, 0:1024], IN["moe_b2"]))
        self.v(lambda e: e.tensor_copy(selT, self.pb[1][0:8, 0:128]))
        selb = self.bx[1][0:8, 2548 + ee:2549 + ee]
        def mmb2(e, selb=selb):
            e.matmul(self.pb[2][0:1, 0:512], selb, self.wb[1][0:8, 0:512], start=True, stop=True)
            return e.matmul(self.pb[3][0:1, 0:512], selb, self.wb[1][0:8, 512:1024], start=True, stop=True)
        self.t(mmb2)
        self.v(lambda e: (e.tensor_copy(self.bv[0][0:1, 2048:2560], self.pb[2][0:1, 0:512]),
                          e.tensor_copy(self.bv[1][0:1, 2048:2560], self.pb[3][0:1, 0:512]))[-1])
        P.op16("gpsimd", lambda e: e.dma_start(self.wb[1][0:8, 0:4096], IN["moe_b1"]))
        self.v(lambda e, t_e=t_e: e.tensor_scalar(smal[:, 235:236], smal[:, t_e:t_e + 1], 1024.0, None, op0=ALU.mult))
        self.v(lambda e: e.tensor_tensor(smal[:, 235:236], smal[:, 235:236], smal[:, 236:237], op=ALU.add))
        def i1(e):
            last = None
            for kt in range(8):
                last = e.tensor_scalar(smal[:, 72 + kt:73 + kt], smal[:, 235:236], float(128 * kt), None, op0=ALU.add)
            return last
        self.v(i1)
        self.v(lambda e: e.tensor_copy(idx[:, 0:8], smal[:, 72:80]))
        for kt in range(8):
            P.op16("gpsimd", lambda e, kt=kt: e.indirect_dma_start(
                self.wb[2][:, :], None, IN["moe_w1"],
                IndirectOffsetOnAxis(ap=idx[:, kt:kt + 1], axis=0)))
            self.v(lambda e, kt=kt: (
                e.tensor_copy(self.r[2 * kt][:, :].bitcast(BF16), self.wb[2][:, 0:2048]),
                e.tensor_copy(self.r[2 * kt + 1][:, :].bitcast(BF16), self.wb[2][:, 2048:4096]))[-1])
        self.v(lambda e, t_e=t_e: e.tensor_scalar(smal[:, 235:236], smal[:, t_e:t_e + 1], 4096.0, None, op0=ALU.mult))
        self.v(lambda e: e.tensor_tensor(smal[:, 235:236], smal[:, 235:236], smal[:, 236:237], op=ALU.add))
        def i2(e):
            last = None
            for kt in range(32):
                last = e.tensor_scalar(smal[:, 72 + kt:73 + kt], smal[:, 235:236], float(128 * kt), None, op0=ALU.add)
            return last
        self.v(i2)
        self.v(lambda e: e.tensor_copy(idx[:, 0:32], smal[:, 72:104]))
        for kt in range(32):
            P.op16("gpsimd", lambda e, kt=kt: e.indirect_dma_start(
                self._w2slot(kt), None, IN["moe_w2"],
                IndirectOffsetOnAxis(ap=idx[:, kt:kt + 1], axis=0)))
        for t in range(16):
            P.dma("sync", self.bx[0][:, 0:1024].rearrange("p (k n) -> p k n", n=128),
                  self.DSC["ln3F"][t:t + 1, :, :, :].rearrange("a k p n -> (a p) k n"))
            for half in range(2):
                def mmh1(e, half=half):
                    last = None
                    for b in range(4):
                        for kk in range(8):
                            e.matmul(self.pb[b][:, 0:512], self.bx[0][:, 128 * kk:128 * kk + 128],
                                     self.r[2 * kk + half][:, :].bitcast(BF16)[:, 512 * b:512 * b + 512],
                                     start=(kk == 0), stop=False)
                        last = e.matmul(self.pb[b][:, 0:512], selT,
                                        self.wb[1][0:8, 2048 * half + 512 * b:2048 * half + 512 * b + 512],
                                        start=False, stop=True)
                    return last
                self.t(mmh1)
                def gel(e, half=half):
                    last = None
                    for b in range(4):
                        last = e.activation(self.wb[2][:, 2048 * half + 512 * b:2048 * half + 512 * b + 512],
                                            self.pb[b][:, 0:512], AF.Gelu)
                    return last
                self.s(gel)
            for grp in range(8):
                def trs(e, grp=grp):
                    last = None
                    for q in range(4):
                        kt2 = 4 * grp + q
                        last = e.transpose(self.pbt[:, 128 * q:128 * q + 128],
                                          self.wb[2][:, 128 * kt2:128 * kt2 + 128], self.eyeb)
                    return last
                self.t(trs)
                def cpy(e, grp=grp):
                    last = None
                    for q in range(4):
                        kt2 = 4 * grp + q
                        dst = (self.bx[0] if kt2 < 16 else self.bx[1])[:, 128 * (kt2 % 16):128 * (kt2 % 16) + 128]
                        last = e.tensor_copy(dst, self.pbt[:, 128 * q:128 * q + 128])
                    return last
                self.v(cpy)
            def mmo(e):
                last = None
                for nh in range(2):
                    for kt2 in range(32):
                        lhs = (self.bx[0] if kt2 < 16 else self.bx[1])[:, 128 * (kt2 % 16):128 * (kt2 % 16) + 128]
                        e.matmul(self.pb[4 + nh][:, 0:512], lhs,
                                 self._w2slot(kt2)[:, 512 * nh:512 * nh + 512],
                                 start=(kt2 == 0), stop=False)
                    last = e.matmul(self.pb[4 + nh][:, 0:512], self.onesb_row,
                                    self.bv[nh][0:1, 2048:2560], start=False, stop=True)
                return last
            self.t(mmo)
            P.dma("sync", self.repa[:, :], self.DSC["acc"][128 * t:128 * t + 128, :])
            def accu(e, t_wt=t_wt):
                e.scalar_tensor_tensor(self.repa[:, 0:512], self.pb[4][:, 0:512], smal[:, t_wt:t_wt + 1],
                                       self.repa[:, 0:512], op0=ALU.mult, op1=ALU.add)
                return e.scalar_tensor_tensor(self.repa[:, 512:1024], self.pb[5][:, 0:512], smal[:, t_wt:t_wt + 1],
                                              self.repa[:, 512:1024], op0=ALU.mult, op1=ALU.add)
            self.v(accu)
            P.dma("sync", self.DSC["acc"][128 * t:128 * t + 128, :], self.repa[:, :])


def kernel(**inputs):
    shared, per_core = _host_stage(inputs)
    nc, dbg_aps = build_program(debug=DEBUG)
    in_maps = []
    for c in range(NC):
        m = dict(shared)
        m.update(per_core[c])
        in_maps.append({k: np.ascontiguousarray(v, dtype=np.float32) for k, v in m.items()})
    res = run_bass_kernel_spmd(nc, in_maps, core_ids=list(range(NC)))
    kernel.last_results = res.results
    outs = [np.asarray(res.results[c]["out"]) for c in range(NC)]
    return np.concatenate(outs, 0).reshape(1, T, D)


# version 02 (final)
# speedup vs baseline: 1.0136x; 1.0136x over previous
import numpy as np
import contextlib
import concourse.bass as bass
import concourse.mybir as mybir
from concourse.bass_utils import run_bass_kernel_spmd
from concourse.bass_core_rust import IndirectOffsetOnAxis

F32 = mybir.dt.float32
BF16 = mybir.dt.bfloat16
U32 = mybir.dt.uint32
I8 = mybir.dt.int8
AF = mybir.ActivationFunctionType
ALU = mybir.AluOpType
AXX = mybir.AxisListType.X

NC, S, T, D = 8, 2048, 16384, 1024
XT, AT, XZT = 20, 18, 17
XCOLS, ACOLS, XZCOLS = 2560, 2304, 2176
MAGIC = 12582912.0
NEG = -1e30

DEBUG = []



def _host_stage(inputs):
    g = lambda k: np.asarray(inputs[k], dtype=np.float32)
    x = np.ascontiguousarray(g("L0_attn_x").reshape(T, D))
    rep = lambda v: np.ascontiguousarray(np.tile(np.asarray(v, np.float32).reshape(1, -1), (128, 1)))
    shared = {
        "w_qkv": g("L0_attn_wqkv_l"), "w_ol": g("L0_attn_wo_l"),
        "w_qh": g("L0_attn_wq_h"), "w_kvh": g("L0_attn_wkv_h"), "w_oh": g("L0_attn_wo_h"),
        "gam_h": np.ascontiguousarray(g("L0_attn_gamma_h").reshape(8, 128).T),
        "rtq_rep": rep(g("L0_attn_rt_q")), "rtkv_rep": rep(g("L0_attn_rt_kv")),
        "lngb": np.concatenate([rep(g("L0_attn_ln_g")), rep(g("L0_attn_ln_b"))], 1),
        "w1T": np.ascontiguousarray(g("L0_ffn_w1").T),
        "w2T": np.ascontiguousarray(g("L0_ffn_w2").T),
        "cw0r": np.ascontiguousarray(g("L0_ffn_conv_w").reshape(32, 128, 3).transpose(1, 0, 2).reshape(128, 96)),
        "cb0r": np.ascontiguousarray(g("L0_ffn_conv_b").reshape(32, 128).T),
        "w_in_z": np.ascontiguousarray(g("L1_attn_W_in")[:, 0:2048]),
        "w_in_fm": None,
        "cw1r": np.ascontiguousarray(g("L1_attn_conv_w").reshape(17, 128, 4).transpose(1, 0, 2).reshape(128, 68)),
        "cb1r": np.ascontiguousarray(g("L1_attn_conv_b").reshape(17, 128).T),
        "a_log": g("L1_attn_A_log").reshape(32, 1), "d_par": g("L1_attn_D_param").reshape(32, 1),
        "dt_b": g("L1_attn_dt_bias").reshape(32, 1), "nrm_w": rep(g("L1_attn_norm_w")),
        "w_out": g("L1_attn_W_out"),
        "moe_w1": np.ascontiguousarray(g("L1_ffn_W1").reshape(8 * 1024, 4096)),
        "moe_b1": g("L1_ffn_b1"),
        "moe_w2": np.ascontiguousarray(g("L1_ffn_W2").reshape(8 * 4096, 1024)),
        "moe_b2": g("L1_ffn_b2"),
        "w_r": g("L1_ffn_Wr"), "b_r": g("L1_ffn_br").reshape(1, 8),
        "gamma": rep(g("L1_ffn_gamma")), "beta": rep(g("L1_ffn_beta")),
        "xfull": x,
        "iota_r": np.tile(np.arange(1024, dtype=np.float32)[None, :], (128, 1)),
    }
    w_in = g("L1_attn_W_in")
    bc = w_in[:, 2048:4224].reshape(8, 128, 17, 128).transpose(1, 0, 2, 3).reshape(128, 8 * 17 * 128)
    dtp = np.zeros((128, 8, 128), np.float32)
    dtp[:, :, 0:32] = w_in[:, 4224:4256].reshape(8, 128, 32).transpose(1, 0, 2)
    shared["w_in_fm"] = np.ascontiguousarray(np.concatenate([bc, dtp.reshape(128, 1024)], 1))
    c2 = np.zeros((128, 768), np.float32)
    c2[:, 0:128] = (np.arange(128)[:, None] < np.arange(128)[None, :])
    c2[:, 256:384] = np.where(np.arange(128)[:, None] > np.arange(128)[None, :], NEG, 0.0)
    c2[:, 384:512] = np.eye(128)
    c2[:, 512] = 1.0
    c2[0, 640:768] = 1.0
    cb = np.zeros((128, 1024), np.float32)
    cb[0, 0:64] = 1.0; cb[1, 64:128] = 1.0; cb[2, 0:128] = 1.0
    cb[:, 128:256] = np.eye(128)
    cb[:, 256] = 1.0
    cb[0, 257:385] = 1.0
    cb[:, 385:513] = np.tile(np.arange(128, dtype=np.float32)[None, :], (128, 1))
    cb[:, 513] = np.arange(128) * 128.0
    cb[:, 514:642] = (np.arange(128)[:, None] <= np.arange(128)[None, :])
    cb[:, 642:770] = 1.0
    shared.update({"cst2c": c2, "cstbc": cb,
                   "hsel": np.repeat(np.eye(32, dtype=np.float32), 64, 1)})

    per_core = []
    for c in range(NC):
        start = c * S
        xrows = np.clip(np.arange(start - 256, start - 256 + XCOLS), 0, T - 1)
        maskV = np.zeros((AT, 4, 256), np.float32)
        for t in range(AT):
            maskV[t, 0, 192:] = NEG
            maskV[t, 1, :64] = NEG
            gcols = start - 192 + 128 * t + np.arange(256)
            maskV[t, 2, (gcols < 0) | (gcols >= T)] = NEG
        tokrow = np.tile((start - 128 + np.arange(ACOLS, dtype=np.float32))[None, :], (128, 1))
        PP, FF = np.meshgrid(np.arange(128), np.arange(128), indexing="ij")
        gtok = (PP // 16) * 2048 + (FF % 16) * 128 + (PP % 16) * 8 + FF // 16
        ghi = (PP // 16) * 16 + (FF % 16)
        glo = (PP % 16) * 8 + FF // 16
        hmask = np.ones((128, 2), np.float32)
        if c == 0:
            hmask[:, 0] = 0.0
        if c == NC - 1:
            hmask[:, 1] = 0.0
        xzmask = np.zeros((128, 1), np.float32) if c == 0 else np.ones((128, 1), np.float32)
        m8 = np.zeros((1, 8), np.float32); m8[0, c] = 1.0
        pm8 = np.zeros((128, 8), np.float32); pm8[:, :c] = 1.0
        per_core.append({"xloc": np.ascontiguousarray(x[xrows]),
                         "maskV": np.ascontiguousarray(maskV), "tokrow": tokrow,
                         "gtok": gtok.astype(np.float32), "ghi": ghi.astype(np.float32),
                         "glo": glo.astype(np.float32), "hmask": hmask,
                         "xzmask": xzmask, "m8": m8, "pm8": pm8})
    return shared, per_core



class Prog:
    def __init__(self):
        self.phases = []

    def dma(self, eng, dst, src, slow=False):
        self.phases.append((eng, "dma", [(dst, src, slow)]))

    def dmas(self, eng, pairs, slow=False):
        self.phases.append((eng, "dma", [(d, s, slow) for d, s in pairs]))

    def op(self, eng, fn):
        self.phases.append((eng, "op", fn))

    def op16(self, eng, fn):
        self.phases.append((eng, "op16", fn))

    def pdma(self, items):
        self.phases.append(("*", "pdma", items))

    def emit(self, blk, sem):
        starts, c = [], 0
        for eng, kind, payload in self.phases:
            starts.append(c)
            c += 16 * len(payload) if kind in ("dma", "pdma") else (16 if kind == "op16" else 1)
        total = c

        def run(engobj, engname):
            for (eng, kind, payload), s0 in zip(self.phases, starts):
                if kind == "pdma":
                    if not any(p[0] == engname for p in payload):
                        continue
                elif eng != engname:
                    continue
                engobj.wait_ge(sem, s0)
                if kind == "pdma":
                    for peng, dst, src in payload:
                        if peng == engname:
                            engobj.dma_start(dst, src).then_inc(sem, 16)
                elif kind == "dma":
                    for dst, src, slow in payload:
                        engobj.dma_start(dst, src, allow_slow_non_contiguous=slow).then_inc(sem, 16)
                elif kind == "op16":
                    payload(engobj).then_inc(sem, 16)
                else:
                    last = payload(engobj)
                    (engobj.nop() if last is None else last).then_inc(sem, 1)

        @blk.sync
        def _(e): run(e, "sync")
        @blk.vector
        def _(e): run(e, "vector")
        @blk.scalar
        def _(e): run(e, "scalar")
        @blk.tensor
        def _(e): run(e, "tensor")
        @blk.gpsimd
        def _(e):
            run(e, "gpsimd")
            e.wait_ge(sem, total)


IN_SHAPES = [
    ("w_qkv", (1024, 1536)), ("w_ol", (512, 1024)), ("w_qh", (1024, 512)),
    ("w_kvh", (1024, 1024)), ("w_oh", (512, 1024)), ("gam_h", (128, 8)),
    ("rtq_rep", (128, D)), ("rtkv_rep", (128, D)), ("lngb", (128, 2048)),
    ("w1T", (1024, 4096)), ("w2T", (4096, 1024)), ("cw0r", (128, 96)), ("cb0r", (128, 32)),
    ("w_in_z", (1024, 2048)), ("w_in_fm", (128, 18432)), ("cw1r", (128, 68)), ("cb1r", (128, 17)),
    ("a_log", (32, 1)), ("d_par", (32, 1)), ("dt_b", (32, 1)), ("nrm_w", (128, 2048)),
    ("w_out", (2048, 1024)), ("moe_w1", (8 * 1024, 4096)), ("moe_b1", (8, 4096)),
    ("moe_w2", (8 * 4096, 1024)), ("moe_b2", (8, 1024)), ("w_r", (1024, 8)), ("b_r", (1, 8)),
    ("gamma", (128, D)), ("beta", (128, D)), ("xfull", (T, D)),
    ("iota_r", (128, 1024)), ("cst2c", (128, 768)), ("cstbc", (128, 1024)),
    ("hsel", (32, 2048)),
    ("xloc", (XCOLS, D)), ("maskV", (AT, 4, 256)), ("tokrow", (128, ACOLS)),
    ("gtok", (128, 128)), ("ghi", (128, 128)), ("glo", (128, 128)),
    ("hmask", (128, 2)), ("xzmask", (128, 1)), ("m8", (1, 8)), ("pm8", (128, 8)),
]


def build_program(debug=None):
    nc = bass.Bass("TRN2", target_bir_lowering=False)
    IN = {n: nc.dram_tensor(n, list(sh), F32, kind="ExternalInput").ap() for n, sh in IN_SHAPES}
    out = nc.dram_tensor("out", [S, D], F32, kind="ExternalOutput").ap()
    dbg_aps = {}
    dr = lambda n, sh, dt=F32: nc.dram_tensor(n, list(sh), dt).ap()
    DSC = {
        "y1": dr("d_y1", (ACOLS, D)),
        "z": dr("d_z", (XZCOLS, 2048), BF16),
        "hvt": dr("d_hvt", (1024, 1024), BF16),
        "hq": dr("d_hq", (AT, 128, 32, 128), I8),
        "y2": dr("d_y2", (ACOLS, D)),
        "invs": dr("d_invs", (2305, 1)),
        "Y": dr("d_Y", (S, 2048), BF16),
        "cc": dr("d_cc", (16, 64, 128), BF16),
        "fac": dr("d_fac", (16, 128, 32)),
        "y3": dr("d_y3", (S, D)),
        "ln3F": dr("d_ln3F", (16, 8, 128, 128), BF16),
        "acc": dr("d_acc", (S, D)),
    }
    CC = {
        "sc_in": nc.dram_tensor("cc_sc_in", [1, 2 * S], F32),
        "sc_out": nc.dram_tensor("cc_sc_out", [NC, 2 * S], F32),
        "mb_in": nc.dram_tensor("cc_mb_in", [1, 32 * 64 * 64 + 64], F32),
        "mb_out": nc.dram_tensor("cc_mb_out", [NC, 32 * 64 * 64 + 64], F32),
        "mu_in": nc.dram_tensor("cc_mu_in", [1, D], F32),
        "mu_out": nc.dram_tensor("cc_mu_out", [1, D], F32),
    }
    ctx = contextlib.ExitStack()
    with ctx:
        sb = lambda n, sh, dt=F32: ctx.enter_context(nc.sbuf_tensor(n, list(sh), dt))
        ps = lambda n, sh, dt=F32: ctx.enter_context(nc.psum_tensor(n, list(sh), dt))
        B = {
            "r": [sb(f"r{i}", (128, D)) for i in range(XT)],
            "bq": [sb(f"bq{i}", (128, 2304), BF16) for i in range(4)],
            "bk": [sb(f"bk{i}", (128, 2432), BF16) for i in range(4)],
            "bv": [sb(f"bv{i}", (128, 2560), BF16) for i in range(4)],
            "bx": [sb(f"bx{i}", (128, 2560), BF16) for i in range(2)],
            "wb": [sb(f"wb{i}", (128, 4096), BF16) for i in range(3)],
            "st": sb("st", (128, 512)),
            "st2": sb("st2", (128, 512)),
            "cst": sb("cst", (128, 1024)),
            "cst2": sb("cst2", (128, 768)),
            "cstb": sb("cstb", (128, 1024), BF16),
            "repa": sb("repa", (128, 1024)),
            "repb": sb("repb", (128, 1024)),
            "idx": sb("idx", (128, 32), U32),
            "smal": sb("smal", (128, 256)),
            "pb": [ps(f"pb{i}", (128, 512)) for i in range(6)],
            "pbt": ps("pbt", (128, 512), BF16),
        }
        sem = ctx.enter_context(nc.semaphore("S"))
        with nc.Block() as blk:
            P = Prog()
            EM = Emitter(nc, P, IN, out, DSC, CC, B, debug or [], dbg_aps)
            EM.forward()
            P.emit(blk, sem)
    return nc, dbg_aps


class Emitter:
    def __init__(self, nc, P, IN, out, DSC, CC, B, debug, dbg_aps):
        self.nc, self.P, self.IN, self.out = nc, P, IN, out
        self.DSC, self.CC = DSC, CC
        self.debug, self.dbg_aps = debug, dbg_aps
        for k2, v2 in B.items():
            setattr(self, k2, v2)

    def dump(self, name, src_ap, shape, dt=F32):
        if name in self.debug:
            d = self.nc.dram_tensor("dbg_" + name, list(shape), dt, kind="ExternalOutput").ap()
            self.dbg_aps[name] = d
            self.P.dma("scalar", d, src_ap)

    def v(self, fn): self.P.op("vector", fn)
    def s(self, fn): self.P.op("scalar", fn)
    def t(self, fn): self.P.op("tensor", fn)
    def gp(self, fn): self.P.op("gpsimd", fn)

    def rsqrt(self, dst, src):
        eps = self.smal[:, 255:256]
        self.s(lambda e: e.activation(dst, src, AF.Sqrt, bias=eps))
        self.v(lambda e: e.reciprocal(dst, dst))

    @property
    def ones_row(self): return self.cst2[0:1, 640:768]
    @property
    def ones_col(self): return self.cst2[:, 512:513]
    @property
    def onesb_col(self): return self.cstb[:, 256:257]
    @property
    def onesb_row(self): return self.cstb[0:1, 257:385]
    @property
    def eyeb(self): return self.cstb[:, 128:256]

    def ptot(self, cnt_cols, out_cols, n):
        p = self.pb[5]
        self.t(lambda e: e.matmul(p[0:1, 0:n], self.ones_col, cnt_cols, start=True, stop=True))
        self.v(lambda e: e.tensor_copy(self.smal[0:1, 240:240 + n], p[0:1, 0:n]))
        self.t(lambda e: e.matmul(p[:, 4:4 + n], self.ones_row, self.smal[0:1, 240:240 + n], start=True, stop=True))
        self.v(lambda e: e.tensor_copy(out_cols, p[:, 4:4 + n]))

    def forward(self):
        import os
        lim = int(os.environ.get("KSTAGE", "99"))
        stages = [self.setup, self.stageA_ln_scores, self.stageB_routing,
                  self.stageC_heavy, self.stageD_qkv, self.stageE_light,
                  self.stageF_l0ffn, self.stageG_mamba]
        for i, fn in enumerate(stages):
            if i <= lim:
                fn()
        self.stageH_moe()

    def setup(self):
        P, IN = self.P, self.IN
        P.dmas("sync", [(self.cst[:, :], IN["iota_r"]), (self.cst2[:, :], IN["cst2c"])])
        P.op16("gpsimd", lambda e: e.dma_start(self.cstb[:, :], IN["cstbc"]))
        P.dmas("sync", [(self.r[i][:, :], IN["xloc"][128 * i:128 * (i + 1), :]) for i in range(XT)])
        self.v(lambda e: (e.memset(self.smal[:, 255:256], 1e-5),
                          e.memset(self.smal[:, 254:255], 0.0))[-1])

    def stageA_ln_scores(self):
        P, IN = self.P, self.IN
        st, st2 = self.st, self.st2
        P.dmas("sync", [(self.repa[:, :], IN["rtq_rep"]), (self.repb[:, :], IN["rtkv_rep"])])
        scr = self.bq[0][:, 0:2048].bitcast(F32)
        for i in range(2, 18):
            self.v(lambda e, i=i: e.tensor_tensor(scr[:, 0:1024], self.r[i][:, :], self.repa[:, :], op=ALU.mult))
            self.v(lambda e, i=i: e.reduce_sum(st2[:, 128 + i - 2:129 + i - 2], scr[:, 0:1024], axis=AXX))
            self.v(lambda e, i=i: e.tensor_tensor(scr[:, 0:1024], self.r[i][:, :], self.repb[:, :], op=ALU.mult))
            self.v(lambda e, i=i: e.reduce_sum(st2[:, 160 + i - 2:161 + i - 2], scr[:, 0:1024], axis=AXX))
        self.v(lambda e: e.reduce_sum(st2[0:1, 6:7], self.repa[0:1, :], axis=AXX))
        self.v(lambda e: e.reduce_sum(st2[0:1, 7:8], self.repb[0:1, :], axis=AXX))
        p5 = self.pb[5]
        self.t(lambda e: e.matmul(p5[:, 0:2], self.ones_row, st2[0:1, 6:8], start=True, stop=True))
        self.v(lambda e: e.tensor_copy(st2[:, 4:6], p5[:, 0:2]))

        for i in range(XT):
            ri, c0 = self.r[i], i
            m0 = st[:, c0:c0 + 1]
            r0 = st[:, 20 + c0:21 + c0]
            al = st[:, 40 + c0:41 + c0]
            be = st[:, 60 + c0:61 + c0]
            self.v(lambda e, ri=ri, m0=m0: e.reduce_sum(m0, ri[:, :], axis=AXX))
            self.v(lambda e, m0=m0: e.tensor_scalar(m0, m0, 1.0 / D, None, op0=ALU.mult))
            self.v(lambda e, ri=ri, m0=m0: e.tensor_scalar(self.repa[:, :], ri[:, :], m0, None, op0=ALU.subtract))
            self.s(lambda e: e.activation(self.repb[:, :], self.repa[:, :], AF.Square, accum_out=st2[:, 0:1]))
            self.v(lambda e: e.tensor_scalar(st2[:, 0:1], st2[:, 0:1], 1.0 / D, None, op0=ALU.mult))
            self.rsqrt(r0, st2[:, 0:1])
            if 2 <= i < 18:
                for rr, (xc, sc) in enumerate([(128, 64), (160, 96)]):
                    xd = st2[:, xc + i - 2:xc + 1 + i - 2]
                    so = st2[:, sc + i - 2:sc + 1 + i - 2]
                    self.v(lambda e, m0=m0, rr=rr: e.tensor_tensor(st2[:, 1:2], m0, st2[:, 4 + rr:5 + rr], op=ALU.mult))
                    self.v(lambda e, xd=xd: e.tensor_tensor(st2[:, 1:2], xd, st2[:, 1:2], op=ALU.subtract))
                    self.v(lambda e, so=so, r0=r0: e.tensor_tensor(so, st2[:, 1:2], r0, op=ALU.mult))
            self.v(lambda e, r0=r0: e.tensor_scalar(self.repa[:, :], self.repa[:, :], r0, None, op0=ALU.mult))
            self.v(lambda e: e.reduce_sum(st2[:, 1:2], self.repa[:, :], axis=AXX))
            self.v(lambda e: e.tensor_scalar(st2[:, 1:2], st2[:, 1:2], 1.0 / D, None, op0=ALU.mult))
            self.v(lambda e: e.tensor_scalar(self.repb[:, :], self.repa[:, :], st2[:, 1:2], None, op0=ALU.subtract))
            self.s(lambda e: e.activation(self.repb[:, :], self.repb[:, :], AF.Square, accum_out=st2[:, 2:3]))
            self.v(lambda e: e.tensor_scalar(st2[:, 2:3], st2[:, 2:3], 1.0 / D, None, op0=ALU.mult))
            self.rsqrt(st2[:, 3:4], st2[:, 2:3])
            self.v(lambda e, r0=r0, al=al: e.tensor_tensor(al, r0, st2[:, 3:4], op=ALU.mult))
            self.v(lambda e, m0=m0, r0=r0, be=be: e.tensor_tensor(be, m0, r0, op=ALU.mult))
            self.v(lambda e, be=be: e.tensor_tensor(be, be, st2[:, 1:2], op=ALU.add))
            self.v(lambda e, be=be: e.tensor_tensor(be, be, st2[:, 3:4], op=ALU.mult))
            self.v(lambda e, be=be: e.tensor_scalar(be, be, -1.0, None, op0=ALU.mult))
        self.dump("st", st[:, :], (128, 512))
        self.dump("st2", st2[:, :], (128, 512))
        self.dump("scq", st2[:, 64:80], (128, 16))
        self.dump("sckv", st2[:, 96:112], (128, 16))

    def stageB_routing(self):
        P, IN, CC = self.P, self.IN, self.CC
        st2, smal, idx = self.st2, self.smal, self.idx
        P.dma("sync", CC["sc_in"].ap()[0:1, 0:2048].rearrange("a (p c) -> (a p) c", p=128), st2[:, 64:80])
        P.dma("sync", CC["sc_in"].ap()[0:1, 2048:4096].rearrange("a (p c) -> (a p) c", p=128), st2[:, 96:112])
        self.gp(lambda e: e.collective_compute(
            "AllGather", ALU.bypass, replica_groups=[list(range(NC))],
            ins=[CC["sc_in"].ap().opt()], outs=[CC["sc_out"].ap().opt()]))
        P.dmas("sync", [(self.repa[16 * c:16 * (c + 1), 0:128],
                         CC["sc_out"].ap()[c:c + 1, 0:2048].rearrange("a (p c) -> (a p) c", p=16))
                        for c in range(NC)])
        P.dmas("sync", [(self.repb[16 * c:16 * (c + 1), 0:128],
                         CC["sc_out"].ap()[c:c + 1, 2048:4096].rearrange("a (p c) -> (a p) c", p=16))
                        for c in range(NC)])
        self.dump("scall", self.repa[:, 0:128], (128, 128))
        def binit(e):
            e.memset(smal[:, 0:1], -4096.0)
            e.memset(smal[:, 1:2], 4096.0)
            e.memset(smal[:, 2:3], -4096.0)
            return e.memset(smal[:, 3:4], 4096.0)
        self.v(binit)
        scr = self.bq[1][:, 0:512].bitcast(F32)
        for it in range(40):
            def half1(e):
                e.tensor_tensor(smal[:, 4:5], smal[:, 0:1], smal[:, 1:2], op=ALU.add)
                return e.tensor_tensor(smal[:, 5:6], smal[:, 2:3], smal[:, 3:4], op=ALU.add)
            self.v(half1)
            def half2(e):
                e.tensor_scalar(smal[:, 4:5], smal[:, 4:5], 0.5, None, op0=ALU.mult)
                return e.tensor_scalar(smal[:, 5:6], smal[:, 5:6], 0.5, None, op0=ALU.mult)
            self.v(half2)
            def cmps(e):
                e.tensor_scalar(scr[:, 0:128], self.repa[:, 0:128], smal[:, 4:5], None, op0=ALU.is_gt)
                return e.tensor_scalar(scr[:, 128:256], self.repb[:, 0:128], smal[:, 5:6], None, op0=ALU.is_gt)
            self.v(cmps)
            def reds(e):
                e.reduce_sum(smal[:, 8:9], scr[:, 0:128], axis=AXX)
                return e.reduce_sum(smal[:, 9:10], scr[:, 128:256], axis=AXX)
            self.v(reds)
            self.ptot(smal[:, 8:10], smal[:, 10:12], 2)
            def ges(e):
                e.tensor_scalar(smal[:, 12:13], smal[:, 10:11], 1024.0, None, op0=ALU.is_ge)
                return e.tensor_scalar(smal[:, 13:14], smal[:, 11:12], 1024.0, None, op0=ALU.is_ge)
            self.v(ges)
            def d1(e):
                e.tensor_tensor(smal[:, 14:15], smal[:, 4:5], smal[:, 0:1], op=ALU.subtract)
                e.tensor_tensor(smal[:, 15:16], smal[:, 1:2], smal[:, 4:5], op=ALU.subtract)
                e.tensor_tensor(smal[:, 16:17], smal[:, 5:6], smal[:, 2:3], op=ALU.subtract)
                return e.tensor_tensor(smal[:, 17:18], smal[:, 3:4], smal[:, 5:6], op=ALU.subtract)
            self.v(d1)
            def d2(e):
                e.tensor_tensor(smal[:, 14:15], smal[:, 14:15], smal[:, 12:13], op=ALU.mult)
                e.tensor_tensor(smal[:, 15:16], smal[:, 15:16], smal[:, 12:13], op=ALU.mult)
                e.tensor_tensor(smal[:, 16:17], smal[:, 16:17], smal[:, 13:14], op=ALU.mult)
                return e.tensor_tensor(smal[:, 17:18], smal[:, 17:18], smal[:, 13:14], op=ALU.mult)
            self.v(d2)
            def d3(e):
                e.tensor_tensor(smal[:, 0:1], smal[:, 0:1], smal[:, 14:15], op=ALU.add)
                e.tensor_tensor(smal[:, 1:2], smal[:, 4:5], smal[:, 15:16], op=ALU.add)
                e.tensor_tensor(smal[:, 2:3], smal[:, 2:3], smal[:, 16:17], op=ALU.add)
                return e.tensor_tensor(smal[:, 3:4], smal[:, 5:6], smal[:, 17:18], op=ALU.add)
            self.v(d3)
        self.dump("thr", smal[:, 0:4], (128, 4))
        self.compact(self.repa, smal[:, 0:1], 0, st2[:, 192:200])
        self.compact(self.repb, smal[:, 2:3], 8, None)
        self.dump("idx", idx[:, 0:16], (128, 16), U32)

    def compact(self, sc, locol, idx_off, valcols):
        P, IN = self.P, self.IN
        smal, st2, idx = self.smal, self.st2, self.idx
        mk = self.bq[2][:, 0:512].bitcast(F32)
        pf = self.bq[2][:, 512:1024].bitcast(F32)
        gr = self.bq[3][:, 0:512].bitcast(F32)
        ghib = self.bq[3][:, 512:768]
        glob_ = self.bq[3][:, 768:1024]
        sfb = self.bq[1][:, 0:2048]
        P.op16("gpsimd", lambda e: e.dma_start(ghib[:, 0:128], IN["ghi"]))
        P.op16("gpsimd", lambda e: e.dma_start(glob_[:, 0:128], IN["glo"]))
        self.v(lambda e: e.tensor_scalar(mk[:, 0:128], sc[:, 0:128], locol, None, op0=ALU.is_gt))
        self.v(lambda e: e.tensor_copy(pf[:, 0:128], mk[:, 0:128]))
        for sh in [1, 2, 4, 8, 16, 32, 64]:
            self.v(lambda e, sh=sh: e.tensor_tensor(pf[:, sh:128], pf[:, sh:128], pf[:, 0:128 - sh], op=ALU.add))
        p5 = self.pb[5]
        self.t(lambda e: e.matmul(p5[:, 8:9], self.cst2[:, 0:128], pf[:, 127:128], start=True, stop=True))
        self.v(lambda e: e.tensor_copy(smal[:, 16:17], p5[:, 8:9]))
        self.v(lambda e: e.tensor_tensor(gr[:, 0:128], pf[:, 0:128], mk[:, 0:128], op=ALU.subtract))
        self.v(lambda e: e.tensor_scalar(gr[:, 0:128], gr[:, 0:128], smal[:, 16:17], None, op0=ALU.add))
        self.v(lambda e: e.tensor_tensor(gr[:, 0:128], gr[:, 0:128], mk[:, 0:128], op=ALU.mult))
        self.v(lambda e: e.tensor_scalar(mk[:, 0:128], mk[:, 0:128], 1.0, None, op0=ALU.subtract))
        self.v(lambda e: e.tensor_tensor(gr[:, 0:128], gr[:, 0:128], mk[:, 0:128], op=ALU.add))
        for f in range(128):
            self.v(lambda e, f=f: e.tensor_scalar(sfb[:, 0:1024], self.cst[:, :], gr[:, f:f + 1], None, op0=ALU.is_equal))
            def mms(e, f=f):
                st_ = (f == 0)
                e.matmul(self.pb[0][0:1, 0:512], ghib[:, f:f + 1], sfb[:, 0:512], start=st_, stop=(f == 127))
                e.matmul(self.pb[1][0:1, 0:512], ghib[:, f:f + 1], sfb[:, 512:1024], start=st_, stop=(f == 127))
                e.matmul(self.pb[2][0:1, 0:512], glob_[:, f:f + 1], sfb[:, 0:512], start=st_, stop=(f == 127))
                return e.matmul(self.pb[3][0:1, 0:512], glob_[:, f:f + 1], sfb[:, 512:1024], start=st_, stop=(f == 127))
            self.t(mms)
        def comb1(e):
            e.tensor_scalar(self.repa[0:1, 0:512], self.pb[0][0:1, 0:512], 128.0, None, op0=ALU.mult)
            return e.tensor_scalar(self.repa[0:1, 512:1024], self.pb[1][0:1, 0:512], 128.0, None, op0=ALU.mult)
        self.v(comb1)
        def comb2(e):
            e.tensor_tensor(self.repa[0:1, 0:512], self.repa[0:1, 0:512], self.pb[2][0:1, 0:512], op=ALU.add)
            return e.tensor_tensor(self.repa[0:1, 512:1024], self.repa[0:1, 512:1024], self.pb[3][0:1, 0:512], op=ALU.add)
        self.v(comb2)
        one11 = self.cst2[0:1, 512:513]
        for j in range(8):
            self.t(lambda e, j=j: e.matmul(p5[:, 20 + j:21 + j], self.repa[0:1, 128 * j:128 * (j + 1)], one11, start=True, stop=True))
        def tocols(e):
            e.tensor_copy(idx[:, idx_off:idx_off + 8], p5[:, 20:28])
            if valcols is not None:
                e.tensor_copy(valcols, p5[:, 20:28])
            return None
        self.v(tocols)

    def stageC_heavy(self):
        P, IN = self.P, self.IN
        st2, smal, idx = self.st2, self.smal, self.idx
        P.dma("sync", smal[:, 16:24], IN["gam_h"])
        zero = smal[:, 254:255]

        def gather_side(idx_off, dst_bufs):
            for j in range(8):
                P.op16("gpsimd", lambda e, j=j: e.indirect_dma_start(
                    dst_bufs[j // 2][:, (j % 2) * 1024:(j % 2) * 1024 + 1024], None,
                    IN["xfull"], IndirectOffsetOnAxis(ap=idx[:, idx_off + j:idx_off + j + 1], axis=0)))

        def u_side(bufs):
            junk = self.repb[:, :].bitcast(BF16)[:, 0:1024]
            for j in range(8):
                sl = bufs[j // 2][:, (j % 2) * 1024:(j % 2) * 1024 + 1024]
                self.v(lambda e, sl=sl: e.reduce_sum(smal[:, 32:33], sl, axis=AXX))
                self.v(lambda e: e.tensor_scalar(smal[:, 32:33], smal[:, 32:33], 1.0 / D, None, op0=ALU.mult))
                self.s(lambda e, sl=sl: e.activation(junk, sl, AF.Square, accum_out=smal[:, 33:34]))
                def var1(e):
                    e.tensor_scalar(smal[:, 33:34], smal[:, 33:34], 1.0 / D, None, op0=ALU.mult)
                    return e.tensor_tensor(smal[:, 34:35], smal[:, 32:33], smal[:, 32:33], op=ALU.mult)
                self.v(var1)
                self.v(lambda e: e.tensor_tensor(smal[:, 33:34], smal[:, 33:34], smal[:, 34:35], op=ALU.subtract))
                self.s(lambda e: e.activation(smal[:, 33:34], smal[:, 33:34], AF.Sqrt, bias=zero))
                self.v(lambda e: e.reciprocal(smal[:, 33:34], smal[:, 33:34]))
                self.v(lambda e, sl=sl: e.tensor_scalar(sl, sl, smal[:, 32:33], smal[:, 33:34], op0=ALU.subtract, op1=ALU.mult))

        def transpose_side(src_bufs):
            for kk in range(8):
                for grp in range(2):
                    def trs(e, kk=kk, grp=grp):
                        last = None
                        for jj in range(4):
                            j = 4 * grp + jj
                            src = src_bufs[j // 2][:, (j % 2) * 1024 + 128 * kk:(j % 2) * 1024 + 128 * kk + 128]
                            last = e.transpose(self.pbt[:, 128 * jj:128 * jj + 128], src, self.eyeb)
                        return last
                    self.t(trs)
                    dst = self.bk[kk // 2][:, (kk % 2) * 1024 + 512 * grp:(kk % 2) * 1024 + 512 * grp + 512]
                    self.v(lambda e, dst=dst: e.tensor_copy(dst, self.pbt[:, 0:512]))

        gather_side(0, self.bq)
        u_side(self.bq)
        transpose_side(self.bq)
        P.op16("gpsimd", lambda e: e.dma_start(
            self.wb[0][:, :].rearrange("p (k n) -> p k n", k=8),
            IN["w_qh"].rearrange("(k p) n -> p k n", p=128)))
        def gscale0(e):
            last = None
            for kk in range(8):
                last = e.tensor_scalar(self.wb[0][:, 512 * kk:512 * kk + 512],
                                       self.wb[0][:, 512 * kk:512 * kk + 512],
                                       smal[:, 16 + kk:17 + kk], None, op0=ALU.mult)
            return last
        self.v(gscale0)
        for m4 in range(4):
            for nh in range(2):
                def mms(e, m4=m4, nh=nh):
                    last = None
                    for kk in range(8):
                        last = e.matmul(self.pb[0][:, 0:512],
                                        self.wb[0][:, 512 * kk + 128 * m4:512 * kk + 128 * m4 + 128],
                                        self.bk[kk // 2][:, (kk % 2) * 1024 + 512 * nh:(kk % 2) * 1024 + 512 * nh + 512],
                                        start=(kk == 0), stop=(kk == 7))
                    return last
                self.t(mms)
                dst = self.bx[m4 // 2][:, (m4 % 2) * 1024 + 512 * nh:(m4 % 2) * 1024 + 512 * nh + 512]
                self.v(lambda e, dst=dst: e.tensor_scalar(dst, self.pb[0][:, 0:512], 0.125, None, op0=ALU.mult))
        gather_side(8, self.bq)
        u_side(self.bq)
        transpose_side(self.bq)
        P.op16("gpsimd", lambda e: e.dma_start(
            self.wb[1][:, :].rearrange("p (k n) -> p k n", k=4),
            IN["w_kvh"][0:512, :].rearrange("(k p) n -> p k n", p=128)))
        P.op16("gpsimd", lambda e: e.dma_start(
            self.wb[2][:, :].rearrange("p (k n) -> p k n", k=4),
            IN["w_kvh"][512:1024, :].rearrange("(k p) n -> p k n", p=128)))
        def gscale12(e):
            last = None
            for kk in range(8):
                w = self.wb[1 + kk // 4][:, 1024 * (kk % 4):1024 * (kk % 4) + 1024]
                last = e.tensor_scalar(w, w, smal[:, 16 + kk:17 + kk], None, op0=ALU.mult)
            return last
        self.v(gscale12)
        for m4 in range(4):
            for nh in range(2):
                def mmk(e, m4=m4, nh=nh):
                    last = None
                    for kk in range(8):
                        w = self.wb[1 + kk // 4][:, 1024 * (kk % 4) + 128 * m4:1024 * (kk % 4) + 128 * m4 + 128]
                        last = e.matmul(self.pb[0][:, 0:512], w,
                                        self.bk[kk // 2][:, (kk % 2) * 1024 + 512 * nh:(kk % 2) * 1024 + 512 * nh + 512],
                                        start=(kk == 0), stop=(kk == 7))
                    return last
                self.t(mmk)
                dst = self.bq[m4 // 2][:, (m4 % 2) * 1024 + 512 * nh:(m4 % 2) * 1024 + 512 * nh + 512]
                self.v(lambda e, dst=dst: e.tensor_copy(dst, self.pb[0][:, 0:512]))
        for tt in range(8):
            def mmv(e, tt=tt):
                last = None
                for kk in range(8):
                    lhs = self.bk[kk // 2][:, (kk % 2) * 1024 + 128 * tt:(kk % 2) * 1024 + 128 * tt + 128]
                    w = self.wb[1 + kk // 4][:, 1024 * (kk % 4) + 512:1024 * (kk % 4) + 1024]
                    last = e.matmul(self.pb[0][:, 0:512], lhs, w, start=(kk == 0), stop=(kk == 7))
                return last
            self.t(mmv)
            dst = self.bq[2 + tt // 4][:, (tt % 4) * 512:(tt % 4) * 512 + 512]
            self.v(lambda e, dst=dst: e.tensor_copy(dst, self.pb[0][:, 0:512]))
        self.dump("hk", self.bq[0][:, 0:2048], (128, 2048), BF16)
        self.dump("hq", self.bx[0][:, 0:2048], (128, 2048), BF16)
        self._heavy_heads()
        P.op16("gpsimd", lambda e: e.dma_start(
            self.wb[2][:, :].rearrange("p (k n) -> p k n", k=4),
            IN["w_oh"].rearrange("(k p) n -> p k n", p=128)))
        stg = self.repb[:, :].bitcast(BF16)[:, 0:1024]
        for qm in range(8):
            def mmo(e, qm=qm):
                last = None
                for kt in range(4):
                    for nh in range(2):
                        last = e.matmul(self.pb[nh][:, 0:512],
                                        self.wb[1][:, 1024 * kt + 128 * qm:1024 * kt + 128 * qm + 128],
                                        self.wb[2][:, 1024 * kt + 512 * nh:1024 * kt + 512 * nh + 512],
                                        start=(kt == 0), stop=(kt == 3))
                return last
            self.t(mmo)
            def evo(e):
                e.tensor_copy(stg[:, 0:512], self.pb[0][:, 0:512])
                return e.tensor_copy(stg[:, 512:1024], self.pb[1][:, 0:512])
            self.v(evo)
            P.dma("sync", self.DSC["hvt"][128 * qm:128 * qm + 128, :], stg)
        self.dump("hvt_s", stg, (128, 1024), BF16)

    def _heavy_heads(self):
        for h in range(8):
            m4 = h // 2
            po = (h % 2) * 64
            for mt in range(8):
                def mmsim(e, h=h, mt=mt, m4=m4, po=po):
                    last = None
                    for nh in range(2):
                        lhs = self.bq[m4 // 2][po:po + 64, (m4 % 2) * 1024 + 128 * mt:(m4 % 2) * 1024 + 128 * mt + 128]
                        rhs = self.bx[m4 // 2][po:po + 64, (m4 % 2) * 1024 + 512 * nh:(m4 % 2) * 1024 + 512 * nh + 512]
                        last = e.matmul(self.pb[nh][:, 0:512], lhs, rhs, start=True, stop=True)
                    return last
                self.t(mmsim)
                def expev(e, mt=mt):
                    dst = self.bv[mt // 2][:, (mt % 2) * 1024:(mt % 2) * 1024 + 1024]
                    e.activation(dst[:, 0:512], self.pb[0][:, 0:512], AF.Exp)
                    return e.activation(dst[:, 512:1024], self.pb[1][:, 0:512], AF.Exp)
                self.s(expev)
            def mmden(e, h=h):
                last = None
                for mt in range(8):
                    src = self.bv[mt // 2][:, (mt % 2) * 1024:(mt % 2) * 1024 + 1024]
                    last = e.matmul(self.pb[2][0:1, 0:512], self.onesb_col, src[:, 0:512],
                                    start=(mt == 0), stop=(mt == 7))
                    last = e.matmul(self.pb[3][0:1, 0:512], self.onesb_col, src[:, 512:1024],
                                    start=(mt == 0), stop=(mt == 7))
                return last
            self.t(mmden)
            def recip0(e):
                e.tensor_copy(self.repa[0:1, 0:512], self.pb[2][0:1, 0:512])
                return e.tensor_copy(self.repa[0:1, 512:1024], self.pb[3][0:1, 0:512])
            self.v(recip0)
            self.v(lambda e: e.tensor_scalar(self.repa[0:1, 0:1024], self.repa[0:1, 0:1024], 1e-30, None, op0=ALU.max))
            self.v(lambda e: e.reciprocal(self.repa[0:1, 0:1024], self.repa[0:1, 0:1024]))
            for nh in range(2):
                def mmho(e, h=h, nh=nh):
                    for kt in range(8):
                        lhs = self.bq[2 + kt // 4][:, (kt % 4) * 512 + 64 * h:(kt % 4) * 512 + 64 * h + 64]
                        rhs = self.bv[kt // 2][:, (kt % 2) * 1024 + 512 * nh:(kt % 2) * 1024 + 512 * nh + 512]
                        e.matmul(self.pb[0][0:64, 0:512], lhs, rhs, start=(kt == 0), stop=(kt == 7))
                    return e.matmul(self.pb[1][:, 0:512], self.ones_row,
                                    self.repa[0:1, 512 * nh:512 * nh + 512], start=True, stop=True)
                self.t(mmho)
                self.v(lambda e: e.tensor_copy(self.repb[:, 0:512], self.pb[1][:, 0:512]))
                def evho(e, h=h, nh=nh):
                    dst = self.wb[1][(h % 2) * 64:(h % 2) * 64 + 64,
                                     1024 * (h // 2) + 512 * nh:1024 * (h // 2) + 512 * nh + 512]
                    return e.tensor_tensor(dst, self.pb[0][0:64, 0:512], self.repb[0:64, 0:512], op=ALU.mult)
                self.v(evho)

    def stageD_qkv(self):
        P, IN = self.P, self.IN
        st = self.st
        for wi, c0 in ((0, 0), (1, 512), (2, 1024)):
            P.op16("gpsimd", lambda e, wi=wi, c0=c0: e.dma_start(
                self.wb[wi][:, :].rearrange("p (k n) -> p k n", k=8),
                IN["w_qkv"][:, c0:c0 + 512].rearrange("(k p) n -> p k n", p=128)))
        P.dmas("sync", [(self.repa[:, :], IN["lngb"][:, 0:1024]),
                        (self.repb[:, :], IN["lngb"][:, 1024:2048])])
        stg32 = None
        for c in range(10):
            fmb = self.bx[c % 2]
            for half in range(2):
                i = 2 * c + half
                al = st[:, 40 + i:41 + i]
                be = st[:, 60 + i:61 + i]
                def xl(e, i=i, al=al, be=be, half=half, fmb=fmb):
                    last = None
                    for q4 in range(2):
                        stg = (self.bx[0] if q4 == 0 else self.bx[1])[:, 2048:2560]
                        sl = slice(512 * q4, 512 * q4 + 512)
                        e.tensor_scalar(stg, self.r[i][:, sl], al, be, op0=ALU.mult, op1=ALU.add)
                        e.tensor_tensor(stg, stg, self.repa[:, sl], op=ALU.mult)
                        last = e.tensor_tensor(stg, stg, self.repb[:, sl], op=ALU.add)
                    return last
                self.v(xl)
                for grp in range(2):
                    def trs(e, grp=grp):
                        last = None
                        for q in range(4):
                            kk = 4 * grp + q
                            stg = (self.bx[0] if kk < 4 else self.bx[1])[:, 2048 + 128 * (kk % 4):2048 + 128 * (kk % 4) + 128]
                            last = e.transpose(self.pbt[:, 128 * q:128 * q + 128], stg, self.eyeb)
                        return last
                    self.t(trs)
                    self.v(lambda e, grp=grp, half=half, fmb=fmb: e.tensor_copy(
                        fmb[:, 1024 * grp + 128 * half:1024 * grp + 128 * half + 1024]
                        .rearrange("p (k n) -> p k n", n=128)[:, ::2, :],
                        self.pbt[:, 0:512].rearrange("p (k n) -> p k n", n=128)))
            self._qkv_chunk(c, fmb)
        def qs(e):
            last = None
            for m4 in range(4):
                last = e.tensor_scalar(self.bq[m4][:, :], self.bq[m4][:, :], 0.125, None, op0=ALU.mult)
            return last
        self.v(qs)
        self.dump("qfm", self.bq[0][:, :], (128, 2304), BF16)
        self.dump("vsh9", self.bv[1][:, 2048:2560], (128, 512), BF16)
        self.dump("vsh8", self.bv[1][:, 1536:2048], (128, 512), BF16)
        self.dump("kfm", self.bk[0][:, :], (128, 2432), BF16)
        self.dump("vsh", self.bv[0][:, :], (128, 2560), BF16)

    def _qkv_chunk(self, c, fmb):
        for dst_bufs, wbi, off, width in ((self.bq, 0, 128, 2304), (self.bk, 1, 64, 2432)):
            xlo = 256 * c
            lo = max(xlo, off)
            hi = min(xlo + 256, off + width)
            if lo >= hi:
                continue
            for m4 in range(4):
                def mmq(e, m4=m4, fmb=fmb, wbi=wbi, lo=lo, hi=hi, xlo=xlo):
                    last = None
                    for kk in range(8):
                        last = e.matmul(self.pb[0][:, 0:256],
                                        self.wb[wbi][:, 512 * kk + 128 * m4:512 * kk + 128 * m4 + 128],
                                        fmb[:, 256 * kk:256 * kk + 256], start=(kk == 0), stop=(kk == 7))
                    return last
                self.t(mmq)
                self.v(lambda e, m4=m4, dst_bufs=dst_bufs, lo=lo, hi=hi, xlo=xlo, off=off:
                       e.tensor_copy(dst_bufs[m4][:, lo - off:hi - off],
                                     self.pb[0][:, lo - xlo:hi - xlo]))
        for u in (2 * c - 1, 2 * c):
            if u < 0 or u >= 19:
                continue
            prev = self.bx[(c - 1) % 2]
            cur = fmb
            def mmv(e, u=u, c=c, prev=prev, cur=cur):
                last = None
                for kk in range(8):
                    w = self.wb[2][:, 512 * kk:512 * kk + 512]
                    if u == 2 * c:
                        lhs = cur[:, 256 * kk + 64:256 * kk + 64 + 128]
                        last = e.matmul(self.pb[1][:, 0:512], lhs, w, start=(kk == 0), stop=(kk == 7))
                    else:
                        lhs0 = prev[:, 256 * kk + 192:256 * kk + 256]
                        lhs1 = cur[:, 256 * kk:256 * kk + 64]
                        e.matmul(self.pb[1][0:64, 0:512], lhs0, w, start=(kk == 0), stop=False)
                        last = e.matmul(self.pb[2][0:64, 0:512], lhs1, w, start=(kk == 0), stop=(kk == 7))
                return last
            self.t(mmv)
            if u % 2 == 0:
                self.v(lambda e, u=u: e.tensor_copy(
                    self.bv[u // 5][:, 512 * (u % 5):512 * (u % 5) + 512], self.pb[1][:, 0:512]))
            else:
                def evodd(e, u=u):
                    e.tensor_copy(self.bv[u // 5][0:64, 512 * (u % 5):512 * (u % 5) + 512],
                                  self.pb[1][0:64, 0:512])
                    return e.tensor_copy(self.bx[0][0:64, 2048:2560], self.pb[2][0:64, 0:512])
                self.v(evodd)
                self.P.dma("sync", self.bv[u // 5][64:128, 512 * (u % 5):512 * (u % 5) + 512],
                           self.bx[0][0:64, 2048:2560])

    def stageE_light(self):
        import os
        P, IN = self.P, self.IN
        ntiles = int(os.environ.get("KE_TILES", str(AT)))
        parts = os.environ.get("KE_PARTS", "mask,tok,pt,heads,light,scat,evac").split(",")
        self.ke_parts = parts
        P.op16("gpsimd", lambda e: e.dma_start(
            self.wb[0][:, :].rearrange("p (k n) -> p k n", k=4),
            IN["w_ol"].rearrange("(k p) n -> p k n", p=128)))
        for half in range(2):
            P.op16("gpsimd", lambda e, half=half: e.dma_start(
                self.wb[1][:, :].rearrange("p (k n) -> p k n", k=4),
                self.DSC["hvt"][512 * half:512 * half + 512, :].rearrange("(k p) n -> p k n", p=128)))
            for t in range(ntiles):
                self._light_tile(t, half)
        if "y1" in self.debug:
            d = self.nc.dram_tensor("dbg_y1", [ACOLS, D], F32, kind="ExternalOutput").ap()
            self.dbg_aps["y1"] = d
            self.P.dmas("scalar", [(d[128 * t:128 * t + 128, :], self.r[t + 1][:, :]) for t in range(AT)])

    def _light_tile(self, t, half):
        P, IN = self.P, self.IN
        parts = self.ke_parts
        mvb = self.bx[0][0:4, 0:256]
        if "mask" in parts:
            P.op16("gpsimd", lambda e, t=t: e.dma_start(mvb, IN["maskV"][t:t + 1, :, :].rearrange("a m n -> (a m) n")))
        if "tok" in parts:
            P.dma("sync", self.repb[:, 0:128], IN["tokrow"][:, 128 * t:128 * t + 128])
        if "pt" not in parts:
            return
        def ptiles(e, t=t, half=half):
            last = None
            for q in range(4):
                qt = 4 * half + q
                last = e.tensor_scalar(self.bx[1][:, 512 + 128 * q:512 + 128 * q + 128],
                                       self.repb[:, 0:128], self.st2[:, 192 + qt:193 + qt],
                                       None, op0=ALU.is_equal)
            return last
        self.v(ptiles)
        if "heads" not in parts:
            return
        if half == 0:
            for h in range(8):
                if "light" in parts:
                    self._light_head(t, h)
            if t == 8:
                self.dump("lit8", self.bx[1][:, 0:512], (128, 512), BF16)
            def mmlo(e, t=t):
                last = None
                for kk4 in range(4):
                    lhs = self.bx[1][(0):(128), 128 * kk4:128 * kk4 + 128]
                    for nh in range(2):
                        last = e.matmul(self.pb[4 + nh][:, 0:512], lhs,
                                        self.wb[0][:, 1024 * kk4 + 512 * nh:1024 * kk4 + 512 * nh + 512],
                                        start=(kk4 == 0), stop=False)
                return last
            self.t(mmlo)
        if "scat" not in parts:
            return
        def mmsc(e, t=t, half=half):
            last = None
            for q in range(4):
                lhs = self.bx[1][:, 512 + 128 * q:512 + 128 * q + 128]
                for nh in range(2):
                    last = e.matmul(self.pb[4 + nh][:, 0:512], lhs,
                                    self.wb[1][:, 1024 * q + 512 * nh:1024 * q + 512 * nh + 512],
                                    start=(half == 1 and q == 0), stop=(q == 3))
            return last
        self.t(mmsc)
        def evac(e, t=t):
            rt = self.r[t + 1]
            e.tensor_tensor(rt[:, 0:512], self.pb[4][:, 0:512], rt[:, 0:512], op=ALU.add)
            return e.tensor_tensor(rt[:, 512:1024], self.pb[5][:, 0:512], rt[:, 512:1024], op=ALU.add)
        self.v(evac)

    def _light_head(self, t, h):
        m4 = h // 2
        po = (h % 2) * 64
        def mmsim(e, t=t, h=h, m4=m4, po=po):
            qw = self.bq[m4][po:po + 64, 128 * t:128 * t + 128]
            last = None
            for kc in range(2):
                kw = self.bk[m4][po:po + 64, 128 * t + 128 * kc:128 * t + 128 * kc + 128]
                e.matmul(self.pb[2][:, 128 * kc:128 * kc + 128], kw, qw,
                         start=(kc == 0), stop=False)
                last = e.matmul(self.pb[2][:, 128 * kc:128 * kc + 128],
                                self.bx[0][0:4, 128 * kc:128 * kc + 128],
                                self.cstb[0:4, 0:128], start=False, stop=(kc == 1))
            return last
        self.t(mmsim)
        ex = self.bx[0][:, 512:768]
        self.s(lambda e: e.activation(ex, self.pb[2][:, 0:256], AF.Exp))
        if t == 8 and h == 0:
            self.dump("ext8", ex, (128, 256), BF16)
        def mmden(e):
            e.matmul(self.pb[3][0:1, 0:128], self.onesb_col, ex[:, 0:128], start=True, stop=False)
            return e.matmul(self.pb[3][0:1, 0:128], self.onesb_col, ex[:, 128:256], start=False, stop=True)
        self.t(mmden)
        self.v(lambda e: e.tensor_scalar(self.smal[0:1, 32:160], self.pb[3][0:1, 0:128], 1e-30, None, op0=ALU.max))
        self.v(lambda e: e.reciprocal(self.smal[0:1, 32:160], self.smal[0:1, 32:160]))
        self.t(lambda e: e.matmul(self.pb[2][:, 0:128], self.ones_row, self.smal[0:1, 32:160], start=True, stop=True))
        def mmav(e, t=t, h=h):
            last = None
            for kc in range(2):
                u = t + kc
                lhs = self.bv[u // 5][:, 512 * (u % 5) + 64 * h:512 * (u % 5) + 64 * h + 64]
                last = e.matmul(self.pb[3][0:64, 0:128], lhs, ex[:, 128 * kc:128 * kc + 128],
                                start=(kc == 0), stop=(kc == 1))
            return last
        self.t(mmav)
        if t == 8 and h == 0:
            self.v(lambda e: e.tensor_copy(self.repb[0:1, 384:512], self.smal[0:1, 32:160]))
            self.dump("rec8", self.repb[0:1, 384:512], (1, 128))
            self.v(lambda e: e.tensor_copy(self.repb[:, 384:512], self.pb[3][:, 0:128]))
            self.dump("av8", self.repb[:, 384:512], (128, 128))
        self.v(lambda e: e.tensor_copy(self.repb[:, 128:256], self.pb[2][:, 0:128]))
        def ev(e, h=h, po=po, m4=m4):
            dst = self.bx[1][po:po + 64, 128 * m4:128 * m4 + 128]
            return e.tensor_tensor(dst, self.pb[3][0:64, 0:128], self.repb[0:64, 128:256], op=ALU.mult)
        self.v(ev)

    def stageF_l0ffn(self):
        P, IN = self.P, self.IN
        smal = self.smal
        P.dmas("scalar", [(self.DSC["y1"][128 * t:128 * t + 128, :], self.r[t + 1][:, :]) for t in range(AT)])
        self.quant_weights(IN["w1T"], 8, smal[:, 43:44], smal[:, 44:45])
        self.dump("w1q0", self.r[0][:, :].bitcast(BF16), (128, 2048), BF16)
        self.dump("sw1", smal[:, 43:45], (128, 2), F32)
        P.dmas("sync", [(smal[:, 64:160], IN["cw0r"]), (smal[:, 160:192], IN["cb0r"])])
        def diags(e):
            last = None
            for f in range(32):
                for tap in range(3):
                    last = e.tensor_scalar(self._diag(f, tap), self.eyeb,
                                           smal[:, 64 + 3 * f + tap:65 + 3 * f + tap], None, op0=ALU.mult)
            return last
        self.v(diags)
        def zinit(e):
            for wbb in (self.wb[0], self.wb[1]):
                e.memset(wbb[:, 0:2080].rearrange("p (f n) -> p f n", n=130)[:, :, 0:2], 0.0)
            return e.memset(smal[0:1, 49:50], 0.0)
        self.v(zinit)
        P.dma("sync", self.DSC["invs"][2304:2305, :], smal[0:1, 49:50])
        for c in range(AT):
            self._ffn_chunk(c)
        self.quant_weights(IN["w2T"], 32, smal[:, 45:46], smal[:, 46:47])
        for t in range(AT):
            self._bit2_tile(t)
        self.dump("y1full", self.DSC["y1"], (ACOLS, D))
        self.dump("y2full", self.DSC["y2"], (ACOLS, D))

    def _diag(self, f, tap):
        j = 3 * f + tap
        if j < 68:
            return self.bq[j // 17][:, 128 + 128 * (j % 17):256 + 128 * (j % 17)]
        j -= 68
        return self.bk[j // 18][:, 128 + 128 * (j % 18):256 + 128 * (j % 18)]

    def quant_weights(self, wap, ktiles, sw_col, invsw_col):
        P, smal = self.P, self.smal
        N = 4096 if ktiles == 8 else 1024
        nch = N // 1024
        QE = ["sync", "scalar", "gpsimd", "sync"]
        stg = [self.bq[i][:, 0:2048].bitcast(F32)[:, 0:1024] for i in range(4)]
        stg2 = [self.bk[i][:, 0:2048].bitcast(F32)[:, 0:1024] for i in range(4)]
        def chsrc(c):
            kt, ch = c // nch, c % nch
            return wap[128 * kt:128 * kt + 128, 1024 * ch:1024 * ch + 1024]
        def chdst(c):
            kt, ch = c // nch, c % nch
            if ktiles == 8:
                return self.r[2 * kt + ch // 2][:, :].bitcast(BF16)[:, 1024 * (ch % 2):1024 * (ch % 2) + 1024]
            return self.r[kt // 2][:, :].bitcast(BF16)[:, 1024 * (kt % 2):1024 * (kt % 2) + 1024]
        ngr = (ktiles * nch) // 4
        self.v(lambda e: e.memset(smal[:, 40:41], 0.0))
        for g in range(ngr):
            P.pdma([(QE[j], stg[j], chsrc(4 * g + j)) for j in range(4)])
            cols = (41, 49, 50, 51)
            def reds(e):
                last = None
                for j in range(4):
                    last = e.reduce_sum(smal[:, cols[j]:cols[j] + 1], stg[j], axis=AXX,
                                        apply_absolute_value=True)
                return last
            self.v(reds)
            self.v(lambda e: (e.tensor_tensor(smal[:, 52:53], smal[:, 41:42], smal[:, 49:50], op=ALU.add),
                              e.tensor_tensor(smal[:, 53:54], smal[:, 50:51], smal[:, 51:52], op=ALU.add))[-1])
            self.v(lambda e: e.tensor_tensor(smal[:, 41:42], smal[:, 52:53], smal[:, 53:54], op=ALU.add))
            self.v(lambda e: e.tensor_tensor(smal[:, 40:41], smal[:, 40:41], smal[:, 41:42], op=ALU.add))
        self.ptot(smal[:, 40:41], smal[:, 42:43], 1)
        self.v(lambda e, kt=ktiles: e.tensor_scalar(invsw_col, smal[:, 42:43], 1.0 / (128.0 * kt * N), 1e-5,
                                                    op0=ALU.mult, op1=ALU.max))
        self.v(lambda e: e.reciprocal(sw_col, invsw_col))
        for g in range(ngr):
            P.pdma([(QE[j], stg[j], chsrc(4 * g + j)) for j in range(4)])
            def q12(e, g=g):
                last = None
                for j in range(4):
                    dst = chdst(4 * g + j)
                    e.tensor_scalar(stg2[j], stg[j], sw_col, MAGIC, op0=ALU.mult, op1=ALU.add)
                    e.tensor_scalar(stg2[j], stg2[j], MAGIC, 1.0, op0=ALU.subtract, op1=ALU.min)
                    last = e.tensor_scalar(dst, stg2[j], -1.0, None, op0=ALU.max)
                return last
            self.v(q12)

    def _ffn_chunk(self, c):
        P, IN = self.P, self.IN
        smal = self.smal
        yt, lnf = self.repa, self.repb
        m0, r0, v0 = smal[:, 49:50], smal[:, 50:51], smal[:, 51:52]
        mx, scx, invx = smal[:, 52:53], smal[:, 53:54], smal[:, 54:55]
        P.dma("sync", yt[:, :], self.DSC["y1"][128 * c:128 * c + 128, :])
        self.v(lambda e: e.reduce_sum(m0, yt[:, :], axis=AXX))
        self.v(lambda e: e.tensor_scalar(m0, m0, 1.0 / D, None, op0=ALU.mult))
        self.v(lambda e: e.tensor_scalar(lnf[:, :], yt[:, :], m0, None, op0=ALU.subtract))
        self.s(lambda e: e.activation(yt[:, :], lnf[:, :], AF.Square, accum_out=v0))
        self.v(lambda e: e.tensor_scalar(v0, v0, 1.0 / D, None, op0=ALU.mult))
        self.rsqrt(r0, v0)
        self.v(lambda e: e.tensor_scalar(lnf[:, :], lnf[:, :], r0, None, op0=ALU.mult))
        self.v(lambda e: e.reduce_max(mx, lnf[:, :], axis=AXX, apply_absolute_value=True))
        self.v(lambda e: e.tensor_scalar(mx, mx, 1e-5, None, op0=ALU.max))
        self.v(lambda e: e.tensor_scalar(invx, mx, 1.0 / 127.0, None, op0=ALU.mult))
        self.v(lambda e: e.reciprocal(scx, invx))
        xqs = self.bx[1][:, 0:2048].bitcast(F32)
        xqb = self.bx[0][:, 0:1024]
        self.v(lambda e: e.tensor_scalar(xqs[:, :], lnf[:, :], scx, MAGIC, op0=ALU.mult, op1=ALU.add))
        self.v(lambda e: e.tensor_scalar(xqs[:, :], xqs[:, :], MAGIC, None, op0=ALU.subtract))
        self.v(lambda e: e.tensor_scalar(xqb[:, :], xqs[:, :], invx, None, op0=ALU.mult))
        if c == 0 or c == AT - 1:
            hm = IN["hmask"]
            P.dma("sync", smal[:, 56:58], hm)
            col = smal[:, 56:57] if c == 0 else smal[:, 57:58]
            self.v(lambda e, col=col: e.tensor_scalar(xqb[:, :], xqb[:, :], col, None, op0=ALU.mult))
        for grp in range(2):
            def trs(e, grp=grp):
                last = None
                for q in range(4):
                    kk = 4 * grp + q
                    last = e.transpose(self.pbt[:, 128 * q:128 * q + 128],
                                      xqb[:, 128 * kk:128 * kk + 128], self.eyeb)
                return last
            self.t(trs)
            def cpy(e, grp=grp):
                last = None
                for q in range(4):
                    kk = 4 * grp + q
                    dstb = self.bq[kk] if kk < 4 else self.bk[kk - 4]
                    last = e.tensor_copy(dstb[:, 0:128], self.pbt[:, 128 * q:128 * q + 128])
                return last
            self.v(cpy)
        for half in range(2):
            hraw = self.wb[half]
            def mm1(e, half=half):
                last = None
                for g in range(4):
                    for u in range(4):
                        f = 16 * half + 4 * g + u
                        for kk in range(8):
                            lhs = self.r[2 * kk + f // 16][:, :].bitcast(BF16)[:, 128 * (f % 16):128 * (f % 16) + 128]
                            rhs = (self.bq[kk] if kk < 4 else self.bk[kk - 4])[:, 0:128]
                            last = e.matmul(self.pb[g][:, 128 * u:128 * u + 128], lhs, rhs,
                                            start=(kk == 0), stop=(kk == 7))
                return last
            self.t(mm1)
            def ev1(e, half=half, hraw=hraw):
                last = None
                for g in range(4):
                    last = e.tensor_copy(
                        hraw[:, 0:2080].rearrange("p (f n) -> p f n", n=130)[:, 4 * g:4 * g + 4, 2:130],
                        self.pb[g][:, :].rearrange("p (f n) -> p f n", n=128))
                return last
            self.v(ev1)
        for half in range(2):
            hraw = self.wb[half]
            def mmc(e, half=half, hraw=hraw):
                last = None
                for g in range(4):
                    for u in range(4):
                        fl = 4 * g + u
                        for tap in range(3):
                            last = e.matmul(self.pb[g][:, 128 * u:128 * u + 128],
                                            self._diag(16 * half + fl, tap),
                                            hraw[:, 130 * fl + tap:130 * fl + tap + 128],
                                            start=(tap == 0), stop=(tap == 2))
                return last
            self.t(mmc)
            def sil(e, half=half):
                last = None
                for g in range(4):
                    for u in range(4):
                        f = 16 * half + 4 * g + u
                        dst = self.bv[f // 8][:, :].bitcast(F32)[:, 128 * (f % 8):128 * (f % 8) + 128]
                        last = e.activation(dst, self.pb[g][:, 128 * u:128 * u + 128], AF.Silu,
                                            bias=smal[:, 160 + f:161 + f], scale=smal[:, 44:45])
                return last
            self.s(sil)
        def cset(e):
            e.tensor_copy(self.wb[0][:, 0:2080].rearrange("p (f n) -> p f n", n=130)[:, :, 0:2],
                          self.wb[0][:, 0:2080].rearrange("p (f n) -> p f n", n=130)[:, :, 128:130])
            return e.tensor_copy(self.wb[1][:, 0:2080].rearrange("p (f n) -> p f n", n=130)[:, :, 0:2],
                                 self.wb[1][:, 0:2080].rearrange("p (f n) -> p f n", n=130)[:, :, 128:130])
        self.v(cset)
        w2f = self.wb[2][:, :].bitcast(F32)
        def ab01(e):
            e.activation(w2f[:, 0:1024], self.bv[0][:, :].bitcast(F32)[:, 0:1024], AF.Abs)
            return e.activation(w2f[:, 1024:2048], self.bv[1][:, :].bitcast(F32)[:, 0:1024], AF.Abs)
        self.s(ab01)
        for i in (None, 2, 3):
            if i is not None:
                self.s(lambda e, i=i: e.activation(w2f[:, 1024:2048],
                                                   self.bv[i][:, :].bitcast(F32)[:, 0:1024], AF.Abs))
            self.v(lambda e: e.tensor_tensor(w2f[:, 0:1024], w2f[:, 0:1024], w2f[:, 1024:2048], op=ALU.max))
        self.v(lambda e: e.tensor_tensor(w2f[:, 0:512], w2f[:, 0:512], w2f[:, 512:1024], op=ALU.max))
        self.v(lambda e: e.tensor_tensor(w2f[:, 0:256], w2f[:, 0:256], w2f[:, 256:512], op=ALU.max))
        self.v(lambda e: e.tensor_tensor(w2f[:, 0:128], w2f[:, 0:128], w2f[:, 128:256], op=ALU.max))
        eyef = self.cst2[:, 384:512]
        self.t(lambda e: e.transpose(self.pb[4][:, 0:128], w2f[:, 0:128], eyef))
        hmx, invt, sht = smal[:, 52:53], smal[:, 53:54], smal[:, 54:55]
        self.v(lambda e: e.reduce_max(hmx, self.pb[4][:, 0:128], axis=AXX))
        self.v(lambda e: e.tensor_scalar(hmx, hmx, 1e-5, None, op0=ALU.max))
        self.v(lambda e: e.tensor_scalar(invt, hmx, 1.0 / 127.0, None, op0=ALU.mult))
        self.v(lambda e: e.reciprocal(sht, invt))
        P.dma("sync", self.DSC["invs"][128 * c:128 * c + 128, :], invt)
        self.t(lambda e: e.transpose(self.pb[4][0:1, 128:256], sht, eyef))
        self.v(lambda e: e.tensor_copy(self.st2[0:1, 0:128], self.pb[4][0:1, 128:256]))
        self.t(lambda e: e.matmul(self.pb[4][:, 256:384], self.ones_row, self.st2[0:1, 0:128], start=True, stop=True))
        hqi = self.wb[2][:, :].bitcast(I8)
        def qtt(e):
            last = None
            for f in range(32):
                hsf = self.bv[f // 8][:, :].bitcast(F32)[:, 128 * (f % 8):128 * (f % 8) + 128]
                last = e.tensor_tensor(hqi[:, 128 * f:128 * f + 128], hsf, self.pb[4][:, 256:384], op=ALU.mult)
            return last
        self.v(qtt)
        P.dma("sync", self.DSC["hq"][c:c + 1, :, :, :].rearrange("a p f n -> (a p) (f n)"), hqi[:, 0:4096])
        if c == 8:
            self.dump("xq8", xqb, (128, 1024), BF16)
            self.dump("hraw8a", self.wb[0][:, 0:2080], (128, 2080), BF16)
            self.dump("hs8", self.bv[0][:, :].bitcast(F32)[:, 0:1024], (128, 1024))
            self.v(lambda e: e.tensor_copy(self.st2[:, 300:304] if False else self.repa[:, 0:128], self.pb[4][:, 256:384]))
            self.dump("shb8", self.repa[:, 0:128], (128, 128))

    def _bit2_tile(self, t):
        P, smal = self.P, self.smal
        prs = []
        for half in range(2):
            for sl in range(2):
                cc = t + sl
                dst = self.wb[half][:, :].rearrange("p (f n) -> p f n", n=256)[:, :, 128 * sl:128 * sl + 128]
                if cc < AT:
                    src = self.DSC["hq"][cc:cc + 1, :, 16 * half:16 * half + 16, :].rearrange("a p f n -> (a p) f n")
                    prs.append((dst, src))
                else:
                    self.v(lambda e, dst=dst: e.memset(dst, 0.0))
        P.dmas("gpsimd", prs, slow=True)
        def mm2(e, t=t):
            last = None
            for ft in range(32):
                lhs = self.wb[ft // 16][:, 256 * (ft % 16) + 1:256 * (ft % 16) + 129]
                for nh in range(2):
                    rhs = self.r[ft // 2][:, :].bitcast(BF16)[:, 1024 * (ft % 2) + 512 * nh:1024 * (ft % 2) + 512 * nh + 512]
                    last = e.matmul(self.pb[4 + nh][:, 0:512], lhs, rhs, start=(ft == 0), stop=(ft == 31))
            return last
        self.t(mm2)
        P.dma("sync", smal[:, 47:48], self.DSC["invs"][128 * t + 1:128 * t + 129, :])
        self.v(lambda e: e.tensor_tensor(smal[:, 48:49], smal[:, 47:48], smal[:, 46:47], op=ALU.mult))
        P.dma("sync", self.repa[:, :], self.DSC["y1"][128 * t:128 * t + 128, :])
        def ev2(e):
            e.tensor_scalar(self.repb[:, 0:512], self.pb[4][:, 0:512], smal[:, 48:49], None, op0=ALU.mult)
            return e.tensor_scalar(self.repb[:, 512:1024], self.pb[5][:, 0:512], smal[:, 48:49], None, op0=ALU.mult)
        self.v(ev2)
        self.v(lambda e: e.tensor_tensor(self.repb[:, :], self.repb[:, :], self.repa[:, :], op=ALU.add))
        P.dma("sync", self.DSC["y2"][128 * t:128 * t + 128, :], self.repb[:, :])

    def stageG_mamba(self):
        P, IN, smal, st2 = self.P, self.IN, self.smal, self.st2
        eyef = self.cst2[:, 384:512]
        P.dmas("sync", [(smal[:, 64:132], IN["cw1r"]), (smal[:, 132:149], IN["cb1r"]),
                        (smal[0:32, 58:59], IN["dt_b"]), (smal[0:32, 59:60], IN["a_log"]),
                        (smal[0:32, 60:61], IN["d_par"]), (smal[:, 61:62], IN["xzmask"]),
                        (smal[:, 213:221], IN["pm8"])])
        def ginit(e):
            e.memset(smal[:, 253:254], -1.0)
            e.memset(smal[:, 252:253], 1.0)
            e.memset(self.wb[0][:, 0:2227].rearrange("p (f n) -> p f n", n=131)[:, :, 0:3], 0.0)
            e.memset(self.r[19][:, :], 0.0)
            e.memset(self.bk[3][:, :].bitcast(F32)[0:64, 64:1088], 0.0)
            e.memset(st2[:, 256:288], 1.0)
            return e.memset(st2[:, 288:320], 0.0)
        self.v(ginit)
        self.t(lambda e: e.transpose(self.pb[4][0:1, 0:32], smal[0:32, 59:60], eyef[0:32, 0:32]))
        self.t(lambda e: e.transpose(self.pb[4][0:1, 32:64], smal[0:32, 60:61], eyef[0:32, 0:32]))
        self.v(lambda e: e.tensor_copy(smal[0:1, 149:213], self.pb[4][0:1, 0:64]))
        self.t(lambda e: e.matmul(self.pb[4][:, 64:96], self.ones_row, smal[0:1, 149:181], start=True, stop=True))
        self.t(lambda e: e.matmul(self.pb[4][:, 96:128], self.ones_row, smal[0:1, 181:213], start=True, stop=True))
        self.s(lambda e: e.activation(st2[:, 160:192], self.pb[4][:, 64:96], AF.Exp))
        self.v(lambda e: (e.tensor_scalar(st2[:, 160:192], st2[:, 160:192], -1.0, None, op0=ALU.mult),
                          e.tensor_copy(st2[:, 192:224], self.pb[4][:, 96:128]))[-1])
        def gdiags(e):
            last = None
            for f in range(17):
                for tap in range(4):
                    last = e.tensor_scalar(self._gdiag(f, tap), self.eyeb,
                                           smal[:, 64 + 4 * f + tap:65 + 4 * f + tap], None, op0=ALU.mult)
            return last
        self.v(gdiags)
        P.dmas("gpsimd", [(self.r[kk][:, :].bitcast(BF16), IN["w_in_z"][128 * kk:128 * kk + 128, :])
                          for kk in range(8)])
        P.dmas("gpsimd", [(self.r[8 + ss][:, :].bitcast(BF16), IN["w_in_fm"][:, 2048 * ss:2048 * ss + 2048])
                          for ss in range(9)])
        for i in range(XZT):
            self._g_prologue(i)
            if i >= 1:
                self._g_ssd(i - 1)
        self._g_prefix()
        P.dmas("gpsimd", [(self.r[kt // 2][:, :].bitcast(BF16)[:, 1024 * (kt % 2):1024 * (kt % 2) + 1024],
                           IN["w_out"][128 * kt:128 * kt + 128, :]) for kt in range(16)])
        P.op16("gpsimd", lambda e: e.dma_start(self.wb[2][:, 0:2048], IN["nrm_w"]))
        def hcb(e):
            last = None
            for h in range(32):
                last = e.tensor_copy(self.wb[1][0:64, 64 * h:64 * h + 64], self._sst(h))
            return last
        self.v(hcb)
        for ch in range(16):
            self._g_pass2(ch)
        self.dump("y3full", self.DSC["y3"], (S, D))

    def _gdiag(self, f, tap):
        j = 4 * f + tap
        return self.bq[j // 17][:, 128 + 128 * (j % 17):256 + 128 * (j % 17)]

    def _winfm(self, b):
        return self.r[8 + b // 16][:, :].bitcast(BF16)[:, 128 * (b % 16):128 * (b % 16) + 128]

    def _ln2F(self, kk):
        return (self.bq[kk] if kk < 4 else self.bk[kk - 4])[:, 0:128]

    def _sst(self, h):
        if h < 16:
            return self.r[19][0:64, 64 * h:64 * h + 64]
        return self.bk[3][:, :].bitcast(F32)[0:64, 64 + 64 * (h - 16):128 + 64 * (h - 16)]

    def _stg(self, h):
        return self.bv[h // 16][:, :].bitcast(F32)[0:64, 64 * (h % 16):64 * (h % 16) + 64]

    def _g_prologue(self, i):
        P, IN, smal, st2 = self.P, self.IN, self.smal, self.st2
        yt, lnf = self.repa, self.repb
        m0, r0, v0 = smal[:, 49:50], smal[:, 50:51], smal[:, 51:52]
        P.dma("sync", yt[:, :], self.DSC["y2"][128 * i:128 * i + 128, :])
        self.v(lambda e: e.reduce_sum(m0, yt[:, :], axis=AXX))
        self.v(lambda e: e.tensor_scalar(m0, m0, 1.0 / D, None, op0=ALU.mult))
        self.v(lambda e: e.tensor_scalar(lnf[:, :], yt[:, :], m0, None, op0=ALU.subtract))
        self.s(lambda e: e.activation(yt[:, :], lnf[:, :], AF.Square, accum_out=v0))
        self.v(lambda e: e.tensor_scalar(v0, v0, 1.0 / D, None, op0=ALU.mult))
        self.rsqrt(r0, v0)
        lnb = self.bx[0][:, 0:1024]
        self.v(lambda e: e.tensor_scalar(lnf[:, :], lnf[:, :], r0, None, op0=ALU.mult))
        if i == 0:
            self.v(lambda e: e.tensor_scalar(lnb[:, :], lnf[:, :], smal[:, 61:62], None, op0=ALU.mult))
        else:
            self.v(lambda e: e.tensor_copy(lnb[:, :], lnf[:, :]))
        for grp in range(2):
            def trs(e, grp=grp):
                last = None
                for q in range(4):
                    kk = 4 * grp + q
                    last = e.transpose(self.pbt[:, 128 * q:128 * q + 128],
                                      lnb[:, 128 * kk:128 * kk + 128], self.eyeb)
                return last
            self.t(trs)
            def cpy(e, grp=grp):
                last = None
                for q in range(4):
                    kk = 4 * grp + q
                    last = e.tensor_copy(self._ln2F(kk), self.pbt[:, 128 * q:128 * q + 128])
                return last
            self.v(cpy)
        if i >= 1:
            def mmz(e):
                last = None
                for b in range(4):
                    for kk in range(8):
                        last = e.matmul(self.pb[b][:, 0:512], self._ln2F(kk),
                                        self.r[kk][:, :].bitcast(BF16)[:, 512 * b:512 * b + 512],
                                        start=(kk == 0), stop=(kk == 7))
                return last
            self.t(mmz)
            def evz(e):
                last = None
                for b in range(4):
                    last = e.tensor_copy(self.bx[1][:, 512 * b:512 * b + 512], self.pb[b][:, 0:512])
                return last
            self.v(evz)
        def mmxa(e):
            last = None
            for g in range(4):
                for u in range(4):
                    f = 4 * g + u
                    for kk in range(8):
                        last = e.matmul(self.pb[g][:, 128 * u:128 * u + 128],
                                        self._winfm(17 * kk + f), self._ln2F(kk),
                                        start=(kk == 0), stop=(kk == 7))
            return last
        self.t(mmxa)
        hbc3 = self.wb[0][:, 0:2227].rearrange("p (f n) -> p f n", n=131)
        def evxa(e):
            last = None
            for g in range(4):
                last = e.tensor_copy(hbc3[:, 4 * g:4 * g + 4, 3:131],
                                     self.pb[g][:, :].rearrange("p (f n) -> p f n", n=128))
            return last
        self.v(evxa)
        def mmxb(e):
            last = None
            for kk in range(8):
                last = e.matmul(self.pb[4][:, 0:128], self._winfm(17 * kk + 16), self._ln2F(kk),
                                start=(kk == 0), stop=(kk == 7))
            for kk in range(8):
                last = e.matmul(self.pb[4][:, 128:256], self._winfm(136 + kk), self._ln2F(kk),
                                start=(kk == 0), stop=(kk == 7))
            return last
        self.t(mmxb)
        def evxb(e, i=i):
            e.tensor_copy(self.wb[0][:, 131 * 16 + 3:131 * 16 + 131], self.pb[4][:, 0:128])
            return e.tensor_copy(st2[0:32, 384:512], self.pb[4][0:32, 128:256])
        self.v(evxb)
        def mmca(e):
            last = None
            for g in range(4):
                for u in range(4):
                    f = 4 * g + u
                    for tap in range(4):
                        last = e.matmul(self.pb[g][:, 128 * u:128 * u + 128], self._gdiag(f, tap),
                                        self.wb[0][:, 131 * f + tap:131 * f + tap + 128],
                                        start=(tap == 0), stop=(tap == 3))
            return last
        self.t(mmca)
        def sila(e):
            last = None
            for g in range(4):
                for u in range(4):
                    f = 4 * g + u
                    last = e.activation(self.bk[1][:, 128 * f:128 * f + 128],
                                        self.pb[g][:, 128 * u:128 * u + 128], AF.Silu,
                                        bias=smal[:, 132 + f:133 + f])
            return last
        self.s(sila)
        def mmcb(e):
            last = None
            for tap in range(4):
                last = e.matmul(self.pb[4][:, 0:128], self._gdiag(16, tap),
                                self.wb[0][:, 131 * 16 + tap:131 * 16 + tap + 128],
                                start=(tap == 0), stop=(tap == 3))
            return last
        self.t(mmcb)
        self.s(lambda e: e.activation(self.bk[1][:, 2048:2176], self.pb[4][:, 0:128], AF.Silu,
                                      bias=smal[:, 148:149]))
        self.v(lambda e: e.tensor_copy(hbc3[:, :, 0:3], hbc3[:, :, 128:131]))
        if i >= 1:
            dtF = st2[0:32, 384:512]
            scr = self.bx[0][:, :].bitcast(F32)
            self.v(lambda e: e.tensor_scalar(dtF, dtF, smal[0:32, 58:59], None, op0=ALU.add))
            self.s(lambda e: e.activation(scr[0:32, 0:128], dtF, AF.Abs))
            self.s(lambda e: e.activation(scr[0:32, 128:256], scr[0:32, 0:128], AF.Exp,
                                          scale=smal[0:32, 253:254]))
            self.s(lambda e: e.activation(scr[0:32, 256:384], scr[0:32, 128:256], AF.Ln,
                                          bias=smal[0:32, 252:253]))
            self.v(lambda e: e.tensor_scalar(scr[0:32, 384:512], dtF, 0.0, None, op0=ALU.max))
            self.v(lambda e: e.tensor_tensor(scr[0:32, 0:128], scr[0:32, 256:384], scr[0:32, 384:512], op=ALU.add))
            self.t(lambda e: e.transpose(self.pb[4][:, 256:288], scr[0:32, 0:128],
                                         self.cst2[0:32, 384:416]))
            self.v(lambda e: e.tensor_copy(self.st[:, 0:32], self.pb[4][:, 256:288]))

    def _g_ssd(self, ch):
        P, smal, st2 = self.P, self.smal, self.st2
        eyef = self.cst2[:, 384:512]
        tri01 = self.cstb[:, 514:642]
        ones128 = self.cstb[:, 642:770]
        for grp in range(4):
            def trs(e, grp=grp):
                last = None
                for q in range(4):
                    f = 4 * grp + q
                    last = e.transpose(self.pbt[:, 128 * q:128 * q + 128],
                                      self.bk[1][:, 128 * f:128 * f + 128], self.eyeb)
                return last
            self.t(trs)
            def cpy(e, grp=grp):
                last = None
                for q in range(4):
                    f = 4 * grp + q
                    last = e.tensor_copy(self.bv[1][:, 128 * f:128 * f + 128],
                                         self.pbt[:, 128 * q:128 * q + 128])
                return last
            self.v(cpy)
        def xp(e):
            last = None
            for h in range(32):
                last = e.tensor_scalar(self.bv[0][:, 64 * h:64 * h + 64],
                                       self.bv[1][:, 64 * h:64 * h + 64],
                                       self.st[:, h:h + 1], None, op0=ALU.mult)
            return last
        self.v(xp)
        self.v(lambda e: e.tensor_tensor(st2[:, 224:256], self.st[:, 0:32], st2[:, 160:192], op=ALU.mult))
        self.v(lambda e: e.tensor_copy(self.wb[1][:, 1536:1568], st2[:, 224:256]))
        self.v(lambda e: e.tensor_tensor(self.wb[1][:, 1568:1600], st2[:, 224:256],
                                         self.wb[1][:, 1536:1568], op=ALU.subtract))
        def mmcum(e):
            e.matmul(self.pb[5][:, 0:32], tri01, self.wb[1][:, 1536:1568], start=True, stop=False)
            e.matmul(self.pb[5][:, 0:32], tri01, self.wb[1][:, 1568:1600], start=False, stop=True)
            e.matmul(self.pb[5][:, 32:64], ones128, self.wb[1][:, 1536:1568], start=True, stop=False)
            return e.matmul(self.pb[5][:, 32:64], ones128, self.wb[1][:, 1568:1600], start=False, stop=True)
        self.t(mmcum)
        self.v(lambda e: (e.tensor_copy(st2[:, 0:32], self.pb[5][:, 0:32]),
                          e.tensor_copy(st2[:, 128:160], self.pb[5][:, 32:64]))[-1])
        self.v(lambda e: e.tensor_tensor(st2[:, 352:384], st2[:, 128:160], st2[:, 0:32], op=ALU.subtract))
        def exps(e):
            e.activation(st2[:, 32:64], st2[:, 352:384], AF.Exp)
            e.activation(st2[:, 64:96], st2[:, 0:32], AF.Exp)
            return e.activation(st2[:, 96:128], st2[:, 128:160], AF.Exp)
        self.s(exps)
        self.v(lambda e: e.tensor_tensor(st2[:, 320:352], st2[:, 64:96], st2[:, 256:288], op=ALU.mult))
        P.dma("sync", self.DSC["fac"][ch:ch + 1, :, :].rearrange("a p c -> (a p) c"), st2[:, 320:352])
        self.v(lambda e: (e.tensor_tensor(st2[:, 256:288], st2[:, 256:288], st2[:, 96:128], op=ALU.mult),
                          e.tensor_tensor(st2[:, 288:320], st2[:, 288:320], st2[:, 128:160], op=ALU.add))[-1])
        P.dma("sync", self.bk[1][0:64, 2176:2304], self.bk[1][64:128, 2048:2176])
        self.t(lambda e: e.transpose(self.pbt[:, 0:64], self.bk[1][0:64, 2048:2176], self.eyeb[0:64, 0:64]))
        self.v(lambda e: e.tensor_copy(self.bx[1][:, 2048:2112], self.pbt[:, 0:64]))
        self.t(lambda e: e.matmul(self.pb[5][:, 64:192], self.bk[1][0:64, 2048:2176],
                                  self.bk[1][0:64, 2176:2304], start=True, stop=True))
        self.v(lambda e: e.tensor_tensor(self.pb[5][:, 192:320], self.pb[5][:, 64:192], tri01, op=ALU.mult))
        Ef = self.bx[0][:, :].bitcast(F32)
        for w in range(4):
            def trows(e, w=w):
                last = None
                for hh in range(8):
                    bank = self.pb[4] if hh < 4 else self.pb[0]
                    last = e.transpose(bank[0:1, 128 * (hh % 4):128 * (hh % 4) + 128],
                                       st2[:, 8 * w + hh:8 * w + hh + 1], eyef)
                return last
            self.t(trows)
            def rows(e):
                last = None
                for hh in range(8):
                    bank = self.pb[4] if hh < 4 else self.pb[0]
                    src = bank[0:1, 128 * (hh % 4):128 * (hh % 4) + 128]
                    e.tensor_copy(self.r[17][0:1, 128 * hh:128 * hh + 128], src)
                    last = e.tensor_scalar(self.r[18][0:1, 128 * hh:128 * hh + 128],
                                           src, -1.0, None, op0=ALU.mult)
                return last
            self.v(rows)
            def mmdiff(e):
                last = None
                for hh in range(8):
                    reg = self.pb[2 + hh // 4][:, 128 * (hh % 4):128 * (hh % 4) + 128]
                    e.matmul(reg, self.ones_row, self.r[17][0:1, 128 * hh:128 * hh + 128],
                             start=True, stop=False)
                    last = e.matmul(reg, self.r[18][0:1, 128 * hh:128 * hh + 128], self.ones_row,
                                    start=False, stop=True)
                return last
            self.t(mmdiff)
            def clamp(e):
                last = None
                for hh in range(8):
                    reg = self.pb[2 + hh // 4][:, 128 * (hh % 4):128 * (hh % 4) + 128]
                    last = e.tensor_scalar(reg, reg, 0.0, None, op0=ALU.min)
                return last
            self.v(clamp)
            def expd(e):
                last = None
                for hh in range(8):
                    reg = self.pb[2 + hh // 4][:, 128 * (hh % 4):128 * (hh % 4) + 128]
                    last = e.activation(Ef[:, 128 * hh:128 * hh + 128], reg, AF.Exp)
                return last
            self.s(expd)
            def mt(e):
                last = None
                for hh in range(8):
                    last = e.tensor_tensor(self.wb[1][:, 128 * hh:128 * hh + 128],
                                           Ef[:, 128 * hh:128 * hh + 128],
                                           self.pb[5][:, 192:320], op=ALU.mult)
                return last
            self.v(mt)
            def mmyi(e, w=w):
                last = None
                for hh in range(8):
                    h = 8 * w + hh
                    last = e.matmul(self.pb[0][:, 64 * hh:64 * hh + 64],
                                    self.wb[1][:, 128 * hh:128 * hh + 128],
                                    self.bv[0][:, 64 * h:64 * h + 64], start=True, stop=True)
                return last
            self.t(mmyi)
            def stb(e, w=w):
                last = None
                for hh in range(8):
                    h = 8 * w + hh
                    last = e.tensor_copy(self.wb[1][0:64, 1600 + 64 * hh:1664 + 64 * hh], self._sst(h))
                return last
            self.v(stb)
            def mmyint(e, w=w):
                last = None
                for hh in range(8):
                    last = e.matmul(self.pb[1][:, 64 * hh:64 * hh + 64],
                                    self.bk[1][0:64, 2176:2304],
                                    self.wb[1][0:64, 1600 + 64 * hh:1664 + 64 * hh], start=True, stop=True)
                return last
            self.t(mmyint)
            def bs(e, w=w):
                last = None
                for hh in range(8):
                    h = 8 * w + hh
                    last = e.tensor_scalar(self.wb[1][:, 1024 + 64 * hh:1088 + 64 * hh],
                                           self.bx[1][:, 2048:2112], st2[:, 32 + h:33 + h],
                                           None, op0=ALU.mult)
                return last
            self.v(bs)
            def mmtc(e, w=w):
                last = None
                for hh in range(8):
                    h = 8 * w + hh
                    last = e.matmul(self.pb[4][0:64, 64 * hh:64 * hh + 64],
                                    self.wb[1][:, 1024 + 64 * hh:1088 + 64 * hh],
                                    self.bv[0][:, 64 * h:64 * h + 64], start=True, stop=True)
                return last
            self.t(mmtc)
            def ydst(h):
                return self.bv[2 + h // 16][:, :].bitcast(F32)[:, 64 * (h % 16):64 * (h % 16) + 64]
            def ev1(e, w=w):
                last = None
                for hh in range(8):
                    h = 8 * w + hh
                    last = e.tensor_scalar(ydst(h), self.pb[1][:, 64 * hh:64 * hh + 64],
                                           st2[:, 64 + h:65 + h], None, op0=ALU.mult)
                return last
            self.v(ev1)
            def ev2(e, w=w):
                last = None
                for hh in range(8):
                    h = 8 * w + hh
                    last = e.tensor_tensor(ydst(h), ydst(h), self.pb[0][:, 64 * hh:64 * hh + 64], op=ALU.add)
                return last
            self.v(ev2)
            def ev3(e, w=w):
                last = None
                for hh in range(8):
                    h = 8 * w + hh
                    last = e.scalar_tensor_tensor(ydst(h), self.bv[1][:, 64 * h:64 * h + 64],
                                                  st2[:, 192 + h:193 + h], ydst(h),
                                                  op0=ALU.mult, op1=ALU.add)
                return last
            self.v(ev3)
            def sup1(e, w=w):
                last = None
                for hh in range(8):
                    h = 8 * w + hh
                    last = e.tensor_scalar(self._sst(h), self._sst(h), st2[0:64, 96 + h:97 + h],
                                           None, op0=ALU.mult)
                return last
            self.v(sup1)
            def sup2(e, w=w):
                last = None
                for hh in range(8):
                    h = 8 * w + hh
                    last = e.tensor_tensor(self._sst(h), self._sst(h),
                                           self.pb[4][0:64, 64 * hh:64 * hh + 64], op=ALU.add)
                return last
            self.v(sup2)
        if ch == 7:
            self.dump("xbc8", self.bk[1][:, 0:2176], (128, 2176), BF16)
            self.dump("dt8", self.st[:, 0:32], (128, 32))
            self.dump("z8", self.bx[1][:, 0:2048], (128, 2048), BF16)
            self.dump("yloc8a", self.bv[2][:, :].bitcast(F32)[:, 0:1024], (128, 1024))
            self.dump("yloc8b", self.bv[3][:, :].bitcast(F32)[:, 0:1024], (128, 1024))
        P.dmas("gpsimd", [(self.DSC["Y"][128 * ch:128 * ch + 128, 0:1024], self.bv[2][:, :].bitcast(F32)[:, 0:1024]),
                          (self.DSC["Y"][128 * ch:128 * ch + 128, 1024:2048], self.bv[3][:, :].bitcast(F32)[:, 0:1024])])
        P.dma("sync", self.DSC["z"][128 * ch:128 * ch + 128, :], self.bx[1][:, 0:2048])
        P.dma("sync", self.DSC["cc"][ch:ch + 1, :, :].rearrange("a p n -> (a p) n"),
              self.bk[1][64:128, 2048:2176])

    def _g_prefix(self):
        P, CC, smal, st2 = self.P, self.CC, self.smal, self.st2
        self.dump("sloc_a", self.r[19][0:64, 0:1024], (64, 1024))
        self.dump("sloc_b", self.bk[3][:, :].bitcast(F32)[0:64, 64:1088], (64, 1024))
        P.dmas("sync", [(CC["mb_in"].ap()[0:1, 4096 * h:4096 * h + 4096].rearrange("a (s d) -> (a s) d", s=64),
                         self._sst(h)) for h in range(32)])
        self.s(lambda e: e.activation(st2[:, 320:352], st2[:, 288:320], AF.Exp))
        self.dump("gamv", st2[:, 320:352], (128, 32))
        P.dma("sync", CC["mb_in"].ap()[0:1, 131072:131104], st2[0:1, 320:352])
        self.gp(lambda e: e.collective_compute(
            "AllGather", ALU.bypass, replica_groups=[list(range(NC))],
            ins=[CC["mb_in"].ap().opt()], outs=[CC["mb_out"].ap().opt()]))
        def hz(e):
            last = None
            for h in range(32):
                last = e.memset(self._sst(h), 0.0)
            return last
        self.v(hz)
        for cc in range(7):
            P.dmas("sync", [(self._stg(h),
                             CC["mb_out"].ap()[cc:cc + 1, 4096 * h:4096 * h + 4096].rearrange("a (s d) -> (a s) d", s=64))
                            for h in range(32)] +
                           [(smal[0:1, 181:213], CC["mb_out"].ap()[cc:cc + 1, 131072:131104])])
            self.t(lambda e: e.matmul(self.pb[5][:, 320:352], self.ones_row, smal[0:1, 181:213],
                                      start=True, stop=True))
            self.v(lambda e: e.tensor_scalar(st2[:, 352:384], self.pb[5][:, 320:352], 1.0, None, op0=ALU.subtract))
            self.v(lambda e, cc=cc: e.tensor_scalar(st2[:, 352:384], st2[:, 352:384],
                                                    smal[:, 213 + cc:214 + cc], None, op0=ALU.mult))
            self.v(lambda e: e.tensor_scalar(st2[:, 352:384], st2[:, 352:384], 1.0, None, op0=ALU.add))
            def hsc(e):
                last = None
                for h in range(32):
                    last = e.tensor_scalar(self._sst(h), self._sst(h), st2[0:64, 352 + h:353 + h],
                                           None, op0=ALU.mult)
                return last
            self.v(hsc)
            def hup(e, cc=cc):
                last = None
                for h in range(32):
                    last = e.scalar_tensor_tensor(self._sst(h), self._stg(h),
                                                  smal[0:64, 213 + cc:214 + cc], self._sst(h),
                                                  op0=ALU.mult, op1=ALU.add)
                return last
            self.v(hup)

    def _g_pass2(self, ch):
        P, smal, st2 = self.P, self.smal, self.st2
        P.dmas("gpsimd", [(self.bv[2][:, :].bitcast(F32)[:, 0:1024], self.DSC["Y"][128 * ch:128 * ch + 128, 0:1024]),
                          (self.bv[3][:, :].bitcast(F32)[:, 0:1024], self.DSC["Y"][128 * ch:128 * ch + 128, 1024:2048])])
        P.dmas("sync", [(self.bx[1][:, 0:2048], self.DSC["z"][128 * ch:128 * ch + 128, :]),
                        (self.bk[1][0:64, 2048:2176], self.DSC["cc"][ch:ch + 1, :, :].rearrange("a p n -> (a p) n")),
                        (st2[:, 0:32], self.DSC["fac"][ch:ch + 1, :, :].rearrange("a p c -> (a p) c"))])
        def mmcor(e):
            last = None
            for h in range(32):
                last = e.matmul(self.pb[h // 8][:, 64 * (h % 8):64 * (h % 8) + 64],
                                self.bk[1][0:64, 2048:2176], self.wb[1][0:64, 64 * h:64 * h + 64],
                                start=True, stop=True)
            return last
        self.t(mmcor)
        def ydst(h):
            return self.bv[2 + h // 16][:, :].bitcast(F32)[:, 64 * (h % 16):64 * (h % 16) + 64]
        def corr(e):
            last = None
            for h in range(32):
                last = e.scalar_tensor_tensor(ydst(h), self.pb[h // 8][:, 64 * (h % 8):64 * (h % 8) + 64],
                                              st2[:, h:h + 1], ydst(h), op0=ALU.mult, op1=ALU.add)
            return last
        self.v(corr)
        sz = self.bx[0][:, 0:2048]
        self.s(lambda e: e.activation(sz, self.bx[1][:, 0:2048], AF.Silu))
        self.v(lambda e: (e.tensor_tensor(self.bv[2][:, :].bitcast(F32)[:, 0:1024],
                                          self.bv[2][:, :].bitcast(F32)[:, 0:1024], sz[:, 0:1024], op=ALU.mult),
                          e.tensor_tensor(self.bv[3][:, :].bitcast(F32)[:, 0:1024],
                                          self.bv[3][:, :].bitcast(F32)[:, 0:1024], sz[:, 1024:2048], op=ALU.mult))[-1])
        va, vb, rs = smal[:, 49:50], smal[:, 50:51], smal[:, 51:52]
        def sq(e):
            e.activation(self.repa[:, :], self.bv[2][:, :].bitcast(F32)[:, 0:1024], AF.Square, accum_out=va)
            return e.activation(self.repa[:, :], self.bv[3][:, :].bitcast(F32)[:, 0:1024], AF.Square, accum_out=vb)
        self.s(sq)
        self.v(lambda e: e.tensor_tensor(va, va, vb, op=ALU.add))
        self.v(lambda e: e.tensor_scalar(va, va, 1.0 / 2048.0, None, op0=ALU.mult))
        self.rsqrt(rs, va)
        self.v(lambda e: (e.tensor_scalar(self.bv[2][:, :].bitcast(F32)[:, 0:1024],
                                          self.bv[2][:, :].bitcast(F32)[:, 0:1024], rs, None, op0=ALU.mult),
                          e.tensor_scalar(self.bv[3][:, :].bitcast(F32)[:, 0:1024],
                                          self.bv[3][:, :].bitcast(F32)[:, 0:1024], rs, None, op0=ALU.mult))[-1])
        yn = self.bx[0][:, 0:2048]
        self.v(lambda e: (e.tensor_tensor(yn[:, 0:1024], self.bv[2][:, :].bitcast(F32)[:, 0:1024],
                                          self.wb[2][:, 0:1024], op=ALU.mult),
                          e.tensor_tensor(yn[:, 1024:2048], self.bv[3][:, :].bitcast(F32)[:, 0:1024],
                                          self.wb[2][:, 1024:2048], op=ALU.mult))[-1])
        for grp in range(4):
            def trs(e, grp=grp):
                last = None
                for q in range(4):
                    kt = 4 * grp + q
                    last = e.transpose(self.pbt[:, 128 * q:128 * q + 128],
                                      yn[:, 128 * kt:128 * kt + 128], self.eyeb)
                return last
            self.t(trs)
            def cpy(e, grp=grp):
                last = None
                for q in range(4):
                    kt = 4 * grp + q
                    last = e.tensor_copy(self.bk[0][:, 128 * kt:128 * kt + 128],
                                         self.pbt[:, 128 * q:128 * q + 128])
                return last
            self.v(cpy)
        def mmo3(e):
            last = None
            for nh in range(2):
                for kt in range(16):
                    last = e.matmul(self.pb[4 + nh][:, 0:512], self.bk[0][:, 128 * kt:128 * kt + 128],
                                    self.r[kt // 2][:, :].bitcast(BF16)[:, 1024 * (kt % 2) + 512 * nh:1024 * (kt % 2) + 512 * nh + 512],
                                    start=(kt == 0), stop=(kt == 15))
            return last
        self.t(mmo3)
        P.dma("sync", self.repa[:, :], self.DSC["y2"][128 * (ch + 1):128 * (ch + 1) + 128, :])
        self.v(lambda e: (e.tensor_tensor(self.repb[:, 0:512], self.pb[4][:, 0:512], self.repa[:, 0:512], op=ALU.add),
                          e.tensor_tensor(self.repb[:, 512:1024], self.pb[5][:, 0:512], self.repa[:, 512:1024], op=ALU.add))[-1])
        P.dma("sync", self.DSC["y3"][128 * ch:128 * ch + 128, :], self.repb[:, :])


    def stageH_moe(self):
        P, IN, CC, smal, st2 = self.P, self.IN, self.CC, self.smal, self.st2
        eyef = self.cst2[:, 384:512]
        for t in range(16):
            yt, lnf = self.repa, self.repb
            m0, r0, v0 = smal[:, 49:50], smal[:, 50:51], smal[:, 51:52]
            P.dma("sync", yt[:, :], self.DSC["y3"][128 * t:128 * t + 128, :])
            self.v(lambda e: e.reduce_sum(m0, yt[:, :], axis=AXX))
            self.v(lambda e: e.tensor_scalar(m0, m0, 1.0 / D, None, op0=ALU.mult))
            self.v(lambda e: e.tensor_scalar(lnf[:, :], yt[:, :], m0, None, op0=ALU.subtract))
            self.s(lambda e: e.activation(yt[:, :], lnf[:, :], AF.Square, accum_out=v0))
            self.v(lambda e: e.tensor_scalar(v0, v0, 1.0 / D, None, op0=ALU.mult))
            self.rsqrt(r0, v0)
            self.v(lambda e: e.tensor_scalar(lnf[:, :], lnf[:, :], r0, None, op0=ALU.mult))
            P.dma("scalar", self.DSC["acc"][128 * t:128 * t + 128, :], lnf[:, :])
            def mmsum(e, t=t):
                e.matmul(self.pb[4][0:1, 0:512], self.ones_col, lnf[:, 0:512],
                         start=(t == 0), stop=(t == 15))
                return e.matmul(self.pb[5][0:1, 0:512], self.ones_col, lnf[:, 512:1024],
                                start=(t == 0), stop=(t == 15))
            self.t(mmsum)
            lnb = self.bx[0][:, 0:1024]
            self.v(lambda e: e.tensor_copy(lnb[:, :], lnf[:, :]))
            for grp in range(2):
                def trs(e, grp=grp):
                    last = None
                    for q in range(4):
                        kk = 4 * grp + q
                        last = e.transpose(self.pbt[:, 128 * q:128 * q + 128],
                                          lnb[:, 128 * kk:128 * kk + 128], self.eyeb)
                    return last
                self.t(trs)
                def cpy(e, grp=grp):
                    last = None
                    for q in range(4):
                        kk = 4 * grp + q
                        last = e.tensor_copy(self.bx[1][:, 128 * kk:128 * kk + 128],
                                             self.pbt[:, 128 * q:128 * q + 128])
                    return last
                self.v(cpy)
            P.dma("sync", self.DSC["ln3F"][t:t + 1, :, :, :].rearrange("a k p n -> (a p) k n"),
                  self.bx[1][:, 0:1024].rearrange("p (k n) -> p k n", n=128))
        self.v(lambda e: (e.tensor_copy(self.repa[0:1, 0:512], self.pb[4][0:1, 0:512]),
                          e.tensor_copy(self.repa[0:1, 512:1024], self.pb[5][0:1, 0:512]))[-1])
        P.dma("sync", CC["mu_in"].ap()[0:1, :], self.repa[0:1, :])
        self.gp(lambda e: e.collective_compute(
            "AllReduce", ALU.add, replica_groups=[list(range(NC))],
            ins=[CC["mu_in"].ap().opt()], outs=[CC["mu_out"].ap().opt()]))
        P.dma("sync", smal[:, 96:104], CC["mu_out"].ap()[0:1, :].rearrange("a (k p) -> (a p) k", p=128), slow=True)
        self.v(lambda e: e.tensor_scalar(smal[:, 96:104], smal[:, 96:104], 1.0 / T, None, op0=ALU.mult))
        P.dma("sync", smal[:, 104:168].rearrange("p (k e) -> p k e", e=8),
              IN["w_r"].rearrange("(k p) e -> p k e", p=128), slow=True)
        def mmlg(e):
            last = None
            for kt in range(8):
                last = e.matmul(self.pb[0][0:8, 0:1], smal[:, 104 + 8 * kt:112 + 8 * kt],
                                smal[:, 96 + kt:97 + kt], start=(kt == 0), stop=(kt == 7))
            return last
        self.t(mmlg)
        P.dma("sync", smal[0:8, 168:169], IN["b_r"].rearrange("a e -> e a"), slow=True)
        self.v(lambda e: e.tensor_tensor(smal[0:8, 169:170], self.pb[0][0:8, 0:1], smal[0:8, 168:169], op=ALU.add))
        self.t(lambda e: e.transpose(self.pb[0][0:1, 4:12], smal[0:8, 169:170], eyef[0:8, 0:8]))
        self.v(lambda e: e.tensor_copy(smal[0:1, 170:178], self.pb[0][0:1, 4:12]))
        self.s(lambda e: e.activation(smal[0:1, 178:186], smal[0:1, 170:178], AF.Exp))
        self.v(lambda e: e.reduce_sum(smal[0:1, 186:187], smal[0:1, 178:186], axis=AXX))
        self.v(lambda e: e.reciprocal(smal[0:1, 187:188], smal[0:1, 186:187]))
        self.v(lambda e: e.tensor_scalar(smal[0:1, 188:196], smal[0:1, 178:186], smal[0:1, 187:188], None, op0=ALU.mult))
        self.v(lambda e: e.reduce_max(smal[0:1, 196:197], smal[0:1, 188:196], axis=AXX))
        self.v(lambda e: e.memset(smal[0:1, 197:198], NEG))
        self.v(lambda e: e.tensor_scalar(smal[0:1, 198:206], smal[0:1, 188:196], smal[0:1, 196:197], None, op0=ALU.is_equal))
        self.v(lambda e: e.scalar_tensor_tensor(smal[0:1, 206:214], smal[0:1, 198:206], smal[0:1, 197:198],
                                                smal[0:1, 188:196], op0=ALU.mult, op1=ALU.add))
        self.v(lambda e: e.reduce_max(smal[0:1, 214:215], smal[0:1, 206:214], axis=AXX))
        self.v(lambda e: e.tensor_scalar(smal[0:1, 215:223], smal[0:1, 206:214], smal[0:1, 214:215], None, op0=ALU.is_equal))
        self.v(lambda e: (e.tensor_copy(smal[0:1, 223:224], smal[0:1, 196:197]),
                          e.tensor_copy(smal[0:1, 224:225], smal[0:1, 214:215]))[-1])
        self.s(lambda e: e.activation(smal[0:1, 225:227], smal[0:1, 223:225], AF.Exp))
        self.v(lambda e: e.reduce_sum(smal[0:1, 227:228], smal[0:1, 225:227], axis=AXX))
        self.v(lambda e: e.reciprocal(smal[0:1, 228:229], smal[0:1, 227:228]))
        self.v(lambda e: e.tensor_scalar(smal[0:1, 229:231], smal[0:1, 225:227], smal[0:1, 228:229], None, op0=ALU.mult))
        self.v(lambda e: (e.tensor_tensor(smal[0:1, 231:239], smal[0:1, 198:206], self.cst[0:1, 0:8], op=ALU.mult),
                          e.tensor_tensor(smal[0:1, 239:247], smal[0:1, 215:223], self.cst[0:1, 0:8], op=ALU.mult))[-1])
        self.v(lambda e: (e.reduce_sum(smal[0:1, 92:93], smal[0:1, 231:239], axis=AXX),
                          e.reduce_sum(smal[0:1, 93:94], smal[0:1, 239:247], axis=AXX))[-1])
        def bcasts(e):
            e.matmul(self.pb[0][:, 0:1], self.ones_row, smal[0:1, 92:93], start=True, stop=True)
            e.matmul(self.pb[0][:, 1:2], self.ones_row, smal[0:1, 93:94], start=True, stop=True)
            e.matmul(self.pb[0][:, 2:3], self.ones_row, smal[0:1, 229:230], start=True, stop=True)
            return e.matmul(self.pb[0][:, 3:4], self.ones_row, smal[0:1, 230:231], start=True, stop=True)
        self.t(bcasts)
        self.v(lambda e: e.tensor_copy(smal[:, 231:235], self.pb[0][:, 0:4]))
        def selcols(e):
            e.transpose(self.pb[0][0:8, 16:17], smal[0:1, 198:206], eyef[0:1, 0:1])
            return e.transpose(self.pb[0][0:8, 17:18], smal[0:1, 215:223], eyef[0:1, 0:1])
        self.t(selcols)
        self.v(lambda e: e.tensor_copy(smal[0:8, 238:240], self.pb[0][0:8, 16:18]))
        self.v(lambda e: e.tensor_copy(self.bx[1][0:8, 2548:2550], smal[0:8, 238:240]))
        self.v(lambda e: e.tensor_copy(smal[:, 236:237], self.cstb[:, 513:514]))
        self.v(lambda e: e.tensor_scalar(smal[:, 236:237], smal[:, 236:237], 1.0 / 128.0, None, op0=ALU.mult))
        self.dump("router", smal[0:1, 170:247], (1, 77))
        for ee in range(2):
            self._moe_expert(ee, t_sel=238 + ee, t_e=231 + ee, t_wt=233 + ee)
        ga = self.bq[0][:, :].bitcast(F32)[:, 0:1024]
        be = self.bq[1][:, :].bitcast(F32)[:, 0:1024]
        P.dmas("sync", [(ga, IN["gamma"]), (be, IN["beta"])])
        for t in range(16):
            at, lnf = self.repa, self.repb
            m0, r0, v0 = smal[:, 49:50], smal[:, 50:51], smal[:, 51:52]
            P.dma("sync", at[:, :], self.DSC["acc"][128 * t:128 * t + 128, :])
            self.v(lambda e: e.reduce_sum(m0, at[:, :], axis=AXX))
            self.v(lambda e: e.tensor_scalar(m0, m0, 1.0 / D, None, op0=ALU.mult))
            self.v(lambda e: e.tensor_scalar(lnf[:, :], at[:, :], m0, None, op0=ALU.subtract))
            self.s(lambda e: e.activation(at[:, :], lnf[:, :], AF.Square, accum_out=v0))
            self.v(lambda e: e.tensor_scalar(v0, v0, 1.0 / D, None, op0=ALU.mult))
            self.rsqrt(r0, v0)
            self.v(lambda e: e.tensor_scalar(lnf[:, :], lnf[:, :], r0, None, op0=ALU.mult))
            self.v(lambda e: e.tensor_tensor(lnf[:, :], lnf[:, :], ga, op=ALU.mult))
            self.v(lambda e: e.tensor_tensor(lnf[:, :], lnf[:, :], be, op=ALU.add))
            P.dma("sync", at[:, :], self.DSC["y3"][128 * t:128 * t + 128, :])
            self.v(lambda e: e.tensor_tensor(lnf[:, :], lnf[:, :], at[:, :], op=ALU.add))
            P.dma("sync", self.out[128 * t:128 * t + 128, :], lnf[:, :])

    def _w2slot(self, kt2):
        bufs = ([(self.bq[i], o) for i in range(4) for o in (0, 1024)]
                + [(self.bk[i], o) for i in range(4) for o in (0, 1024)]
                + [(self.bv[i], o) for i in range(4) for o in (0, 1024)]
                + [(self.wb[0], o) for o in (0, 1024, 2048, 3072)]
                + [(self.r[16][:, :].bitcast(BF16), o) for o in (0, 1024)]
                + [(self.r[17][:, :].bitcast(BF16), o) for o in (0, 1024)])
        b, o = bufs[kt2]
        return b[:, o:o + 1024]

    def _moe_expert(self, ee, t_sel, t_e, t_wt):
        P, IN, smal, idx = self.P, self.IN, self.smal, self.idx
        selT = self.bk[0][0:8, 2048:2176]
        self.t(lambda e: e.matmul(self.pb[1][0:8, 0:128], smal[0:1, (198 if ee == 0 else 215):(206 if ee == 0 else 223)],
                                  self.ones_row, start=True, stop=True))
        P.op16("gpsimd", lambda e: e.dma_start(self.wb[1][0:8, 0:1024], IN["moe_b2"]))
        self.v(lambda e: e.tensor_copy(selT, self.pb[1][0:8, 0:128]))
        selb = self.bx[1][0:8, 2548 + ee:2549 + ee]
        def mmb2(e, selb=selb):
            e.matmul(self.pb[2][0:1, 0:512], selb, self.wb[1][0:8, 0:512], start=True, stop=True)
            return e.matmul(self.pb[3][0:1, 0:512], selb, self.wb[1][0:8, 512:1024], start=True, stop=True)
        self.t(mmb2)
        self.v(lambda e: (e.tensor_copy(self.bv[0][0:1, 2048:2560], self.pb[2][0:1, 0:512]),
                          e.tensor_copy(self.bv[1][0:1, 2048:2560], self.pb[3][0:1, 0:512]))[-1])
        P.op16("gpsimd", lambda e: e.dma_start(self.wb[1][0:8, 0:4096], IN["moe_b1"]))
        self.v(lambda e, t_e=t_e: e.tensor_scalar(smal[:, 235:236], smal[:, t_e:t_e + 1], 1024.0, None, op0=ALU.mult))
        self.v(lambda e: e.tensor_tensor(smal[:, 235:236], smal[:, 235:236], smal[:, 236:237], op=ALU.add))
        def i1(e):
            last = None
            for kt in range(8):
                last = e.tensor_scalar(smal[:, 72 + kt:73 + kt], smal[:, 235:236], float(128 * kt), None, op0=ALU.add)
            return last
        self.v(i1)
        self.v(lambda e: e.tensor_copy(idx[:, 0:8], smal[:, 72:80]))
        for kt in range(8):
            P.op16("gpsimd", lambda e, kt=kt: e.indirect_dma_start(
                self.wb[2][:, :], None, IN["moe_w1"],
                IndirectOffsetOnAxis(ap=idx[:, kt:kt + 1], axis=0)))
            self.v(lambda e, kt=kt: (
                e.tensor_copy(self.r[2 * kt][:, :].bitcast(BF16), self.wb[2][:, 0:2048]),
                e.tensor_copy(self.r[2 * kt + 1][:, :].bitcast(BF16), self.wb[2][:, 2048:4096]))[-1])
        self.v(lambda e, t_e=t_e: e.tensor_scalar(smal[:, 235:236], smal[:, t_e:t_e + 1], 4096.0, None, op0=ALU.mult))
        self.v(lambda e: e.tensor_tensor(smal[:, 235:236], smal[:, 235:236], smal[:, 236:237], op=ALU.add))
        def i2(e):
            last = None
            for kt in range(32):
                last = e.tensor_scalar(smal[:, 72 + kt:73 + kt], smal[:, 235:236], float(128 * kt), None, op0=ALU.add)
            return last
        self.v(i2)
        self.v(lambda e: e.tensor_copy(idx[:, 0:32], smal[:, 72:104]))
        for kt in range(32):
            P.op16("gpsimd", lambda e, kt=kt: e.indirect_dma_start(
                self._w2slot(kt), None, IN["moe_w2"],
                IndirectOffsetOnAxis(ap=idx[:, kt:kt + 1], axis=0)))
        for t in range(16):
            P.dma("sync", self.bx[0][:, 0:1024].rearrange("p (k n) -> p k n", n=128),
                  self.DSC["ln3F"][t:t + 1, :, :, :].rearrange("a k p n -> (a p) k n"))
            for half in range(2):
                def mmh1(e, half=half):
                    last = None
                    for b in range(4):
                        for kk in range(8):
                            e.matmul(self.pb[b][:, 0:512], self.bx[0][:, 128 * kk:128 * kk + 128],
                                     self.r[2 * kk + half][:, :].bitcast(BF16)[:, 512 * b:512 * b + 512],
                                     start=(kk == 0), stop=False)
                        last = e.matmul(self.pb[b][:, 0:512], selT,
                                        self.wb[1][0:8, 2048 * half + 512 * b:2048 * half + 512 * b + 512],
                                        start=False, stop=True)
                    return last
                self.t(mmh1)
                def gel(e, half=half):
                    last = None
                    for b in range(4):
                        last = e.activation(self.wb[2][:, 2048 * half + 512 * b:2048 * half + 512 * b + 512],
                                            self.pb[b][:, 0:512], AF.Gelu)
                    return last
                self.s(gel)
            for grp in range(8):
                def trs(e, grp=grp):
                    last = None
                    for q in range(4):
                        kt2 = 4 * grp + q
                        last = e.transpose(self.pbt[:, 128 * q:128 * q + 128],
                                          self.wb[2][:, 128 * kt2:128 * kt2 + 128], self.eyeb)
                    return last
                self.t(trs)
                def cpy(e, grp=grp):
                    last = None
                    for q in range(4):
                        kt2 = 4 * grp + q
                        dst = (self.bx[0] if kt2 < 16 else self.bx[1])[:, 128 * (kt2 % 16):128 * (kt2 % 16) + 128]
                        last = e.tensor_copy(dst, self.pbt[:, 128 * q:128 * q + 128])
                    return last
                self.v(cpy)
            def mmo(e):
                last = None
                for nh in range(2):
                    for kt2 in range(32):
                        lhs = (self.bx[0] if kt2 < 16 else self.bx[1])[:, 128 * (kt2 % 16):128 * (kt2 % 16) + 128]
                        e.matmul(self.pb[4 + nh][:, 0:512], lhs,
                                 self._w2slot(kt2)[:, 512 * nh:512 * nh + 512],
                                 start=(kt2 == 0), stop=False)
                    last = e.matmul(self.pb[4 + nh][:, 0:512], self.onesb_row,
                                    self.bv[nh][0:1, 2048:2560], start=False, stop=True)
                return last
            self.t(mmo)
            P.dma("sync", self.repa[:, :], self.DSC["acc"][128 * t:128 * t + 128, :])
            def accu(e, t_wt=t_wt):
                e.scalar_tensor_tensor(self.repa[:, 0:512], self.pb[4][:, 0:512], smal[:, t_wt:t_wt + 1],
                                       self.repa[:, 0:512], op0=ALU.mult, op1=ALU.add)
                return e.scalar_tensor_tensor(self.repa[:, 512:1024], self.pb[5][:, 0:512], smal[:, t_wt:t_wt + 1],
                                              self.repa[:, 512:1024], op0=ALU.mult, op1=ALU.add)
            self.v(accu)
            P.dma("sync", self.DSC["acc"][128 * t:128 * t + 128, :], self.repa[:, :])


def kernel(**inputs):
    shared, per_core = _host_stage(inputs)
    nc, dbg_aps = build_program(debug=DEBUG)
    in_maps = []
    for c in range(NC):
        m = dict(shared)
        m.update(per_core[c])
        in_maps.append({k: np.ascontiguousarray(v, dtype=np.float32) for k, v in m.items()})
    res = run_bass_kernel_spmd(nc, in_maps, core_ids=list(range(NC)))
    kernel.last_results = res.results
    outs = [np.asarray(res.results[c]["out"]) for c in range(NC)]
    return np.concatenate(outs, 0).reshape(1, T, D)


# version 03
# speedup vs baseline: 1.0223x; 1.0086x over previous
import numpy as np
import contextlib
import concourse.bass as bass
import concourse.mybir as mybir
from concourse.bass_utils import run_bass_kernel_spmd
from concourse.bass_core_rust import IndirectOffsetOnAxis

F32 = mybir.dt.float32
BF16 = mybir.dt.bfloat16
U32 = mybir.dt.uint32
I8 = mybir.dt.int8
AF = mybir.ActivationFunctionType
ALU = mybir.AluOpType
AXX = mybir.AxisListType.X

NC, S, T, D = 8, 2048, 16384, 1024
XT, AT, XZT = 20, 18, 17
XCOLS, ACOLS, XZCOLS = 2560, 2304, 2176
MAGIC = 12582912.0
NEG = -1e30

DEBUG = []



def _host_stage(inputs):
    g = lambda k: np.asarray(inputs[k], dtype=np.float32)
    x = np.ascontiguousarray(g("L0_attn_x").reshape(T, D))
    rep = lambda v: np.ascontiguousarray(np.tile(np.asarray(v, np.float32).reshape(1, -1), (128, 1)))
    shared = {
        "w_qkv": g("L0_attn_wqkv_l"), "w_ol": g("L0_attn_wo_l"),
        "w_qh": g("L0_attn_wq_h"), "w_kvh": g("L0_attn_wkv_h"), "w_oh": g("L0_attn_wo_h"),
        "gam_h": np.ascontiguousarray(g("L0_attn_gamma_h").reshape(8, 128).T),
        "rtq_rep": rep(g("L0_attn_rt_q")), "rtkv_rep": rep(g("L0_attn_rt_kv")),
        "lngb": np.concatenate([rep(g("L0_attn_ln_g")), rep(g("L0_attn_ln_b"))], 1),
        "w1T": np.ascontiguousarray(g("L0_ffn_w1").T),
        "w2T": np.ascontiguousarray(g("L0_ffn_w2").T),
        "cw0r": np.ascontiguousarray(g("L0_ffn_conv_w").reshape(32, 128, 3).transpose(1, 0, 2).reshape(128, 96)),
        "cb0r": np.ascontiguousarray(g("L0_ffn_conv_b").reshape(32, 128).T),
        "w_in_z": np.ascontiguousarray(g("L1_attn_W_in")[:, 0:2048]),
        "w_in_fm": None,
        "cw1r": np.ascontiguousarray(g("L1_attn_conv_w").reshape(17, 128, 4).transpose(1, 0, 2).reshape(128, 68)),
        "cb1r": np.ascontiguousarray(g("L1_attn_conv_b").reshape(17, 128).T),
        "a_log": g("L1_attn_A_log").reshape(32, 1), "d_par": g("L1_attn_D_param").reshape(32, 1),
        "dt_b": g("L1_attn_dt_bias").reshape(32, 1), "nrm_w": rep(g("L1_attn_norm_w")),
        "w_out": g("L1_attn_W_out"),
        "moe_w1": np.ascontiguousarray(g("L1_ffn_W1").reshape(8 * 1024, 4096)),
        "moe_b1": g("L1_ffn_b1"),
        "moe_w2": np.ascontiguousarray(g("L1_ffn_W2").reshape(8 * 4096, 1024)),
        "moe_b2": g("L1_ffn_b2"),
        "w_r": g("L1_ffn_Wr"), "b_r": g("L1_ffn_br").reshape(1, 8),
        "gamma": rep(g("L1_ffn_gamma")), "beta": rep(g("L1_ffn_beta")),
        "xfull": x,
        "iota_r": np.tile(np.arange(1024, dtype=np.float32)[None, :], (128, 1)),
    }
    w_in = g("L1_attn_W_in")
    bc = w_in[:, 2048:4224].reshape(8, 128, 17, 128).transpose(1, 0, 2, 3).reshape(128, 8 * 17 * 128)
    dtp = np.zeros((128, 8, 128), np.float32)
    dtp[:, :, 0:32] = w_in[:, 4224:4256].reshape(8, 128, 32).transpose(1, 0, 2)
    shared["w_in_fm"] = np.ascontiguousarray(np.concatenate([bc, dtp.reshape(128, 1024)], 1))
    c2 = np.zeros((128, 768), np.float32)
    c2[:, 0:128] = (np.arange(128)[:, None] < np.arange(128)[None, :])
    c2[:, 256:384] = np.where(np.arange(128)[:, None] > np.arange(128)[None, :], NEG, 0.0)
    c2[:, 384:512] = np.eye(128)
    c2[:, 512] = 1.0
    c2[0, 640:768] = 1.0
    cb = np.zeros((128, 1024), np.float32)
    cb[0, 0:64] = 1.0; cb[1, 64:128] = 1.0; cb[2, 0:128] = 1.0
    cb[:, 128:256] = np.eye(128)
    cb[:, 256] = 1.0
    cb[0, 257:385] = 1.0
    cb[:, 385:513] = np.tile(np.arange(128, dtype=np.float32)[None, :], (128, 1))
    cb[:, 513] = np.arange(128) * 128.0
    cb[:, 514:642] = (np.arange(128)[:, None] <= np.arange(128)[None, :])
    cb[:, 642:770] = 1.0
    shared.update({"cst2c": c2, "cstbc": cb,
                   "hsel": np.repeat(np.eye(32, dtype=np.float32), 64, 1)})

    per_core = []
    for c in range(NC):
        start = c * S
        xrows = np.clip(np.arange(start - 256, start - 256 + XCOLS), 0, T - 1)
        maskV = np.zeros((AT, 4, 256), np.float32)
        for t in range(AT):
            maskV[t, 0, 192:] = NEG
            maskV[t, 1, :64] = NEG
            gcols = start - 192 + 128 * t + np.arange(256)
            maskV[t, 2, (gcols < 0) | (gcols >= T)] = NEG
        tokrow = np.tile((start - 128 + np.arange(ACOLS, dtype=np.float32))[None, :], (128, 1))
        PP, FF = np.meshgrid(np.arange(128), np.arange(128), indexing="ij")
        gtok = (PP // 16) * 2048 + (FF % 16) * 128 + (PP % 16) * 8 + FF // 16
        ghi = (PP // 16) * 16 + (FF % 16)
        glo = (PP % 16) * 8 + FF // 16
        hmask = np.ones((128, 2), np.float32)
        if c == 0:
            hmask[:, 0] = 0.0
        if c == NC - 1:
            hmask[:, 1] = 0.0
        xzmask = np.zeros((128, 1), np.float32) if c == 0 else np.ones((128, 1), np.float32)
        m8 = np.zeros((1, 8), np.float32); m8[0, c] = 1.0
        pm8 = np.zeros((128, 8), np.float32); pm8[:, :c] = 1.0
        per_core.append({"xloc": np.ascontiguousarray(x[xrows]),
                         "maskV": np.ascontiguousarray(maskV), "tokrow": tokrow,
                         "gtok": gtok.astype(np.float32), "ghi": ghi.astype(np.float32),
                         "glo": glo.astype(np.float32), "hmask": hmask,
                         "xzmask": xzmask, "m8": m8, "pm8": pm8})
    return shared, per_core



class Prog:
    def __init__(self):
        self.phases = []

    def dma(self, eng, dst, src, slow=False):
        self.phases.append((eng, "dma", [(dst, src, slow)]))

    def dmas(self, eng, pairs, slow=False):
        self.phases.append((eng, "dma", [(d, s, slow) for d, s in pairs]))

    def op(self, eng, fn):
        self.phases.append((eng, "op", fn))

    def op16(self, eng, fn):
        self.phases.append((eng, "op16", fn))

    def dmas_mq(self, pairs):
        QE = ["sync", "scalar", "gpsimd"]
        self.pdma([(QE[i % 3], d, sr) for i, (d, sr) in enumerate(pairs)])

    def pdma(self, items):
        self.phases.append(("*", "pdma", items))

    def emit(self, blk, sem):
        starts, c = [], 0
        for eng, kind, payload in self.phases:
            starts.append(c)
            c += 16 * len(payload) if kind in ("dma", "pdma") else (16 if kind == "op16" else 1)
        total = c

        def run(engobj, engname):
            for (eng, kind, payload), s0 in zip(self.phases, starts):
                if kind == "pdma":
                    if not any(p[0] == engname for p in payload):
                        continue
                elif eng != engname:
                    continue
                engobj.wait_ge(sem, s0)
                if kind == "pdma":
                    for peng, dst, src in payload:
                        if peng == engname:
                            engobj.dma_start(dst, src).then_inc(sem, 16)
                elif kind == "dma":
                    for dst, src, slow in payload:
                        engobj.dma_start(dst, src, allow_slow_non_contiguous=slow).then_inc(sem, 16)
                elif kind == "op16":
                    payload(engobj).then_inc(sem, 16)
                else:
                    last = payload(engobj)
                    (engobj.nop() if last is None else last).then_inc(sem, 1)

        @blk.sync
        def _(e): run(e, "sync")
        @blk.vector
        def _(e): run(e, "vector")
        @blk.scalar
        def _(e): run(e, "scalar")
        @blk.tensor
        def _(e): run(e, "tensor")
        @blk.gpsimd
        def _(e):
            run(e, "gpsimd")
            e.wait_ge(sem, total)


IN_SHAPES = [
    ("w_qkv", (1024, 1536)), ("w_ol", (512, 1024)), ("w_qh", (1024, 512)),
    ("w_kvh", (1024, 1024)), ("w_oh", (512, 1024)), ("gam_h", (128, 8)),
    ("rtq_rep", (128, D)), ("rtkv_rep", (128, D)), ("lngb", (128, 2048)),
    ("w1T", (1024, 4096)), ("w2T", (4096, 1024)), ("cw0r", (128, 96)), ("cb0r", (128, 32)),
    ("w_in_z", (1024, 2048)), ("w_in_fm", (128, 18432)), ("cw1r", (128, 68)), ("cb1r", (128, 17)),
    ("a_log", (32, 1)), ("d_par", (32, 1)), ("dt_b", (32, 1)), ("nrm_w", (128, 2048)),
    ("w_out", (2048, 1024)), ("moe_w1", (8 * 1024, 4096)), ("moe_b1", (8, 4096)),
    ("moe_w2", (8 * 4096, 1024)), ("moe_b2", (8, 1024)), ("w_r", (1024, 8)), ("b_r", (1, 8)),
    ("gamma", (128, D)), ("beta", (128, D)), ("xfull", (T, D)),
    ("iota_r", (128, 1024)), ("cst2c", (128, 768)), ("cstbc", (128, 1024)),
    ("hsel", (32, 2048)),
    ("xloc", (XCOLS, D)), ("maskV", (AT, 4, 256)), ("tokrow", (128, ACOLS)),
    ("gtok", (128, 128)), ("ghi", (128, 128)), ("glo", (128, 128)),
    ("hmask", (128, 2)), ("xzmask", (128, 1)), ("m8", (1, 8)), ("pm8", (128, 8)),
]


def build_program(debug=None):
    nc = bass.Bass("TRN2", target_bir_lowering=False)
    IN = {n: nc.dram_tensor(n, list(sh), F32, kind="ExternalInput").ap() for n, sh in IN_SHAPES}
    out = nc.dram_tensor("out", [S, D], F32, kind="ExternalOutput").ap()
    dbg_aps = {}
    dr = lambda n, sh, dt=F32: nc.dram_tensor(n, list(sh), dt).ap()
    DSC = {
        "y1": dr("d_y1", (ACOLS, D)),
        "z": dr("d_z", (XZCOLS, 2048), BF16),
        "hvt": dr("d_hvt", (1024, 1024), BF16),
        "hq": dr("d_hq", (AT, 128, 32, 128), I8),
        "y2": dr("d_y2", (ACOLS, D)),
        "invs": dr("d_invs", (2305, 1)),
        "Y": dr("d_Y", (S, 2048), BF16),
        "cc": dr("d_cc", (16, 64, 128), BF16),
        "fac": dr("d_fac", (16, 128, 32)),
        "y3": dr("d_y3", (S, D)),
        "ln3F": dr("d_ln3F", (16, 8, 128, 128), BF16),
        "acc": dr("d_acc", (S, D)),
    }
    CC = {
        "sc_in": nc.dram_tensor("cc_sc_in", [1, 2 * S], F32),
        "sc_out": nc.dram_tensor("cc_sc_out", [NC, 2 * S], F32),
        "mb_in": nc.dram_tensor("cc_mb_in", [1, 32 * 64 * 64 + 64], F32),
        "mb_out": nc.dram_tensor("cc_mb_out", [NC, 32 * 64 * 64 + 64], F32),
        "mu_in": nc.dram_tensor("cc_mu_in", [1, D], F32),
        "mu_out": nc.dram_tensor("cc_mu_out", [1, D], F32),
    }
    ctx = contextlib.ExitStack()
    with ctx:
        sb = lambda n, sh, dt=F32: ctx.enter_context(nc.sbuf_tensor(n, list(sh), dt))
        ps = lambda n, sh, dt=F32: ctx.enter_context(nc.psum_tensor(n, list(sh), dt))
        B = {
            "r": [sb(f"r{i}", (128, D)) for i in range(XT)],
            "bq": [sb(f"bq{i}", (128, 2304), BF16) for i in range(4)],
            "bk": [sb(f"bk{i}", (128, 2432), BF16) for i in range(4)],
            "bv": [sb(f"bv{i}", (128, 2560), BF16) for i in range(4)],
            "bx": [sb(f"bx{i}", (128, 2560), BF16) for i in range(2)],
            "wb": [sb(f"wb{i}", (128, 4096), BF16) for i in range(3)],
            "st": sb("st", (128, 512)),
            "st2": sb("st2", (128, 512)),
            "cst": sb("cst", (128, 1024)),
            "cst2": sb("cst2", (128, 768)),
            "cstb": sb("cstb", (128, 1024), BF16),
            "repa": sb("repa", (128, 1024)),
            "repb": sb("repb", (128, 1024)),
            "idx": sb("idx", (128, 32), U32),
            "smal": sb("smal", (128, 256)),
            "pb": [ps(f"pb{i}", (128, 512)) for i in range(6)],
            "pbt": ps("pbt", (128, 512), BF16),
        }
        sem = ctx.enter_context(nc.semaphore("S"))
        with nc.Block() as blk:
            P = Prog()
            EM = Emitter(nc, P, IN, out, DSC, CC, B, debug or [], dbg_aps)
            EM.forward()
            P.emit(blk, sem)
    return nc, dbg_aps


class Emitter:
    def __init__(self, nc, P, IN, out, DSC, CC, B, debug, dbg_aps):
        self.nc, self.P, self.IN, self.out = nc, P, IN, out
        self.DSC, self.CC = DSC, CC
        self.debug, self.dbg_aps = debug, dbg_aps
        for k2, v2 in B.items():
            setattr(self, k2, v2)

    def dump(self, name, src_ap, shape, dt=F32):
        if name in self.debug:
            d = self.nc.dram_tensor("dbg_" + name, list(shape), dt, kind="ExternalOutput").ap()
            self.dbg_aps[name] = d
            self.P.dma("scalar", d, src_ap)

    def v(self, fn): self.P.op("vector", fn)
    def s(self, fn): self.P.op("scalar", fn)
    def t(self, fn): self.P.op("tensor", fn)
    def gp(self, fn): self.P.op("gpsimd", fn)

    def rsqrt(self, dst, src):
        eps = self.smal[:, 255:256]
        self.s(lambda e: e.activation(dst, src, AF.Sqrt, bias=eps))
        self.v(lambda e: e.reciprocal(dst, dst))

    @property
    def ones_row(self): return self.cst2[0:1, 640:768]
    @property
    def ones_col(self): return self.cst2[:, 512:513]
    @property
    def onesb_col(self): return self.cstb[:, 256:257]
    @property
    def onesb_row(self): return self.cstb[0:1, 257:385]
    @property
    def eyeb(self): return self.cstb[:, 128:256]

    def ptot(self, cnt_cols, out_cols, n):
        p = self.pb[5]
        self.t(lambda e: e.matmul(p[0:1, 0:n], self.ones_col, cnt_cols, start=True, stop=True))
        self.v(lambda e: e.tensor_copy(self.smal[0:1, 240:240 + n], p[0:1, 0:n]))
        self.t(lambda e: e.matmul(p[:, 4:4 + n], self.ones_row, self.smal[0:1, 240:240 + n], start=True, stop=True))
        self.v(lambda e: e.tensor_copy(out_cols, p[:, 4:4 + n]))

    def forward(self):
        import os
        lim = int(os.environ.get("KSTAGE", "99"))
        stages = [self.setup, self.stageA_ln_scores, self.stageB_routing,
                  self.stageC_heavy, self.stageD_qkv, self.stageE_light,
                  self.stageF_l0ffn, self.stageG_mamba]
        for i, fn in enumerate(stages):
            if i <= lim:
                fn()
        self.stageH_moe()

    def setup(self):
        P, IN = self.P, self.IN
        P.dmas("sync", [(self.cst[:, :], IN["iota_r"]), (self.cst2[:, :], IN["cst2c"])])
        P.op16("gpsimd", lambda e: e.dma_start(self.cstb[:, :], IN["cstbc"]))
        P.dmas_mq([(self.r[i][:, :], IN["xloc"][128 * i:128 * (i + 1), :]) for i in range(XT)])
        self.v(lambda e: (e.memset(self.smal[:, 255:256], 1e-5),
                          e.memset(self.smal[:, 254:255], 0.0))[-1])

    def stageA_ln_scores(self):
        P, IN = self.P, self.IN
        st, st2 = self.st, self.st2
        P.dmas("sync", [(self.repa[:, :], IN["rtq_rep"]), (self.repb[:, :], IN["rtkv_rep"])])
        scr = self.bq[0][:, 0:2048].bitcast(F32)
        for i in range(2, 18):
            self.v(lambda e, i=i: e.tensor_tensor(scr[:, 0:1024], self.r[i][:, :], self.repa[:, :], op=ALU.mult))
            self.v(lambda e, i=i: e.reduce_sum(st2[:, 128 + i - 2:129 + i - 2], scr[:, 0:1024], axis=AXX))
            self.v(lambda e, i=i: e.tensor_tensor(scr[:, 0:1024], self.r[i][:, :], self.repb[:, :], op=ALU.mult))
            self.v(lambda e, i=i: e.reduce_sum(st2[:, 160 + i - 2:161 + i - 2], scr[:, 0:1024], axis=AXX))
        self.v(lambda e: e.reduce_sum(st2[0:1, 6:7], self.repa[0:1, :], axis=AXX))
        self.v(lambda e: e.reduce_sum(st2[0:1, 7:8], self.repb[0:1, :], axis=AXX))
        p5 = self.pb[5]
        self.t(lambda e: e.matmul(p5[:, 0:2], self.ones_row, st2[0:1, 6:8], start=True, stop=True))
        self.v(lambda e: e.tensor_copy(st2[:, 4:6], p5[:, 0:2]))

        for i in range(XT):
            ri, c0 = self.r[i], i
            m0 = st[:, c0:c0 + 1]
            r0 = st[:, 20 + c0:21 + c0]
            al = st[:, 40 + c0:41 + c0]
            be = st[:, 60 + c0:61 + c0]
            self.v(lambda e, ri=ri, m0=m0: e.reduce_sum(m0, ri[:, :], axis=AXX))
            self.v(lambda e, m0=m0: e.tensor_scalar(m0, m0, 1.0 / D, None, op0=ALU.mult))
            self.v(lambda e, ri=ri, m0=m0: e.tensor_scalar(self.repa[:, :], ri[:, :], m0, None, op0=ALU.subtract))
            self.s(lambda e: e.activation(self.repb[:, :], self.repa[:, :], AF.Square, accum_out=st2[:, 0:1]))
            self.v(lambda e: e.tensor_scalar(st2[:, 0:1], st2[:, 0:1], 1.0 / D, None, op0=ALU.mult))
            self.rsqrt(r0, st2[:, 0:1])
            if 2 <= i < 18:
                for rr, (xc, sc) in enumerate([(128, 64), (160, 96)]):
                    xd = st2[:, xc + i - 2:xc + 1 + i - 2]
                    so = st2[:, sc + i - 2:sc + 1 + i - 2]
                    self.v(lambda e, m0=m0, rr=rr: e.tensor_tensor(st2[:, 1:2], m0, st2[:, 4 + rr:5 + rr], op=ALU.mult))
                    self.v(lambda e, xd=xd: e.tensor_tensor(st2[:, 1:2], xd, st2[:, 1:2], op=ALU.subtract))
                    self.v(lambda e, so=so, r0=r0: e.tensor_tensor(so, st2[:, 1:2], r0, op=ALU.mult))
            self.v(lambda e, r0=r0: e.tensor_scalar(self.repa[:, :], self.repa[:, :], r0, None, op0=ALU.mult))
            self.v(lambda e: e.reduce_sum(st2[:, 1:2], self.repa[:, :], axis=AXX))
            self.v(lambda e: e.tensor_scalar(st2[:, 1:2], st2[:, 1:2], 1.0 / D, None, op0=ALU.mult))
            self.v(lambda e: e.tensor_scalar(self.repb[:, :], self.repa[:, :], st2[:, 1:2], None, op0=ALU.subtract))
            self.s(lambda e: e.activation(self.repb[:, :], self.repb[:, :], AF.Square, accum_out=st2[:, 2:3]))
            self.v(lambda e: e.tensor_scalar(st2[:, 2:3], st2[:, 2:3], 1.0 / D, None, op0=ALU.mult))
            self.rsqrt(st2[:, 3:4], st2[:, 2:3])
            self.v(lambda e, r0=r0, al=al: e.tensor_tensor(al, r0, st2[:, 3:4], op=ALU.mult))
            self.v(lambda e, m0=m0, r0=r0, be=be: e.tensor_tensor(be, m0, r0, op=ALU.mult))
            self.v(lambda e, be=be: e.tensor_tensor(be, be, st2[:, 1:2], op=ALU.add))
            self.v(lambda e, be=be: e.tensor_tensor(be, be, st2[:, 3:4], op=ALU.mult))
            self.v(lambda e, be=be: e.tensor_scalar(be, be, -1.0, None, op0=ALU.mult))
        self.dump("st", st[:, :], (128, 512))
        self.dump("st2", st2[:, :], (128, 512))
        self.dump("scq", st2[:, 64:80], (128, 16))
        self.dump("sckv", st2[:, 96:112], (128, 16))

    def stageB_routing(self):
        P, IN, CC = self.P, self.IN, self.CC
        st2, smal, idx = self.st2, self.smal, self.idx
        P.dma("sync", CC["sc_in"].ap()[0:1, 0:2048].rearrange("a (p c) -> (a p) c", p=128), st2[:, 64:80])
        P.dma("sync", CC["sc_in"].ap()[0:1, 2048:4096].rearrange("a (p c) -> (a p) c", p=128), st2[:, 96:112])
        self.gp(lambda e: e.collective_compute(
            "AllGather", ALU.bypass, replica_groups=[list(range(NC))],
            ins=[CC["sc_in"].ap().opt()], outs=[CC["sc_out"].ap().opt()]))
        P.dmas("sync", [(self.repa[16 * c:16 * (c + 1), 0:128],
                         CC["sc_out"].ap()[c:c + 1, 0:2048].rearrange("a (p c) -> (a p) c", p=16))
                        for c in range(NC)])
        P.dmas("sync", [(self.repb[16 * c:16 * (c + 1), 0:128],
                         CC["sc_out"].ap()[c:c + 1, 2048:4096].rearrange("a (p c) -> (a p) c", p=16))
                        for c in range(NC)])
        self.dump("scall", self.repa[:, 0:128], (128, 128))
        def binit(e):
            e.memset(smal[:, 0:1], -4096.0)
            e.memset(smal[:, 1:2], 4096.0)
            e.memset(smal[:, 2:3], -4096.0)
            return e.memset(smal[:, 3:4], 4096.0)
        self.v(binit)
        scr = self.bq[1][:, 0:512].bitcast(F32)
        for it in range(40):
            def half1(e):
                e.tensor_tensor(smal[:, 4:5], smal[:, 0:1], smal[:, 1:2], op=ALU.add)
                return e.tensor_tensor(smal[:, 5:6], smal[:, 2:3], smal[:, 3:4], op=ALU.add)
            self.v(half1)
            def half2(e):
                e.tensor_scalar(smal[:, 4:5], smal[:, 4:5], 0.5, None, op0=ALU.mult)
                return e.tensor_scalar(smal[:, 5:6], smal[:, 5:6], 0.5, None, op0=ALU.mult)
            self.v(half2)
            def cmps(e):
                e.tensor_scalar(scr[:, 0:128], self.repa[:, 0:128], smal[:, 4:5], None, op0=ALU.is_gt)
                return e.tensor_scalar(scr[:, 128:256], self.repb[:, 0:128], smal[:, 5:6], None, op0=ALU.is_gt)
            self.v(cmps)
            def reds(e):
                e.reduce_sum(smal[:, 8:9], scr[:, 0:128], axis=AXX)
                return e.reduce_sum(smal[:, 9:10], scr[:, 128:256], axis=AXX)
            self.v(reds)
            self.ptot(smal[:, 8:10], smal[:, 10:12], 2)
            def ges(e):
                e.tensor_scalar(smal[:, 12:13], smal[:, 10:11], 1024.0, None, op0=ALU.is_ge)
                return e.tensor_scalar(smal[:, 13:14], smal[:, 11:12], 1024.0, None, op0=ALU.is_ge)
            self.v(ges)
            def d1(e):
                e.tensor_tensor(smal[:, 14:15], smal[:, 4:5], smal[:, 0:1], op=ALU.subtract)
                e.tensor_tensor(smal[:, 15:16], smal[:, 1:2], smal[:, 4:5], op=ALU.subtract)
                e.tensor_tensor(smal[:, 16:17], smal[:, 5:6], smal[:, 2:3], op=ALU.subtract)
                return e.tensor_tensor(smal[:, 17:18], smal[:, 3:4], smal[:, 5:6], op=ALU.subtract)
            self.v(d1)
            def d2(e):
                e.tensor_tensor(smal[:, 14:15], smal[:, 14:15], smal[:, 12:13], op=ALU.mult)
                e.tensor_tensor(smal[:, 15:16], smal[:, 15:16], smal[:, 12:13], op=ALU.mult)
                e.tensor_tensor(smal[:, 16:17], smal[:, 16:17], smal[:, 13:14], op=ALU.mult)
                return e.tensor_tensor(smal[:, 17:18], smal[:, 17:18], smal[:, 13:14], op=ALU.mult)
            self.v(d2)
            def d3(e):
                e.tensor_tensor(smal[:, 0:1], smal[:, 0:1], smal[:, 14:15], op=ALU.add)
                e.tensor_tensor(smal[:, 1:2], smal[:, 4:5], smal[:, 15:16], op=ALU.add)
                e.tensor_tensor(smal[:, 2:3], smal[:, 2:3], smal[:, 16:17], op=ALU.add)
                return e.tensor_tensor(smal[:, 3:4], smal[:, 5:6], smal[:, 17:18], op=ALU.add)
            self.v(d3)
        self.dump("thr", smal[:, 0:4], (128, 4))
        self.compact(self.repa, smal[:, 0:1], 0, st2[:, 192:200])
        self.compact(self.repb, smal[:, 2:3], 8, None)
        self.dump("idx", idx[:, 0:16], (128, 16), U32)

    def compact(self, sc, locol, idx_off, valcols):
        P, IN = self.P, self.IN
        smal, st2, idx = self.smal, self.st2, self.idx
        mk = self.bq[2][:, 0:512].bitcast(F32)
        pf = self.bq[2][:, 512:1024].bitcast(F32)
        gr = self.bq[3][:, 0:512].bitcast(F32)
        ghib = self.bq[3][:, 512:768]
        glob_ = self.bq[3][:, 768:1024]
        sfb = self.bq[1][:, 0:2048]
        P.op16("gpsimd", lambda e: e.dma_start(ghib[:, 0:128], IN["ghi"]))
        P.op16("gpsimd", lambda e: e.dma_start(glob_[:, 0:128], IN["glo"]))
        self.v(lambda e: e.tensor_scalar(mk[:, 0:128], sc[:, 0:128], locol, None, op0=ALU.is_gt))
        self.v(lambda e: e.tensor_copy(pf[:, 0:128], mk[:, 0:128]))
        for sh in [1, 2, 4, 8, 16, 32, 64]:
            self.v(lambda e, sh=sh: e.tensor_tensor(pf[:, sh:128], pf[:, sh:128], pf[:, 0:128 - sh], op=ALU.add))
        p5 = self.pb[5]
        self.t(lambda e: e.matmul(p5[:, 8:9], self.cst2[:, 0:128], pf[:, 127:128], start=True, stop=True))
        self.v(lambda e: e.tensor_copy(smal[:, 16:17], p5[:, 8:9]))
        self.v(lambda e: e.tensor_tensor(gr[:, 0:128], pf[:, 0:128], mk[:, 0:128], op=ALU.subtract))
        self.v(lambda e: e.tensor_scalar(gr[:, 0:128], gr[:, 0:128], smal[:, 16:17], None, op0=ALU.add))
        self.v(lambda e: e.tensor_tensor(gr[:, 0:128], gr[:, 0:128], mk[:, 0:128], op=ALU.mult))
        self.v(lambda e: e.tensor_scalar(mk[:, 0:128], mk[:, 0:128], 1.0, None, op0=ALU.subtract))
        self.v(lambda e: e.tensor_tensor(gr[:, 0:128], gr[:, 0:128], mk[:, 0:128], op=ALU.add))
        for f in range(128):
            self.v(lambda e, f=f: e.tensor_scalar(sfb[:, 0:1024], self.cst[:, :], gr[:, f:f + 1], None, op0=ALU.is_equal))
            def mms(e, f=f):
                st_ = (f == 0)
                e.matmul(self.pb[0][0:1, 0:512], ghib[:, f:f + 1], sfb[:, 0:512], start=st_, stop=(f == 127))
                e.matmul(self.pb[1][0:1, 0:512], ghib[:, f:f + 1], sfb[:, 512:1024], start=st_, stop=(f == 127))
                e.matmul(self.pb[2][0:1, 0:512], glob_[:, f:f + 1], sfb[:, 0:512], start=st_, stop=(f == 127))
                return e.matmul(self.pb[3][0:1, 0:512], glob_[:, f:f + 1], sfb[:, 512:1024], start=st_, stop=(f == 127))
            self.t(mms)
        def comb1(e):
            e.tensor_scalar(self.repa[0:1, 0:512], self.pb[0][0:1, 0:512], 128.0, None, op0=ALU.mult)
            return e.tensor_scalar(self.repa[0:1, 512:1024], self.pb[1][0:1, 0:512], 128.0, None, op0=ALU.mult)
        self.v(comb1)
        def comb2(e):
            e.tensor_tensor(self.repa[0:1, 0:512], self.repa[0:1, 0:512], self.pb[2][0:1, 0:512], op=ALU.add)
            return e.tensor_tensor(self.repa[0:1, 512:1024], self.repa[0:1, 512:1024], self.pb[3][0:1, 0:512], op=ALU.add)
        self.v(comb2)
        one11 = self.cst2[0:1, 512:513]
        for j in range(8):
            self.t(lambda e, j=j: e.matmul(p5[:, 20 + j:21 + j], self.repa[0:1, 128 * j:128 * (j + 1)], one11, start=True, stop=True))
        def tocols(e):
            e.tensor_copy(idx[:, idx_off:idx_off + 8], p5[:, 20:28])
            if valcols is not None:
                e.tensor_copy(valcols, p5[:, 20:28])
            return None
        self.v(tocols)

    def stageC_heavy(self):
        P, IN = self.P, self.IN
        st2, smal, idx = self.st2, self.smal, self.idx
        P.dma("sync", smal[:, 16:24], IN["gam_h"])
        zero = smal[:, 254:255]

        def gather_side(idx_off, dst_bufs):
            for j in range(8):
                P.op16("gpsimd", lambda e, j=j: e.indirect_dma_start(
                    dst_bufs[j // 2][:, (j % 2) * 1024:(j % 2) * 1024 + 1024], None,
                    IN["xfull"], IndirectOffsetOnAxis(ap=idx[:, idx_off + j:idx_off + j + 1], axis=0)))

        def u_side(bufs):
            junk = self.repb[:, :].bitcast(BF16)[:, 0:1024]
            for j in range(8):
                sl = bufs[j // 2][:, (j % 2) * 1024:(j % 2) * 1024 + 1024]
                self.v(lambda e, sl=sl: e.reduce_sum(smal[:, 32:33], sl, axis=AXX))
                self.v(lambda e: e.tensor_scalar(smal[:, 32:33], smal[:, 32:33], 1.0 / D, None, op0=ALU.mult))
                self.s(lambda e, sl=sl: e.activation(junk, sl, AF.Square, accum_out=smal[:, 33:34]))
                def var1(e):
                    e.tensor_scalar(smal[:, 33:34], smal[:, 33:34], 1.0 / D, None, op0=ALU.mult)
                    return e.tensor_tensor(smal[:, 34:35], smal[:, 32:33], smal[:, 32:33], op=ALU.mult)
                self.v(var1)
                self.v(lambda e: e.tensor_tensor(smal[:, 33:34], smal[:, 33:34], smal[:, 34:35], op=ALU.subtract))
                self.s(lambda e: e.activation(smal[:, 33:34], smal[:, 33:34], AF.Sqrt, bias=zero))
                self.v(lambda e: e.reciprocal(smal[:, 33:34], smal[:, 33:34]))
                self.v(lambda e, sl=sl: e.tensor_scalar(sl, sl, smal[:, 32:33], smal[:, 33:34], op0=ALU.subtract, op1=ALU.mult))

        def transpose_side(src_bufs):
            for kk in range(8):
                for grp in range(2):
                    def trs(e, kk=kk, grp=grp):
                        last = None
                        for jj in range(4):
                            j = 4 * grp + jj
                            src = src_bufs[j // 2][:, (j % 2) * 1024 + 128 * kk:(j % 2) * 1024 + 128 * kk + 128]
                            last = e.transpose(self.pbt[:, 128 * jj:128 * jj + 128], src, self.eyeb)
                        return last
                    self.t(trs)
                    dst = self.bk[kk // 2][:, (kk % 2) * 1024 + 512 * grp:(kk % 2) * 1024 + 512 * grp + 512]
                    self.v(lambda e, dst=dst: e.tensor_copy(dst, self.pbt[:, 0:512]))

        gather_side(0, self.bq)
        u_side(self.bq)
        transpose_side(self.bq)
        P.op16("gpsimd", lambda e: e.dma_start(
            self.wb[0][:, :].rearrange("p (k n) -> p k n", k=8),
            IN["w_qh"].rearrange("(k p) n -> p k n", p=128)))
        def gscale0(e):
            last = None
            for kk in range(8):
                last = e.tensor_scalar(self.wb[0][:, 512 * kk:512 * kk + 512],
                                       self.wb[0][:, 512 * kk:512 * kk + 512],
                                       smal[:, 16 + kk:17 + kk], None, op0=ALU.mult)
            return last
        self.v(gscale0)
        for m4 in range(4):
            for nh in range(2):
                def mms(e, m4=m4, nh=nh):
                    last = None
                    for kk in range(8):
                        last = e.matmul(self.pb[0][:, 0:512],
                                        self.wb[0][:, 512 * kk + 128 * m4:512 * kk + 128 * m4 + 128],
                                        self.bk[kk // 2][:, (kk % 2) * 1024 + 512 * nh:(kk % 2) * 1024 + 512 * nh + 512],
                                        start=(kk == 0), stop=(kk == 7))
                    return last
                self.t(mms)
                dst = self.bx[m4 // 2][:, (m4 % 2) * 1024 + 512 * nh:(m4 % 2) * 1024 + 512 * nh + 512]
                self.v(lambda e, dst=dst: e.tensor_scalar(dst, self.pb[0][:, 0:512], 0.125, None, op0=ALU.mult))
        gather_side(8, self.bq)
        u_side(self.bq)
        transpose_side(self.bq)
        P.op16("gpsimd", lambda e: e.dma_start(
            self.wb[1][:, :].rearrange("p (k n) -> p k n", k=4),
            IN["w_kvh"][0:512, :].rearrange("(k p) n -> p k n", p=128)))
        P.op16("gpsimd", lambda e: e.dma_start(
            self.wb[2][:, :].rearrange("p (k n) -> p k n", k=4),
            IN["w_kvh"][512:1024, :].rearrange("(k p) n -> p k n", p=128)))
        def gscale12(e):
            last = None
            for kk in range(8):
                w = self.wb[1 + kk // 4][:, 1024 * (kk % 4):1024 * (kk % 4) + 1024]
                last = e.tensor_scalar(w, w, smal[:, 16 + kk:17 + kk], None, op0=ALU.mult)
            return last
        self.v(gscale12)
        for m4 in range(4):
            for nh in range(2):
                def mmk(e, m4=m4, nh=nh):
                    last = None
                    for kk in range(8):
                        w = self.wb[1 + kk // 4][:, 1024 * (kk % 4) + 128 * m4:1024 * (kk % 4) + 128 * m4 + 128]
                        last = e.matmul(self.pb[0][:, 0:512], w,
                                        self.bk[kk // 2][:, (kk % 2) * 1024 + 512 * nh:(kk % 2) * 1024 + 512 * nh + 512],
                                        start=(kk == 0), stop=(kk == 7))
                    return last
                self.t(mmk)
                dst = self.bq[m4 // 2][:, (m4 % 2) * 1024 + 512 * nh:(m4 % 2) * 1024 + 512 * nh + 512]
                self.v(lambda e, dst=dst: e.tensor_copy(dst, self.pb[0][:, 0:512]))
        for tt in range(8):
            def mmv(e, tt=tt):
                last = None
                for kk in range(8):
                    lhs = self.bk[kk // 2][:, (kk % 2) * 1024 + 128 * tt:(kk % 2) * 1024 + 128 * tt + 128]
                    w = self.wb[1 + kk // 4][:, 1024 * (kk % 4) + 512:1024 * (kk % 4) + 1024]
                    last = e.matmul(self.pb[0][:, 0:512], lhs, w, start=(kk == 0), stop=(kk == 7))
                return last
            self.t(mmv)
            dst = self.bq[2 + tt // 4][:, (tt % 4) * 512:(tt % 4) * 512 + 512]
            self.v(lambda e, dst=dst: e.tensor_copy(dst, self.pb[0][:, 0:512]))
        self.dump("hk", self.bq[0][:, 0:2048], (128, 2048), BF16)
        self.dump("hq", self.bx[0][:, 0:2048], (128, 2048), BF16)
        self._heavy_heads()
        P.op16("gpsimd", lambda e: e.dma_start(
            self.wb[2][:, :].rearrange("p (k n) -> p k n", k=4),
            IN["w_oh"].rearrange("(k p) n -> p k n", p=128)))
        stg = self.repb[:, :].bitcast(BF16)[:, 0:1024]
        for qm in range(8):
            def mmo(e, qm=qm):
                last = None
                for kt in range(4):
                    for nh in range(2):
                        last = e.matmul(self.pb[nh][:, 0:512],
                                        self.wb[1][:, 1024 * kt + 128 * qm:1024 * kt + 128 * qm + 128],
                                        self.wb[2][:, 1024 * kt + 512 * nh:1024 * kt + 512 * nh + 512],
                                        start=(kt == 0), stop=(kt == 3))
                return last
            self.t(mmo)
            def evo(e):
                e.tensor_copy(stg[:, 0:512], self.pb[0][:, 0:512])
                return e.tensor_copy(stg[:, 512:1024], self.pb[1][:, 0:512])
            self.v(evo)
            P.dma("sync", self.DSC["hvt"][128 * qm:128 * qm + 128, :], stg)
        self.dump("hvt_s", stg, (128, 1024), BF16)

    def _heavy_heads(self):
        for h in range(8):
            m4 = h // 2
            po = (h % 2) * 64
            for mt in range(8):
                def mmsim(e, h=h, mt=mt, m4=m4, po=po):
                    last = None
                    for nh in range(2):
                        lhs = self.bq[m4 // 2][po:po + 64, (m4 % 2) * 1024 + 128 * mt:(m4 % 2) * 1024 + 128 * mt + 128]
                        rhs = self.bx[m4 // 2][po:po + 64, (m4 % 2) * 1024 + 512 * nh:(m4 % 2) * 1024 + 512 * nh + 512]
                        last = e.matmul(self.pb[nh][:, 0:512], lhs, rhs, start=True, stop=True)
                    return last
                self.t(mmsim)
                def expev(e, mt=mt):
                    dst = self.bv[mt // 2][:, (mt % 2) * 1024:(mt % 2) * 1024 + 1024]
                    e.activation(dst[:, 0:512], self.pb[0][:, 0:512], AF.Exp)
                    return e.activation(dst[:, 512:1024], self.pb[1][:, 0:512], AF.Exp)
                self.s(expev)
            def mmden(e, h=h):
                last = None
                for mt in range(8):
                    src = self.bv[mt // 2][:, (mt % 2) * 1024:(mt % 2) * 1024 + 1024]
                    last = e.matmul(self.pb[2][0:1, 0:512], self.onesb_col, src[:, 0:512],
                                    start=(mt == 0), stop=(mt == 7))
                    last = e.matmul(self.pb[3][0:1, 0:512], self.onesb_col, src[:, 512:1024],
                                    start=(mt == 0), stop=(mt == 7))
                return last
            self.t(mmden)
            def recip0(e):
                e.tensor_copy(self.repa[0:1, 0:512], self.pb[2][0:1, 0:512])
                return e.tensor_copy(self.repa[0:1, 512:1024], self.pb[3][0:1, 0:512])
            self.v(recip0)
            self.v(lambda e: e.tensor_scalar(self.repa[0:1, 0:1024], self.repa[0:1, 0:1024], 1e-30, None, op0=ALU.max))
            self.v(lambda e: e.reciprocal(self.repa[0:1, 0:1024], self.repa[0:1, 0:1024]))
            for nh in range(2):
                def mmho(e, h=h, nh=nh):
                    for kt in range(8):
                        lhs = self.bq[2 + kt // 4][:, (kt % 4) * 512 + 64 * h:(kt % 4) * 512 + 64 * h + 64]
                        rhs = self.bv[kt // 2][:, (kt % 2) * 1024 + 512 * nh:(kt % 2) * 1024 + 512 * nh + 512]
                        e.matmul(self.pb[0][0:64, 0:512], lhs, rhs, start=(kt == 0), stop=(kt == 7))
                    return e.matmul(self.pb[1][:, 0:512], self.ones_row,
                                    self.repa[0:1, 512 * nh:512 * nh + 512], start=True, stop=True)
                self.t(mmho)
                self.v(lambda e: e.tensor_copy(self.repb[:, 0:512], self.pb[1][:, 0:512]))
                def evho(e, h=h, nh=nh):
                    dst = self.wb[1][(h % 2) * 64:(h % 2) * 64 + 64,
                                     1024 * (h // 2) + 512 * nh:1024 * (h // 2) + 512 * nh + 512]
                    return e.tensor_tensor(dst, self.pb[0][0:64, 0:512], self.repb[0:64, 0:512], op=ALU.mult)
                self.v(evho)

    def stageD_qkv(self):
        P, IN = self.P, self.IN
        st = self.st
        for wi, c0 in ((0, 0), (1, 512), (2, 1024)):
            P.op16("gpsimd", lambda e, wi=wi, c0=c0: e.dma_start(
                self.wb[wi][:, :].rearrange("p (k n) -> p k n", k=8),
                IN["w_qkv"][:, c0:c0 + 512].rearrange("(k p) n -> p k n", p=128)))
        P.dmas("sync", [(self.repa[:, :], IN["lngb"][:, 0:1024]),
                        (self.repb[:, :], IN["lngb"][:, 1024:2048])])
        stg32 = None
        for c in range(10):
            fmb = self.bx[c % 2]
            for half in range(2):
                i = 2 * c + half
                al = st[:, 40 + i:41 + i]
                be = st[:, 60 + i:61 + i]
                def xl(e, i=i, al=al, be=be, half=half, fmb=fmb):
                    last = None
                    for q4 in range(2):
                        stg = (self.bx[0] if q4 == 0 else self.bx[1])[:, 2048:2560]
                        sl = slice(512 * q4, 512 * q4 + 512)
                        e.tensor_scalar(stg, self.r[i][:, sl], al, be, op0=ALU.mult, op1=ALU.add)
                        e.tensor_tensor(stg, stg, self.repa[:, sl], op=ALU.mult)
                        last = e.tensor_tensor(stg, stg, self.repb[:, sl], op=ALU.add)
                    return last
                self.v(xl)
                for grp in range(2):
                    def trs(e, grp=grp):
                        last = None
                        for q in range(4):
                            kk = 4 * grp + q
                            stg = (self.bx[0] if kk < 4 else self.bx[1])[:, 2048 + 128 * (kk % 4):2048 + 128 * (kk % 4) + 128]
                            last = e.transpose(self.pbt[:, 128 * q:128 * q + 128], stg, self.eyeb)
                        return last
                    self.t(trs)
                    self.v(lambda e, grp=grp, half=half, fmb=fmb: e.tensor_copy(
                        fmb[:, 1024 * grp + 128 * half:1024 * grp + 128 * half + 1024]
                        .rearrange("p (k n) -> p k n", n=128)[:, ::2, :],
                        self.pbt[:, 0:512].rearrange("p (k n) -> p k n", n=128)))
            self._qkv_chunk(c, fmb)
        def qs(e):
            last = None
            for m4 in range(4):
                last = e.tensor_scalar(self.bq[m4][:, :], self.bq[m4][:, :], 0.125, None, op0=ALU.mult)
            return last
        self.v(qs)
        self.dump("qfm", self.bq[0][:, :], (128, 2304), BF16)
        self.dump("vsh9", self.bv[1][:, 2048:2560], (128, 512), BF16)
        self.dump("vsh8", self.bv[1][:, 1536:2048], (128, 512), BF16)
        self.dump("kfm", self.bk[0][:, :], (128, 2432), BF16)
        self.dump("vsh", self.bv[0][:, :], (128, 2560), BF16)

    def _qkv_chunk(self, c, fmb):
        for dst_bufs, wbi, off, width in ((self.bq, 0, 128, 2304), (self.bk, 1, 64, 2432)):
            xlo = 256 * c
            lo = max(xlo, off)
            hi = min(xlo + 256, off + width)
            if lo >= hi:
                continue
            for m4 in range(4):
                def mmq(e, m4=m4, fmb=fmb, wbi=wbi, lo=lo, hi=hi, xlo=xlo):
                    last = None
                    for kk in range(8):
                        last = e.matmul(self.pb[0][:, 0:256],
                                        self.wb[wbi][:, 512 * kk + 128 * m4:512 * kk + 128 * m4 + 128],
                                        fmb[:, 256 * kk:256 * kk + 256], start=(kk == 0), stop=(kk == 7))
                    return last
                self.t(mmq)
                self.v(lambda e, m4=m4, dst_bufs=dst_bufs, lo=lo, hi=hi, xlo=xlo, off=off:
                       e.tensor_copy(dst_bufs[m4][:, lo - off:hi - off],
                                     self.pb[0][:, lo - xlo:hi - xlo]))
        for u in (2 * c - 1, 2 * c):
            if u < 0 or u >= 19:
                continue
            prev = self.bx[(c - 1) % 2]
            cur = fmb
            def mmv(e, u=u, c=c, prev=prev, cur=cur):
                last = None
                for kk in range(8):
                    w = self.wb[2][:, 512 * kk:512 * kk + 512]
                    if u == 2 * c:
                        lhs = cur[:, 256 * kk + 64:256 * kk + 64 + 128]
                        last = e.matmul(self.pb[1][:, 0:512], lhs, w, start=(kk == 0), stop=(kk == 7))
                    else:
                        lhs0 = prev[:, 256 * kk + 192:256 * kk + 256]
                        lhs1 = cur[:, 256 * kk:256 * kk + 64]
                        e.matmul(self.pb[1][0:64, 0:512], lhs0, w, start=(kk == 0), stop=False)
                        last = e.matmul(self.pb[2][0:64, 0:512], lhs1, w, start=(kk == 0), stop=(kk == 7))
                return last
            self.t(mmv)
            if u % 2 == 0:
                self.v(lambda e, u=u: e.tensor_copy(
                    self.bv[u // 5][:, 512 * (u % 5):512 * (u % 5) + 512], self.pb[1][:, 0:512]))
            else:
                def evodd(e, u=u):
                    e.tensor_copy(self.bv[u // 5][0:64, 512 * (u % 5):512 * (u % 5) + 512],
                                  self.pb[1][0:64, 0:512])
                    return e.tensor_copy(self.bx[0][0:64, 2048:2560], self.pb[2][0:64, 0:512])
                self.v(evodd)
                self.P.dma("sync", self.bv[u // 5][64:128, 512 * (u % 5):512 * (u % 5) + 512],
                           self.bx[0][0:64, 2048:2560])

    def stageE_light(self):
        import os
        P, IN = self.P, self.IN
        ntiles = int(os.environ.get("KE_TILES", str(AT)))
        parts = os.environ.get("KE_PARTS", "mask,tok,pt,heads,light,scat,evac").split(",")
        self.ke_parts = parts
        P.op16("gpsimd", lambda e: e.dma_start(
            self.wb[0][:, :].rearrange("p (k n) -> p k n", k=4),
            IN["w_ol"].rearrange("(k p) n -> p k n", p=128)))
        for half in range(2):
            P.op16("gpsimd", lambda e, half=half: e.dma_start(
                self.wb[1][:, :].rearrange("p (k n) -> p k n", k=4),
                self.DSC["hvt"][512 * half:512 * half + 512, :].rearrange("(k p) n -> p k n", p=128)))
            for t in range(ntiles):
                self._light_tile(t, half)
        if "y1" in self.debug:
            d = self.nc.dram_tensor("dbg_y1", [ACOLS, D], F32, kind="ExternalOutput").ap()
            self.dbg_aps["y1"] = d
            self.P.dmas("scalar", [(d[128 * t:128 * t + 128, :], self.r[t + 1][:, :]) for t in range(AT)])

    def _light_tile(self, t, half):
        P, IN = self.P, self.IN
        parts = self.ke_parts
        mvb = self.bx[0][0:4, 0:256]
        if "mask" in parts:
            P.op16("gpsimd", lambda e, t=t: e.dma_start(mvb, IN["maskV"][t:t + 1, :, :].rearrange("a m n -> (a m) n")))
        if "tok" in parts:
            P.dma("sync", self.repb[:, 0:128], IN["tokrow"][:, 128 * t:128 * t + 128])
        if "pt" not in parts:
            return
        def ptiles(e, t=t, half=half):
            last = None
            for q in range(4):
                qt = 4 * half + q
                last = e.tensor_scalar(self.bx[1][:, 512 + 128 * q:512 + 128 * q + 128],
                                       self.repb[:, 0:128], self.st2[:, 192 + qt:193 + qt],
                                       None, op0=ALU.is_equal)
            return last
        self.v(ptiles)
        if "heads" not in parts:
            return
        if half == 0:
            for h in range(8):
                if "light" in parts:
                    self._light_head(t, h)
            if t == 8:
                self.dump("lit8", self.bx[1][:, 0:512], (128, 512), BF16)
            def mmlo(e, t=t):
                last = None
                for kk4 in range(4):
                    lhs = self.bx[1][(0):(128), 128 * kk4:128 * kk4 + 128]
                    for nh in range(2):
                        last = e.matmul(self.pb[4 + nh][:, 0:512], lhs,
                                        self.wb[0][:, 1024 * kk4 + 512 * nh:1024 * kk4 + 512 * nh + 512],
                                        start=(kk4 == 0), stop=False)
                return last
            self.t(mmlo)
        if "scat" not in parts:
            return
        def mmsc(e, t=t, half=half):
            last = None
            for q in range(4):
                lhs = self.bx[1][:, 512 + 128 * q:512 + 128 * q + 128]
                for nh in range(2):
                    last = e.matmul(self.pb[4 + nh][:, 0:512], lhs,
                                    self.wb[1][:, 1024 * q + 512 * nh:1024 * q + 512 * nh + 512],
                                    start=(half == 1 and q == 0), stop=(q == 3))
            return last
        self.t(mmsc)
        def evac(e, t=t):
            rt = self.r[t + 1]
            e.tensor_tensor(rt[:, 0:512], self.pb[4][:, 0:512], rt[:, 0:512], op=ALU.add)
            return e.tensor_tensor(rt[:, 512:1024], self.pb[5][:, 0:512], rt[:, 512:1024], op=ALU.add)
        self.v(evac)

    def _light_head(self, t, h):
        m4 = h // 2
        po = (h % 2) * 64
        def mmsim(e, t=t, h=h, m4=m4, po=po):
            qw = self.bq[m4][po:po + 64, 128 * t:128 * t + 128]
            last = None
            for kc in range(2):
                kw = self.bk[m4][po:po + 64, 128 * t + 128 * kc:128 * t + 128 * kc + 128]
                e.matmul(self.pb[2][:, 128 * kc:128 * kc + 128], kw, qw,
                         start=(kc == 0), stop=False)
                last = e.matmul(self.pb[2][:, 128 * kc:128 * kc + 128],
                                self.bx[0][0:4, 128 * kc:128 * kc + 128],
                                self.cstb[0:4, 0:128], start=False, stop=(kc == 1))
            return last
        self.t(mmsim)
        ex = self.bx[0][:, 512:768]
        self.s(lambda e: e.activation(ex, self.pb[2][:, 0:256], AF.Exp))
        if t == 8 and h == 0:
            self.dump("ext8", ex, (128, 256), BF16)
        def mmden(e):
            e.matmul(self.pb[3][0:1, 0:128], self.onesb_col, ex[:, 0:128], start=True, stop=False)
            return e.matmul(self.pb[3][0:1, 0:128], self.onesb_col, ex[:, 128:256], start=False, stop=True)
        self.t(mmden)
        self.v(lambda e: e.tensor_scalar(self.smal[0:1, 32:160], self.pb[3][0:1, 0:128], 1e-30, None, op0=ALU.max))
        self.v(lambda e: e.reciprocal(self.smal[0:1, 32:160], self.smal[0:1, 32:160]))
        self.t(lambda e: e.matmul(self.pb[2][:, 0:128], self.ones_row, self.smal[0:1, 32:160], start=True, stop=True))
        def mmav(e, t=t, h=h):
            last = None
            for kc in range(2):
                u = t + kc
                lhs = self.bv[u // 5][:, 512 * (u % 5) + 64 * h:512 * (u % 5) + 64 * h + 64]
                last = e.matmul(self.pb[3][0:64, 0:128], lhs, ex[:, 128 * kc:128 * kc + 128],
                                start=(kc == 0), stop=(kc == 1))
            return last
        self.t(mmav)
        if t == 8 and h == 0:
            self.v(lambda e: e.tensor_copy(self.repb[0:1, 384:512], self.smal[0:1, 32:160]))
            self.dump("rec8", self.repb[0:1, 384:512], (1, 128))
            self.v(lambda e: e.tensor_copy(self.repb[:, 384:512], self.pb[3][:, 0:128]))
            self.dump("av8", self.repb[:, 384:512], (128, 128))
        self.v(lambda e: e.tensor_copy(self.repb[:, 128:256], self.pb[2][:, 0:128]))
        def ev(e, h=h, po=po, m4=m4):
            dst = self.bx[1][po:po + 64, 128 * m4:128 * m4 + 128]
            return e.tensor_tensor(dst, self.pb[3][0:64, 0:128], self.repb[0:64, 128:256], op=ALU.mult)
        self.v(ev)

    def stageF_l0ffn(self):
        P, IN = self.P, self.IN
        smal = self.smal
        P.dmas_mq([(self.DSC["y1"][128 * t:128 * t + 128, :], self.r[t + 1][:, :]) for t in range(AT)])
        self.quant_weights(IN["w1T"], 8, smal[:, 43:44], smal[:, 44:45])
        self.dump("w1q0", self.r[0][:, :].bitcast(BF16), (128, 2048), BF16)
        self.dump("sw1", smal[:, 43:45], (128, 2), F32)
        P.dmas("sync", [(smal[:, 64:160], IN["cw0r"]), (smal[:, 160:192], IN["cb0r"])])
        def diags(e):
            last = None
            for f in range(32):
                for tap in range(3):
                    last = e.tensor_scalar(self._diag(f, tap), self.eyeb,
                                           smal[:, 64 + 3 * f + tap:65 + 3 * f + tap], None, op0=ALU.mult)
            return last
        self.v(diags)
        def zinit(e):
            for wbb in (self.wb[0], self.wb[1]):
                e.memset(wbb[:, 0:2080].rearrange("p (f n) -> p f n", n=130)[:, :, 0:2], 0.0)
            return e.memset(smal[0:1, 49:50], 0.0)
        self.v(zinit)
        P.dma("sync", self.DSC["invs"][2304:2305, :], smal[0:1, 49:50])
        for c in range(AT):
            self._ffn_chunk(c)
        self.quant_weights(IN["w2T"], 32, smal[:, 45:46], smal[:, 46:47])
        for t in range(AT):
            self._bit2_tile(t)
        self.dump("y1full", self.DSC["y1"], (ACOLS, D))
        self.dump("y2full", self.DSC["y2"], (ACOLS, D))

    def _diag(self, f, tap):
        j = 3 * f + tap
        if j < 68:
            return self.bq[j // 17][:, 128 + 128 * (j % 17):256 + 128 * (j % 17)]
        j -= 68
        return self.bk[j // 18][:, 128 + 128 * (j % 18):256 + 128 * (j % 18)]

    def quant_weights(self, wap, ktiles, sw_col, invsw_col):
        P, smal = self.P, self.smal
        N = 4096 if ktiles == 8 else 1024
        nch = N // 1024
        QE = ["sync", "scalar", "gpsimd", "sync"]
        stg = [self.bq[i][:, 0:2048].bitcast(F32)[:, 0:1024] for i in range(4)]
        stg2 = [self.bk[i][:, 0:2048].bitcast(F32)[:, 0:1024] for i in range(4)]
        def chsrc(c):
            kt, ch = c // nch, c % nch
            return wap[128 * kt:128 * kt + 128, 1024 * ch:1024 * ch + 1024]
        def chdst(c):
            kt, ch = c // nch, c % nch
            if ktiles == 8:
                return self.r[2 * kt + ch // 2][:, :].bitcast(BF16)[:, 1024 * (ch % 2):1024 * (ch % 2) + 1024]
            return self.r[kt // 2][:, :].bitcast(BF16)[:, 1024 * (kt % 2):1024 * (kt % 2) + 1024]
        ngr = (ktiles * nch) // 4
        self.v(lambda e: e.memset(smal[:, 40:41], 0.0))
        for g in range(ngr):
            P.pdma([(QE[j], stg[j], chsrc(4 * g + j)) for j in range(4)])
            cols = (41, 49, 50, 51)
            def reds(e):
                last = None
                for j in range(4):
                    last = e.reduce_sum(smal[:, cols[j]:cols[j] + 1], stg[j], axis=AXX,
                                        apply_absolute_value=True)
                return last
            self.v(reds)
            self.v(lambda e: (e.tensor_tensor(smal[:, 52:53], smal[:, 41:42], smal[:, 49:50], op=ALU.add),
                              e.tensor_tensor(smal[:, 53:54], smal[:, 50:51], smal[:, 51:52], op=ALU.add))[-1])
            self.v(lambda e: e.tensor_tensor(smal[:, 41:42], smal[:, 52:53], smal[:, 53:54], op=ALU.add))
            self.v(lambda e: e.tensor_tensor(smal[:, 40:41], smal[:, 40:41], smal[:, 41:42], op=ALU.add))
        self.ptot(smal[:, 40:41], smal[:, 42:43], 1)
        self.v(lambda e, kt=ktiles: e.tensor_scalar(invsw_col, smal[:, 42:43], 1.0 / (128.0 * kt * N), 1e-5,
                                                    op0=ALU.mult, op1=ALU.max))
        self.v(lambda e: e.reciprocal(sw_col, invsw_col))
        for g in range(ngr):
            P.pdma([(QE[j], stg[j], chsrc(4 * g + j)) for j in range(4)])
            def q12(e, g=g):
                last = None
                for j in range(4):
                    dst = chdst(4 * g + j)
                    e.tensor_scalar(stg2[j], stg[j], sw_col, MAGIC, op0=ALU.mult, op1=ALU.add)
                    e.tensor_scalar(stg2[j], stg2[j], MAGIC, 1.0, op0=ALU.subtract, op1=ALU.min)
                    last = e.tensor_scalar(dst, stg2[j], -1.0, None, op0=ALU.max)
                return last
            self.v(q12)

    def _ffn_chunk(self, c):
        P, IN = self.P, self.IN
        smal = self.smal
        yt, lnf = self.repa, self.repb
        m0, r0, v0 = smal[:, 49:50], smal[:, 50:51], smal[:, 51:52]
        mx, scx, invx = smal[:, 52:53], smal[:, 53:54], smal[:, 54:55]
        P.dma("sync", yt[:, :], self.DSC["y1"][128 * c:128 * c + 128, :])
        self.v(lambda e: e.reduce_sum(m0, yt[:, :], axis=AXX))
        self.v(lambda e: e.tensor_scalar(m0, m0, 1.0 / D, None, op0=ALU.mult))
        self.v(lambda e: e.tensor_scalar(lnf[:, :], yt[:, :], m0, None, op0=ALU.subtract))
        self.s(lambda e: e.activation(yt[:, :], lnf[:, :], AF.Square, accum_out=v0))
        self.v(lambda e: e.tensor_scalar(v0, v0, 1.0 / D, None, op0=ALU.mult))
        self.rsqrt(r0, v0)
        self.v(lambda e: e.tensor_scalar(lnf[:, :], lnf[:, :], r0, None, op0=ALU.mult))
        self.v(lambda e: e.reduce_max(mx, lnf[:, :], axis=AXX, apply_absolute_value=True))
        self.v(lambda e: e.tensor_scalar(mx, mx, 1e-5, None, op0=ALU.max))
        self.v(lambda e: e.tensor_scalar(invx, mx, 1.0 / 127.0, None, op0=ALU.mult))
        self.v(lambda e: e.reciprocal(scx, invx))
        xqs = self.bx[1][:, 0:2048].bitcast(F32)
        xqb = self.bx[0][:, 0:1024]
        self.v(lambda e: e.tensor_scalar(xqs[:, :], lnf[:, :], scx, MAGIC, op0=ALU.mult, op1=ALU.add))
        self.v(lambda e: e.tensor_scalar(xqs[:, :], xqs[:, :], MAGIC, None, op0=ALU.subtract))
        self.v(lambda e: e.tensor_scalar(xqb[:, :], xqs[:, :], invx, None, op0=ALU.mult))
        if c == 0 or c == AT - 1:
            hm = IN["hmask"]
            P.dma("sync", smal[:, 56:58], hm)
            col = smal[:, 56:57] if c == 0 else smal[:, 57:58]
            self.v(lambda e, col=col: e.tensor_scalar(xqb[:, :], xqb[:, :], col, None, op0=ALU.mult))
        for grp in range(2):
            def trs(e, grp=grp):
                last = None
                for q in range(4):
                    kk = 4 * grp + q
                    last = e.transpose(self.pbt[:, 128 * q:128 * q + 128],
                                      xqb[:, 128 * kk:128 * kk + 128], self.eyeb)
                return last
            self.t(trs)
            def cpy(e, grp=grp):
                last = None
                for q in range(4):
                    kk = 4 * grp + q
                    dstb = self.bq[kk] if kk < 4 else self.bk[kk - 4]
                    last = e.tensor_copy(dstb[:, 0:128], self.pbt[:, 128 * q:128 * q + 128])
                return last
            self.v(cpy)
        for half in range(2):
            hraw = self.wb[half]
            def mm1(e, half=half):
                last = None
                for g in range(4):
                    for u in range(4):
                        f = 16 * half + 4 * g + u
                        for kk in range(8):
                            lhs = self.r[2 * kk + f // 16][:, :].bitcast(BF16)[:, 128 * (f % 16):128 * (f % 16) + 128]
                            rhs = (self.bq[kk] if kk < 4 else self.bk[kk - 4])[:, 0:128]
                            last = e.matmul(self.pb[g][:, 128 * u:128 * u + 128], lhs, rhs,
                                            start=(kk == 0), stop=(kk == 7))
                return last
            self.t(mm1)
            def ev1(e, half=half, hraw=hraw):
                last = None
                for g in range(4):
                    last = e.tensor_copy(
                        hraw[:, 0:2080].rearrange("p (f n) -> p f n", n=130)[:, 4 * g:4 * g + 4, 2:130],
                        self.pb[g][:, :].rearrange("p (f n) -> p f n", n=128))
                return last
            self.v(ev1)
        for half in range(2):
            hraw = self.wb[half]
            def mmc(e, half=half, hraw=hraw):
                last = None
                for g in range(4):
                    for u in range(4):
                        fl = 4 * g + u
                        for tap in range(3):
                            last = e.matmul(self.pb[g][:, 128 * u:128 * u + 128],
                                            self._diag(16 * half + fl, tap),
                                            hraw[:, 130 * fl + tap:130 * fl + tap + 128],
                                            start=(tap == 0), stop=(tap == 2))
                return last
            self.t(mmc)
            def sil(e, half=half):
                last = None
                for g in range(4):
                    for u in range(4):
                        f = 16 * half + 4 * g + u
                        dst = self.bv[f // 8][:, :].bitcast(F32)[:, 128 * (f % 8):128 * (f % 8) + 128]
                        last = e.activation(dst, self.pb[g][:, 128 * u:128 * u + 128], AF.Silu,
                                            bias=smal[:, 160 + f:161 + f], scale=smal[:, 44:45])
                return last
            self.s(sil)
        def cset(e):
            e.tensor_copy(self.wb[0][:, 0:2080].rearrange("p (f n) -> p f n", n=130)[:, :, 0:2],
                          self.wb[0][:, 0:2080].rearrange("p (f n) -> p f n", n=130)[:, :, 128:130])
            return e.tensor_copy(self.wb[1][:, 0:2080].rearrange("p (f n) -> p f n", n=130)[:, :, 0:2],
                                 self.wb[1][:, 0:2080].rearrange("p (f n) -> p f n", n=130)[:, :, 128:130])
        self.v(cset)
        w2f = self.wb[2][:, :].bitcast(F32)
        def ab01(e):
            e.activation(w2f[:, 0:1024], self.bv[0][:, :].bitcast(F32)[:, 0:1024], AF.Abs)
            return e.activation(w2f[:, 1024:2048], self.bv[1][:, :].bitcast(F32)[:, 0:1024], AF.Abs)
        self.s(ab01)
        for i in (None, 2, 3):
            if i is not None:
                self.s(lambda e, i=i: e.activation(w2f[:, 1024:2048],
                                                   self.bv[i][:, :].bitcast(F32)[:, 0:1024], AF.Abs))
            self.v(lambda e: e.tensor_tensor(w2f[:, 0:1024], w2f[:, 0:1024], w2f[:, 1024:2048], op=ALU.max))
        self.v(lambda e: e.tensor_tensor(w2f[:, 0:512], w2f[:, 0:512], w2f[:, 512:1024], op=ALU.max))
        self.v(lambda e: e.tensor_tensor(w2f[:, 0:256], w2f[:, 0:256], w2f[:, 256:512], op=ALU.max))
        self.v(lambda e: e.tensor_tensor(w2f[:, 0:128], w2f[:, 0:128], w2f[:, 128:256], op=ALU.max))
        eyef = self.cst2[:, 384:512]
        self.t(lambda e: e.transpose(self.pb[4][:, 0:128], w2f[:, 0:128], eyef))
        hmx, invt, sht = smal[:, 52:53], smal[:, 53:54], smal[:, 54:55]
        self.v(lambda e: e.reduce_max(hmx, self.pb[4][:, 0:128], axis=AXX))
        self.v(lambda e: e.tensor_scalar(hmx, hmx, 1e-5, None, op0=ALU.max))
        self.v(lambda e: e.tensor_scalar(invt, hmx, 1.0 / 127.0, None, op0=ALU.mult))
        self.v(lambda e: e.reciprocal(sht, invt))
        P.dma("sync", self.DSC["invs"][128 * c:128 * c + 128, :], invt)
        self.t(lambda e: e.transpose(self.pb[4][0:1, 128:256], sht, eyef))
        self.v(lambda e: e.tensor_copy(self.st2[0:1, 0:128], self.pb[4][0:1, 128:256]))
        self.t(lambda e: e.matmul(self.pb[4][:, 256:384], self.ones_row, self.st2[0:1, 0:128], start=True, stop=True))
        hqi = self.wb[2][:, :].bitcast(I8)
        def qtt(e):
            last = None
            for f in range(32):
                hsf = self.bv[f // 8][:, :].bitcast(F32)[:, 128 * (f % 8):128 * (f % 8) + 128]
                last = e.tensor_tensor(hqi[:, 128 * f:128 * f + 128], hsf, self.pb[4][:, 256:384], op=ALU.mult)
            return last
        self.v(qtt)
        P.dma("sync", self.DSC["hq"][c:c + 1, :, :, :].rearrange("a p f n -> (a p) (f n)"), hqi[:, 0:4096])
        if c == 8:
            self.dump("xq8", xqb, (128, 1024), BF16)
            self.dump("hraw8a", self.wb[0][:, 0:2080], (128, 2080), BF16)
            self.dump("hs8", self.bv[0][:, :].bitcast(F32)[:, 0:1024], (128, 1024))
            self.v(lambda e: e.tensor_copy(self.st2[:, 300:304] if False else self.repa[:, 0:128], self.pb[4][:, 256:384]))
            self.dump("shb8", self.repa[:, 0:128], (128, 128))

    def _bit2_tile(self, t):
        P, smal = self.P, self.smal
        prs = []
        for half in range(2):
            for sl in range(2):
                cc = t + sl
                dst = self.wb[half][:, :].rearrange("p (f n) -> p f n", n=256)[:, :, 128 * sl:128 * sl + 128]
                if cc < AT:
                    src = self.DSC["hq"][cc:cc + 1, :, 16 * half:16 * half + 16, :].rearrange("a p f n -> (a p) f n")
                    prs.append((dst, src))
                else:
                    self.v(lambda e, dst=dst: e.memset(dst, 0.0))
        P.dmas("gpsimd", prs, slow=True)
        def mm2(e, t=t):
            last = None
            for ft in range(32):
                lhs = self.wb[ft // 16][:, 256 * (ft % 16) + 1:256 * (ft % 16) + 129]
                for nh in range(2):
                    rhs = self.r[ft // 2][:, :].bitcast(BF16)[:, 1024 * (ft % 2) + 512 * nh:1024 * (ft % 2) + 512 * nh + 512]
                    last = e.matmul(self.pb[4 + nh][:, 0:512], lhs, rhs, start=(ft == 0), stop=(ft == 31))
            return last
        self.t(mm2)
        P.dma("sync", smal[:, 47:48], self.DSC["invs"][128 * t + 1:128 * t + 129, :])
        self.v(lambda e: e.tensor_tensor(smal[:, 48:49], smal[:, 47:48], smal[:, 46:47], op=ALU.mult))
        P.dma("sync", self.repa[:, :], self.DSC["y1"][128 * t:128 * t + 128, :])
        def ev2(e):
            e.tensor_scalar(self.repb[:, 0:512], self.pb[4][:, 0:512], smal[:, 48:49], None, op0=ALU.mult)
            return e.tensor_scalar(self.repb[:, 512:1024], self.pb[5][:, 0:512], smal[:, 48:49], None, op0=ALU.mult)
        self.v(ev2)
        self.v(lambda e: e.tensor_tensor(self.repb[:, :], self.repb[:, :], self.repa[:, :], op=ALU.add))
        P.dma("sync", self.DSC["y2"][128 * t:128 * t + 128, :], self.repb[:, :])

    def stageG_mamba(self):
        P, IN, smal, st2 = self.P, self.IN, self.smal, self.st2
        eyef = self.cst2[:, 384:512]
        P.dmas("sync", [(smal[:, 64:132], IN["cw1r"]), (smal[:, 132:149], IN["cb1r"]),
                        (smal[0:32, 58:59], IN["dt_b"]), (smal[0:32, 59:60], IN["a_log"]),
                        (smal[0:32, 60:61], IN["d_par"]), (smal[:, 61:62], IN["xzmask"]),
                        (smal[:, 213:221], IN["pm8"])])
        def ginit(e):
            e.memset(smal[:, 253:254], -1.0)
            e.memset(smal[:, 252:253], 1.0)
            e.memset(self.wb[0][:, 0:2227].rearrange("p (f n) -> p f n", n=131)[:, :, 0:3], 0.0)
            e.memset(self.r[19][:, :], 0.0)
            e.memset(self.bk[3][:, :].bitcast(F32)[0:64, 64:1088], 0.0)
            e.memset(st2[:, 256:288], 1.0)
            return e.memset(st2[:, 288:320], 0.0)
        self.v(ginit)
        self.t(lambda e: e.transpose(self.pb[4][0:1, 0:32], smal[0:32, 59:60], eyef[0:32, 0:32]))
        self.t(lambda e: e.transpose(self.pb[4][0:1, 32:64], smal[0:32, 60:61], eyef[0:32, 0:32]))
        self.v(lambda e: e.tensor_copy(smal[0:1, 149:213], self.pb[4][0:1, 0:64]))
        self.t(lambda e: e.matmul(self.pb[4][:, 64:96], self.ones_row, smal[0:1, 149:181], start=True, stop=True))
        self.t(lambda e: e.matmul(self.pb[4][:, 96:128], self.ones_row, smal[0:1, 181:213], start=True, stop=True))
        self.s(lambda e: e.activation(st2[:, 160:192], self.pb[4][:, 64:96], AF.Exp))
        self.v(lambda e: (e.tensor_scalar(st2[:, 160:192], st2[:, 160:192], -1.0, None, op0=ALU.mult),
                          e.tensor_copy(st2[:, 192:224], self.pb[4][:, 96:128]))[-1])
        def gdiags(e):
            last = None
            for f in range(17):
                for tap in range(4):
                    last = e.tensor_scalar(self._gdiag(f, tap), self.eyeb,
                                           smal[:, 64 + 4 * f + tap:65 + 4 * f + tap], None, op0=ALU.mult)
            return last
        self.v(gdiags)
        P.dmas("gpsimd", [(self.r[kk][:, :].bitcast(BF16), IN["w_in_z"][128 * kk:128 * kk + 128, :])
                          for kk in range(8)])
        P.dmas("gpsimd", [(self.r[8 + ss][:, :].bitcast(BF16), IN["w_in_fm"][:, 2048 * ss:2048 * ss + 2048])
                          for ss in range(9)])
        for i in range(XZT):
            self._g_prologue(i)
            if i >= 1:
                self._g_ssd(i - 1)
        self._g_prefix()
        P.dmas("gpsimd", [(self.r[kt // 2][:, :].bitcast(BF16)[:, 1024 * (kt % 2):1024 * (kt % 2) + 1024],
                           IN["w_out"][128 * kt:128 * kt + 128, :]) for kt in range(16)])
        P.op16("gpsimd", lambda e: e.dma_start(self.wb[2][:, 0:2048], IN["nrm_w"]))
        def hcb(e):
            last = None
            for h in range(32):
                last = e.tensor_copy(self.wb[1][0:64, 64 * h:64 * h + 64], self._sst(h))
            return last
        self.v(hcb)
        for ch in range(16):
            self._g_pass2(ch)
        self.dump("y3full", self.DSC["y3"], (S, D))

    def _gdiag(self, f, tap):
        j = 4 * f + tap
        return self.bq[j // 17][:, 128 + 128 * (j % 17):256 + 128 * (j % 17)]

    def _winfm(self, b):
        return self.r[8 + b // 16][:, :].bitcast(BF16)[:, 128 * (b % 16):128 * (b % 16) + 128]

    def _ln2F(self, kk):
        return (self.bq[kk] if kk < 4 else self.bk[kk - 4])[:, 0:128]

    def _sst(self, h):
        if h < 16:
            return self.r[19][0:64, 64 * h:64 * h + 64]
        return self.bk[3][:, :].bitcast(F32)[0:64, 64 + 64 * (h - 16):128 + 64 * (h - 16)]

    def _stg(self, h):
        return self.bv[h // 16][:, :].bitcast(F32)[0:64, 64 * (h % 16):64 * (h % 16) + 64]

    def _g_prologue(self, i):
        P, IN, smal, st2 = self.P, self.IN, self.smal, self.st2
        yt, lnf = self.repa, self.repb
        m0, r0, v0 = smal[:, 49:50], smal[:, 50:51], smal[:, 51:52]
        P.dma("sync", yt[:, :], self.DSC["y2"][128 * i:128 * i + 128, :])
        self.v(lambda e: e.reduce_sum(m0, yt[:, :], axis=AXX))
        self.v(lambda e: e.tensor_scalar(m0, m0, 1.0 / D, None, op0=ALU.mult))
        self.v(lambda e: e.tensor_scalar(lnf[:, :], yt[:, :], m0, None, op0=ALU.subtract))
        self.s(lambda e: e.activation(yt[:, :], lnf[:, :], AF.Square, accum_out=v0))
        self.v(lambda e: e.tensor_scalar(v0, v0, 1.0 / D, None, op0=ALU.mult))
        self.rsqrt(r0, v0)
        lnb = self.bx[0][:, 0:1024]
        self.v(lambda e: e.tensor_scalar(lnf[:, :], lnf[:, :], r0, None, op0=ALU.mult))
        if i == 0:
            self.v(lambda e: e.tensor_scalar(lnb[:, :], lnf[:, :], smal[:, 61:62], None, op0=ALU.mult))
        else:
            self.v(lambda e: e.tensor_copy(lnb[:, :], lnf[:, :]))
        for grp in range(2):
            def trs(e, grp=grp):
                last = None
                for q in range(4):
                    kk = 4 * grp + q
                    last = e.transpose(self.pbt[:, 128 * q:128 * q + 128],
                                      lnb[:, 128 * kk:128 * kk + 128], self.eyeb)
                return last
            self.t(trs)
            def cpy(e, grp=grp):
                last = None
                for q in range(4):
                    kk = 4 * grp + q
                    last = e.tensor_copy(self._ln2F(kk), self.pbt[:, 128 * q:128 * q + 128])
                return last
            self.v(cpy)
        if i >= 1:
            def mmz(e):
                last = None
                for b in range(4):
                    for kk in range(8):
                        last = e.matmul(self.pb[b][:, 0:512], self._ln2F(kk),
                                        self.r[kk][:, :].bitcast(BF16)[:, 512 * b:512 * b + 512],
                                        start=(kk == 0), stop=(kk == 7))
                return last
            self.t(mmz)
            def evz(e):
                last = None
                for b in range(4):
                    last = e.tensor_copy(self.bx[1][:, 512 * b:512 * b + 512], self.pb[b][:, 0:512])
                return last
            self.v(evz)
        def mmxa(e):
            last = None
            for g in range(4):
                for u in range(4):
                    f = 4 * g + u
                    for kk in range(8):
                        last = e.matmul(self.pb[g][:, 128 * u:128 * u + 128],
                                        self._winfm(17 * kk + f), self._ln2F(kk),
                                        start=(kk == 0), stop=(kk == 7))
            return last
        self.t(mmxa)
        hbc3 = self.wb[0][:, 0:2227].rearrange("p (f n) -> p f n", n=131)
        def evxa(e):
            last = None
            for g in range(4):
                last = e.tensor_copy(hbc3[:, 4 * g:4 * g + 4, 3:131],
                                     self.pb[g][:, :].rearrange("p (f n) -> p f n", n=128))
            return last
        self.v(evxa)
        def mmxb(e):
            last = None
            for kk in range(8):
                last = e.matmul(self.pb[4][:, 0:128], self._winfm(17 * kk + 16), self._ln2F(kk),
                                start=(kk == 0), stop=(kk == 7))
            for kk in range(8):
                last = e.matmul(self.pb[4][:, 128:256], self._winfm(136 + kk), self._ln2F(kk),
                                start=(kk == 0), stop=(kk == 7))
            return last
        self.t(mmxb)
        def evxb(e, i=i):
            e.tensor_copy(self.wb[0][:, 131 * 16 + 3:131 * 16 + 131], self.pb[4][:, 0:128])
            return e.tensor_copy(st2[0:32, 384:512], self.pb[4][0:32, 128:256])
        self.v(evxb)
        def mmca(e):
            last = None
            for g in range(4):
                for u in range(4):
                    f = 4 * g + u
                    for tap in range(4):
                        last = e.matmul(self.pb[g][:, 128 * u:128 * u + 128], self._gdiag(f, tap),
                                        self.wb[0][:, 131 * f + tap:131 * f + tap + 128],
                                        start=(tap == 0), stop=(tap == 3))
            return last
        self.t(mmca)
        def sila(e):
            last = None
            for g in range(4):
                for u in range(4):
                    f = 4 * g + u
                    last = e.activation(self.bk[1][:, 128 * f:128 * f + 128],
                                        self.pb[g][:, 128 * u:128 * u + 128], AF.Silu,
                                        bias=smal[:, 132 + f:133 + f])
            return last
        self.s(sila)
        def mmcb(e):
            last = None
            for tap in range(4):
                last = e.matmul(self.pb[4][:, 0:128], self._gdiag(16, tap),
                                self.wb[0][:, 131 * 16 + tap:131 * 16 + tap + 128],
                                start=(tap == 0), stop=(tap == 3))
            return last
        self.t(mmcb)
        self.s(lambda e: e.activation(self.bk[1][:, 2048:2176], self.pb[4][:, 0:128], AF.Silu,
                                      bias=smal[:, 148:149]))
        self.v(lambda e: e.tensor_copy(hbc3[:, :, 0:3], hbc3[:, :, 128:131]))
        if i >= 1:
            dtF = st2[0:32, 384:512]
            scr = self.bx[0][:, :].bitcast(F32)
            self.v(lambda e: e.tensor_scalar(dtF, dtF, smal[0:32, 58:59], None, op0=ALU.add))
            self.s(lambda e: e.activation(scr[0:32, 0:128], dtF, AF.Abs))
            self.s(lambda e: e.activation(scr[0:32, 128:256], scr[0:32, 0:128], AF.Exp,
                                          scale=smal[0:32, 253:254]))
            self.s(lambda e: e.activation(scr[0:32, 256:384], scr[0:32, 128:256], AF.Ln,
                                          bias=smal[0:32, 252:253]))
            self.v(lambda e: e.tensor_scalar(scr[0:32, 384:512], dtF, 0.0, None, op0=ALU.max))
            self.v(lambda e: e.tensor_tensor(scr[0:32, 0:128], scr[0:32, 256:384], scr[0:32, 384:512], op=ALU.add))
            self.t(lambda e: e.transpose(self.pb[4][:, 256:288], scr[0:32, 0:128],
                                         self.cst2[0:32, 384:416]))
            self.v(lambda e: e.tensor_copy(self.st[:, 0:32], self.pb[4][:, 256:288]))

    def _g_ssd(self, ch):
        P, smal, st2 = self.P, self.smal, self.st2
        eyef = self.cst2[:, 384:512]
        tri01 = self.cstb[:, 514:642]
        ones128 = self.cstb[:, 642:770]
        for grp in range(4):
            def trs(e, grp=grp):
                last = None
                for q in range(4):
                    f = 4 * grp + q
                    last = e.transpose(self.pbt[:, 128 * q:128 * q + 128],
                                      self.bk[1][:, 128 * f:128 * f + 128], self.eyeb)
                return last
            self.t(trs)
            def cpy(e, grp=grp):
                last = None
                for q in range(4):
                    f = 4 * grp + q
                    last = e.tensor_copy(self.bv[1][:, 128 * f:128 * f + 128],
                                         self.pbt[:, 128 * q:128 * q + 128])
                return last
            self.v(cpy)
        def xp(e):
            last = None
            for h in range(32):
                last = e.tensor_scalar(self.bv[0][:, 64 * h:64 * h + 64],
                                       self.bv[1][:, 64 * h:64 * h + 64],
                                       self.st[:, h:h + 1], None, op0=ALU.mult)
            return last
        self.v(xp)
        self.v(lambda e: e.tensor_tensor(st2[:, 224:256], self.st[:, 0:32], st2[:, 160:192], op=ALU.mult))
        self.v(lambda e: e.tensor_copy(self.wb[1][:, 1536:1568], st2[:, 224:256]))
        self.v(lambda e: e.tensor_tensor(self.wb[1][:, 1568:1600], st2[:, 224:256],
                                         self.wb[1][:, 1536:1568], op=ALU.subtract))
        def mmcum(e):
            e.matmul(self.pb[5][:, 0:32], tri01, self.wb[1][:, 1536:1568], start=True, stop=False)
            e.matmul(self.pb[5][:, 0:32], tri01, self.wb[1][:, 1568:1600], start=False, stop=True)
            e.matmul(self.pb[5][:, 32:64], ones128, self.wb[1][:, 1536:1568], start=True, stop=False)
            return e.matmul(self.pb[5][:, 32:64], ones128, self.wb[1][:, 1568:1600], start=False, stop=True)
        self.t(mmcum)
        self.v(lambda e: (e.tensor_copy(st2[:, 0:32], self.pb[5][:, 0:32]),
                          e.tensor_copy(st2[:, 128:160], self.pb[5][:, 32:64]))[-1])
        self.v(lambda e: e.tensor_tensor(st2[:, 352:384], st2[:, 128:160], st2[:, 0:32], op=ALU.subtract))
        def exps(e):
            e.activation(st2[:, 32:64], st2[:, 352:384], AF.Exp)
            e.activation(st2[:, 64:96], st2[:, 0:32], AF.Exp)
            return e.activation(st2[:, 96:128], st2[:, 128:160], AF.Exp)
        self.s(exps)
        self.v(lambda e: e.tensor_tensor(st2[:, 320:352], st2[:, 64:96], st2[:, 256:288], op=ALU.mult))
        P.dma("sync", self.DSC["fac"][ch:ch + 1, :, :].rearrange("a p c -> (a p) c"), st2[:, 320:352])
        self.v(lambda e: (e.tensor_tensor(st2[:, 256:288], st2[:, 256:288], st2[:, 96:128], op=ALU.mult),
                          e.tensor_tensor(st2[:, 288:320], st2[:, 288:320], st2[:, 128:160], op=ALU.add))[-1])
        P.dma("sync", self.bk[1][0:64, 2176:2304], self.bk[1][64:128, 2048:2176])
        self.t(lambda e: e.transpose(self.pbt[:, 0:64], self.bk[1][0:64, 2048:2176], self.eyeb[0:64, 0:64]))
        self.v(lambda e: e.tensor_copy(self.bx[1][:, 2048:2112], self.pbt[:, 0:64]))
        self.t(lambda e: e.matmul(self.pb[5][:, 64:192], self.bk[1][0:64, 2048:2176],
                                  self.bk[1][0:64, 2176:2304], start=True, stop=True))
        self.v(lambda e: e.tensor_tensor(self.pb[5][:, 192:320], self.pb[5][:, 64:192], tri01, op=ALU.mult))
        Ef = self.bx[0][:, :].bitcast(F32)
        for w in range(4):
            def trows(e, w=w):
                last = None
                for hh in range(8):
                    bank = self.pb[4] if hh < 4 else self.pb[0]
                    last = e.transpose(bank[0:1, 128 * (hh % 4):128 * (hh % 4) + 128],
                                       st2[:, 8 * w + hh:8 * w + hh + 1], eyef)
                return last
            self.t(trows)
            def rows(e):
                last = None
                for hh in range(8):
                    bank = self.pb[4] if hh < 4 else self.pb[0]
                    src = bank[0:1, 128 * (hh % 4):128 * (hh % 4) + 128]
                    e.tensor_copy(self.r[17][0:1, 128 * hh:128 * hh + 128], src)
                    last = e.tensor_scalar(self.r[18][0:1, 128 * hh:128 * hh + 128],
                                           src, -1.0, None, op0=ALU.mult)
                return last
            self.v(rows)
            def mmdiff(e):
                last = None
                for hh in range(8):
                    reg = self.pb[2 + hh // 4][:, 128 * (hh % 4):128 * (hh % 4) + 128]
                    e.matmul(reg, self.ones_row, self.r[17][0:1, 128 * hh:128 * hh + 128],
                             start=True, stop=False)
                    last = e.matmul(reg, self.r[18][0:1, 128 * hh:128 * hh + 128], self.ones_row,
                                    start=False, stop=True)
                return last
            self.t(mmdiff)
            def clamp(e):
                last = None
                for hh in range(8):
                    reg = self.pb[2 + hh // 4][:, 128 * (hh % 4):128 * (hh % 4) + 128]
                    last = e.tensor_scalar(reg, reg, 0.0, None, op0=ALU.min)
                return last
            self.v(clamp)
            def expd(e):
                last = None
                for hh in range(8):
                    reg = self.pb[2 + hh // 4][:, 128 * (hh % 4):128 * (hh % 4) + 128]
                    last = e.activation(Ef[:, 128 * hh:128 * hh + 128], reg, AF.Exp)
                return last
            self.s(expd)
            def mt(e):
                last = None
                for hh in range(8):
                    last = e.tensor_tensor(self.wb[1][:, 128 * hh:128 * hh + 128],
                                           Ef[:, 128 * hh:128 * hh + 128],
                                           self.pb[5][:, 192:320], op=ALU.mult)
                return last
            self.v(mt)
            def mmyi(e, w=w):
                last = None
                for hh in range(8):
                    h = 8 * w + hh
                    last = e.matmul(self.pb[0][:, 64 * hh:64 * hh + 64],
                                    self.wb[1][:, 128 * hh:128 * hh + 128],
                                    self.bv[0][:, 64 * h:64 * h + 64], start=True, stop=True)
                return last
            self.t(mmyi)
            def stb(e, w=w):
                last = None
                for hh in range(8):
                    h = 8 * w + hh
                    last = e.tensor_copy(self.wb[1][0:64, 1600 + 64 * hh:1664 + 64 * hh], self._sst(h))
                return last
            self.v(stb)
            def mmyint(e, w=w):
                last = None
                for hh in range(8):
                    last = e.matmul(self.pb[1][:, 64 * hh:64 * hh + 64],
                                    self.bk[1][0:64, 2176:2304],
                                    self.wb[1][0:64, 1600 + 64 * hh:1664 + 64 * hh], start=True, stop=True)
                return last
            self.t(mmyint)
            def bs(e, w=w):
                last = None
                for hh in range(8):
                    h = 8 * w + hh
                    last = e.tensor_scalar(self.wb[1][:, 1024 + 64 * hh:1088 + 64 * hh],
                                           self.bx[1][:, 2048:2112], st2[:, 32 + h:33 + h],
                                           None, op0=ALU.mult)
                return last
            self.v(bs)
            def mmtc(e, w=w):
                last = None
                for hh in range(8):
                    h = 8 * w + hh
                    last = e.matmul(self.pb[4][0:64, 64 * hh:64 * hh + 64],
                                    self.wb[1][:, 1024 + 64 * hh:1088 + 64 * hh],
                                    self.bv[0][:, 64 * h:64 * h + 64], start=True, stop=True)
                return last
            self.t(mmtc)
            def ydst(h):
                return self.bv[2 + h // 16][:, :].bitcast(F32)[:, 64 * (h % 16):64 * (h % 16) + 64]
            def ev1(e, w=w):
                last = None
                for hh in range(8):
                    h = 8 * w + hh
                    last = e.tensor_scalar(ydst(h), self.pb[1][:, 64 * hh:64 * hh + 64],
                                           st2[:, 64 + h:65 + h], None, op0=ALU.mult)
                return last
            self.v(ev1)
            def ev2(e, w=w):
                last = None
                for hh in range(8):
                    h = 8 * w + hh
                    last = e.tensor_tensor(ydst(h), ydst(h), self.pb[0][:, 64 * hh:64 * hh + 64], op=ALU.add)
                return last
            self.v(ev2)
            def ev3(e, w=w):
                last = None
                for hh in range(8):
                    h = 8 * w + hh
                    last = e.scalar_tensor_tensor(ydst(h), self.bv[1][:, 64 * h:64 * h + 64],
                                                  st2[:, 192 + h:193 + h], ydst(h),
                                                  op0=ALU.mult, op1=ALU.add)
                return last
            self.v(ev3)
            def sup1(e, w=w):
                last = None
                for hh in range(8):
                    h = 8 * w + hh
                    last = e.tensor_scalar(self._sst(h), self._sst(h), st2[0:64, 96 + h:97 + h],
                                           None, op0=ALU.mult)
                return last
            self.v(sup1)
            def sup2(e, w=w):
                last = None
                for hh in range(8):
                    h = 8 * w + hh
                    last = e.tensor_tensor(self._sst(h), self._sst(h),
                                           self.pb[4][0:64, 64 * hh:64 * hh + 64], op=ALU.add)
                return last
            self.v(sup2)
        if ch == 7:
            self.dump("xbc8", self.bk[1][:, 0:2176], (128, 2176), BF16)
            self.dump("dt8", self.st[:, 0:32], (128, 32))
            self.dump("z8", self.bx[1][:, 0:2048], (128, 2048), BF16)
            self.dump("yloc8a", self.bv[2][:, :].bitcast(F32)[:, 0:1024], (128, 1024))
            self.dump("yloc8b", self.bv[3][:, :].bitcast(F32)[:, 0:1024], (128, 1024))
        P.dmas("gpsimd", [(self.DSC["Y"][128 * ch:128 * ch + 128, 0:1024], self.bv[2][:, :].bitcast(F32)[:, 0:1024]),
                          (self.DSC["Y"][128 * ch:128 * ch + 128, 1024:2048], self.bv[3][:, :].bitcast(F32)[:, 0:1024])])
        P.dma("sync", self.DSC["z"][128 * ch:128 * ch + 128, :], self.bx[1][:, 0:2048])
        P.dma("sync", self.DSC["cc"][ch:ch + 1, :, :].rearrange("a p n -> (a p) n"),
              self.bk[1][64:128, 2048:2176])

    def _g_prefix(self):
        P, CC, smal, st2 = self.P, self.CC, self.smal, self.st2
        self.dump("sloc_a", self.r[19][0:64, 0:1024], (64, 1024))
        self.dump("sloc_b", self.bk[3][:, :].bitcast(F32)[0:64, 64:1088], (64, 1024))
        P.dmas("sync", [(CC["mb_in"].ap()[0:1, 4096 * h:4096 * h + 4096].rearrange("a (s d) -> (a s) d", s=64),
                         self._sst(h)) for h in range(32)])
        self.s(lambda e: e.activation(st2[:, 320:352], st2[:, 288:320], AF.Exp))
        self.dump("gamv", st2[:, 320:352], (128, 32))
        P.dma("sync", CC["mb_in"].ap()[0:1, 131072:131104], st2[0:1, 320:352])
        self.gp(lambda e: e.collective_compute(
            "AllGather", ALU.bypass, replica_groups=[list(range(NC))],
            ins=[CC["mb_in"].ap().opt()], outs=[CC["mb_out"].ap().opt()]))
        def hz(e):
            last = None
            for h in range(32):
                last = e.memset(self._sst(h), 0.0)
            return last
        self.v(hz)
        for cc in range(7):
            P.dmas_mq([(self._stg(h),
                        CC["mb_out"].ap()[cc:cc + 1, 4096 * h:4096 * h + 4096].rearrange("a (s d) -> (a s) d", s=64))
                       for h in range(32)] +
                      [(smal[0:1, 181:213], CC["mb_out"].ap()[cc:cc + 1, 131072:131104])])
            self.t(lambda e: e.matmul(self.pb[5][:, 320:352], self.ones_row, smal[0:1, 181:213],
                                      start=True, stop=True))
            self.v(lambda e: e.tensor_scalar(st2[:, 352:384], self.pb[5][:, 320:352], 1.0, None, op0=ALU.subtract))
            self.v(lambda e, cc=cc: e.tensor_scalar(st2[:, 352:384], st2[:, 352:384],
                                                    smal[:, 213 + cc:214 + cc], None, op0=ALU.mult))
            self.v(lambda e: e.tensor_scalar(st2[:, 352:384], st2[:, 352:384], 1.0, None, op0=ALU.add))
            def hsc(e):
                last = None
                for h in range(32):
                    last = e.tensor_scalar(self._sst(h), self._sst(h), st2[0:64, 352 + h:353 + h],
                                           None, op0=ALU.mult)
                return last
            self.v(hsc)
            def hup(e, cc=cc):
                last = None
                for h in range(32):
                    last = e.scalar_tensor_tensor(self._sst(h), self._stg(h),
                                                  smal[0:64, 213 + cc:214 + cc], self._sst(h),
                                                  op0=ALU.mult, op1=ALU.add)
                return last
            self.v(hup)

    def _g_pass2(self, ch):
        P, smal, st2 = self.P, self.smal, self.st2
        P.dmas("gpsimd", [(self.bv[2][:, :].bitcast(F32)[:, 0:1024], self.DSC["Y"][128 * ch:128 * ch + 128, 0:1024]),
                          (self.bv[3][:, :].bitcast(F32)[:, 0:1024], self.DSC["Y"][128 * ch:128 * ch + 128, 1024:2048])])
        P.dmas("sync", [(self.bx[1][:, 0:2048], self.DSC["z"][128 * ch:128 * ch + 128, :]),
                        (self.bk[1][0:64, 2048:2176], self.DSC["cc"][ch:ch + 1, :, :].rearrange("a p n -> (a p) n")),
                        (st2[:, 0:32], self.DSC["fac"][ch:ch + 1, :, :].rearrange("a p c -> (a p) c"))])
        def mmcor(e):
            last = None
            for h in range(32):
                last = e.matmul(self.pb[h // 8][:, 64 * (h % 8):64 * (h % 8) + 64],
                                self.bk[1][0:64, 2048:2176], self.wb[1][0:64, 64 * h:64 * h + 64],
                                start=True, stop=True)
            return last
        self.t(mmcor)
        def ydst(h):
            return self.bv[2 + h // 16][:, :].bitcast(F32)[:, 64 * (h % 16):64 * (h % 16) + 64]
        def corr(e):
            last = None
            for h in range(32):
                last = e.scalar_tensor_tensor(ydst(h), self.pb[h // 8][:, 64 * (h % 8):64 * (h % 8) + 64],
                                              st2[:, h:h + 1], ydst(h), op0=ALU.mult, op1=ALU.add)
            return last
        self.v(corr)
        sz = self.bx[0][:, 0:2048]
        self.s(lambda e: e.activation(sz, self.bx[1][:, 0:2048], AF.Silu))
        self.v(lambda e: (e.tensor_tensor(self.bv[2][:, :].bitcast(F32)[:, 0:1024],
                                          self.bv[2][:, :].bitcast(F32)[:, 0:1024], sz[:, 0:1024], op=ALU.mult),
                          e.tensor_tensor(self.bv[3][:, :].bitcast(F32)[:, 0:1024],
                                          self.bv[3][:, :].bitcast(F32)[:, 0:1024], sz[:, 1024:2048], op=ALU.mult))[-1])
        va, vb, rs = smal[:, 49:50], smal[:, 50:51], smal[:, 51:52]
        def sq(e):
            e.activation(self.repa[:, :], self.bv[2][:, :].bitcast(F32)[:, 0:1024], AF.Square, accum_out=va)
            return e.activation(self.repa[:, :], self.bv[3][:, :].bitcast(F32)[:, 0:1024], AF.Square, accum_out=vb)
        self.s(sq)
        self.v(lambda e: e.tensor_tensor(va, va, vb, op=ALU.add))
        self.v(lambda e: e.tensor_scalar(va, va, 1.0 / 2048.0, None, op0=ALU.mult))
        self.rsqrt(rs, va)
        self.v(lambda e: (e.tensor_scalar(self.bv[2][:, :].bitcast(F32)[:, 0:1024],
                                          self.bv[2][:, :].bitcast(F32)[:, 0:1024], rs, None, op0=ALU.mult),
                          e.tensor_scalar(self.bv[3][:, :].bitcast(F32)[:, 0:1024],
                                          self.bv[3][:, :].bitcast(F32)[:, 0:1024], rs, None, op0=ALU.mult))[-1])
        yn = self.bx[0][:, 0:2048]
        self.v(lambda e: (e.tensor_tensor(yn[:, 0:1024], self.bv[2][:, :].bitcast(F32)[:, 0:1024],
                                          self.wb[2][:, 0:1024], op=ALU.mult),
                          e.tensor_tensor(yn[:, 1024:2048], self.bv[3][:, :].bitcast(F32)[:, 0:1024],
                                          self.wb[2][:, 1024:2048], op=ALU.mult))[-1])
        for grp in range(4):
            def trs(e, grp=grp):
                last = None
                for q in range(4):
                    kt = 4 * grp + q
                    last = e.transpose(self.pbt[:, 128 * q:128 * q + 128],
                                      yn[:, 128 * kt:128 * kt + 128], self.eyeb)
                return last
            self.t(trs)
            def cpy(e, grp=grp):
                last = None
                for q in range(4):
                    kt = 4 * grp + q
                    last = e.tensor_copy(self.bk[0][:, 128 * kt:128 * kt + 128],
                                         self.pbt[:, 128 * q:128 * q + 128])
                return last
            self.v(cpy)
        def mmo3(e):
            last = None
            for nh in range(2):
                for kt in range(16):
                    last = e.matmul(self.pb[4 + nh][:, 0:512], self.bk[0][:, 128 * kt:128 * kt + 128],
                                    self.r[kt // 2][:, :].bitcast(BF16)[:, 1024 * (kt % 2) + 512 * nh:1024 * (kt % 2) + 512 * nh + 512],
                                    start=(kt == 0), stop=(kt == 15))
            return last
        self.t(mmo3)
        P.dma("sync", self.repa[:, :], self.DSC["y2"][128 * (ch + 1):128 * (ch + 1) + 128, :])
        self.v(lambda e: (e.tensor_tensor(self.repb[:, 0:512], self.pb[4][:, 0:512], self.repa[:, 0:512], op=ALU.add),
                          e.tensor_tensor(self.repb[:, 512:1024], self.pb[5][:, 0:512], self.repa[:, 512:1024], op=ALU.add))[-1])
        P.dma("sync", self.DSC["y3"][128 * ch:128 * ch + 128, :], self.repb[:, :])


    def stageH_moe(self):
        P, IN, CC, smal, st2 = self.P, self.IN, self.CC, self.smal, self.st2
        eyef = self.cst2[:, 384:512]
        for t in range(16):
            yt, lnf = self.repa, self.repb
            m0, r0, v0 = smal[:, 49:50], smal[:, 50:51], smal[:, 51:52]
            P.dma("sync", yt[:, :], self.DSC["y3"][128 * t:128 * t + 128, :])
            self.v(lambda e: e.reduce_sum(m0, yt[:, :], axis=AXX))
            self.v(lambda e: e.tensor_scalar(m0, m0, 1.0 / D, None, op0=ALU.mult))
            self.v(lambda e: e.tensor_scalar(lnf[:, :], yt[:, :], m0, None, op0=ALU.subtract))
            self.s(lambda e: e.activation(yt[:, :], lnf[:, :], AF.Square, accum_out=v0))
            self.v(lambda e: e.tensor_scalar(v0, v0, 1.0 / D, None, op0=ALU.mult))
            self.rsqrt(r0, v0)
            self.v(lambda e: e.tensor_scalar(lnf[:, :], lnf[:, :], r0, None, op0=ALU.mult))
            P.dma("scalar", self.DSC["acc"][128 * t:128 * t + 128, :], lnf[:, :])
            def mmsum(e, t=t):
                e.matmul(self.pb[4][0:1, 0:512], self.ones_col, lnf[:, 0:512],
                         start=(t == 0), stop=(t == 15))
                return e.matmul(self.pb[5][0:1, 0:512], self.ones_col, lnf[:, 512:1024],
                                start=(t == 0), stop=(t == 15))
            self.t(mmsum)
            lnb = self.bx[0][:, 0:1024]
            self.v(lambda e: e.tensor_copy(lnb[:, :], lnf[:, :]))
            for grp in range(2):
                def trs(e, grp=grp):
                    last = None
                    for q in range(4):
                        kk = 4 * grp + q
                        last = e.transpose(self.pbt[:, 128 * q:128 * q + 128],
                                          lnb[:, 128 * kk:128 * kk + 128], self.eyeb)
                    return last
                self.t(trs)
                def cpy(e, grp=grp):
                    last = None
                    for q in range(4):
                        kk = 4 * grp + q
                        last = e.tensor_copy(self.bx[1][:, 128 * kk:128 * kk + 128],
                                             self.pbt[:, 128 * q:128 * q + 128])
                    return last
                self.v(cpy)
            P.dma("sync", self.DSC["ln3F"][t:t + 1, :, :, :].rearrange("a k p n -> (a p) k n"),
                  self.bx[1][:, 0:1024].rearrange("p (k n) -> p k n", n=128))
        self.v(lambda e: (e.tensor_copy(self.repa[0:1, 0:512], self.pb[4][0:1, 0:512]),
                          e.tensor_copy(self.repa[0:1, 512:1024], self.pb[5][0:1, 0:512]))[-1])
        P.dma("sync", CC["mu_in"].ap()[0:1, :], self.repa[0:1, :])
        self.gp(lambda e: e.collective_compute(
            "AllReduce", ALU.add, replica_groups=[list(range(NC))],
            ins=[CC["mu_in"].ap().opt()], outs=[CC["mu_out"].ap().opt()]))
        P.dma("sync", smal[:, 96:104], CC["mu_out"].ap()[0:1, :].rearrange("a (k p) -> (a p) k", p=128), slow=True)
        self.v(lambda e: e.tensor_scalar(smal[:, 96:104], smal[:, 96:104], 1.0 / T, None, op0=ALU.mult))
        P.dma("sync", smal[:, 104:168].rearrange("p (k e) -> p k e", e=8),
              IN["w_r"].rearrange("(k p) e -> p k e", p=128), slow=True)
        def mmlg(e):
            last = None
            for kt in range(8):
                last = e.matmul(self.pb[0][0:8, 0:1], smal[:, 104 + 8 * kt:112 + 8 * kt],
                                smal[:, 96 + kt:97 + kt], start=(kt == 0), stop=(kt == 7))
            return last
        self.t(mmlg)
        P.dma("sync", smal[0:8, 168:169], IN["b_r"].rearrange("a e -> e a"), slow=True)
        self.v(lambda e: e.tensor_tensor(smal[0:8, 169:170], self.pb[0][0:8, 0:1], smal[0:8, 168:169], op=ALU.add))
        self.t(lambda e: e.transpose(self.pb[0][0:1, 4:12], smal[0:8, 169:170], eyef[0:8, 0:8]))
        self.v(lambda e: e.tensor_copy(smal[0:1, 170:178], self.pb[0][0:1, 4:12]))
        self.s(lambda e: e.activation(smal[0:1, 178:186], smal[0:1, 170:178], AF.Exp))
        self.v(lambda e: e.reduce_sum(smal[0:1, 186:187], smal[0:1, 178:186], axis=AXX))
        self.v(lambda e: e.reciprocal(smal[0:1, 187:188], smal[0:1, 186:187]))
        self.v(lambda e: e.tensor_scalar(smal[0:1, 188:196], smal[0:1, 178:186], smal[0:1, 187:188], None, op0=ALU.mult))
        self.v(lambda e: e.reduce_max(smal[0:1, 196:197], smal[0:1, 188:196], axis=AXX))
        self.v(lambda e: e.memset(smal[0:1, 197:198], NEG))
        self.v(lambda e: e.tensor_scalar(smal[0:1, 198:206], smal[0:1, 188:196], smal[0:1, 196:197], None, op0=ALU.is_equal))
        self.v(lambda e: e.scalar_tensor_tensor(smal[0:1, 206:214], smal[0:1, 198:206], smal[0:1, 197:198],
                                                smal[0:1, 188:196], op0=ALU.mult, op1=ALU.add))
        self.v(lambda e: e.reduce_max(smal[0:1, 214:215], smal[0:1, 206:214], axis=AXX))
        self.v(lambda e: e.tensor_scalar(smal[0:1, 215:223], smal[0:1, 206:214], smal[0:1, 214:215], None, op0=ALU.is_equal))
        self.v(lambda e: (e.tensor_copy(smal[0:1, 223:224], smal[0:1, 196:197]),
                          e.tensor_copy(smal[0:1, 224:225], smal[0:1, 214:215]))[-1])
        self.s(lambda e: e.activation(smal[0:1, 225:227], smal[0:1, 223:225], AF.Exp))
        self.v(lambda e: e.reduce_sum(smal[0:1, 227:228], smal[0:1, 225:227], axis=AXX))
        self.v(lambda e: e.reciprocal(smal[0:1, 228:229], smal[0:1, 227:228]))
        self.v(lambda e: e.tensor_scalar(smal[0:1, 229:231], smal[0:1, 225:227], smal[0:1, 228:229], None, op0=ALU.mult))
        self.v(lambda e: (e.tensor_tensor(smal[0:1, 231:239], smal[0:1, 198:206], self.cst[0:1, 0:8], op=ALU.mult),
                          e.tensor_tensor(smal[0:1, 239:247], smal[0:1, 215:223], self.cst[0:1, 0:8], op=ALU.mult))[-1])
        self.v(lambda e: (e.reduce_sum(smal[0:1, 92:93], smal[0:1, 231:239], axis=AXX),
                          e.reduce_sum(smal[0:1, 93:94], smal[0:1, 239:247], axis=AXX))[-1])
        def bcasts(e):
            e.matmul(self.pb[0][:, 0:1], self.ones_row, smal[0:1, 92:93], start=True, stop=True)
            e.matmul(self.pb[0][:, 1:2], self.ones_row, smal[0:1, 93:94], start=True, stop=True)
            e.matmul(self.pb[0][:, 2:3], self.ones_row, smal[0:1, 229:230], start=True, stop=True)
            return e.matmul(self.pb[0][:, 3:4], self.ones_row, smal[0:1, 230:231], start=True, stop=True)
        self.t(bcasts)
        self.v(lambda e: e.tensor_copy(smal[:, 231:235], self.pb[0][:, 0:4]))
        def selcols(e):
            e.transpose(self.pb[0][0:8, 16:17], smal[0:1, 198:206], eyef[0:1, 0:1])
            return e.transpose(self.pb[0][0:8, 17:18], smal[0:1, 215:223], eyef[0:1, 0:1])
        self.t(selcols)
        self.v(lambda e: e.tensor_copy(smal[0:8, 238:240], self.pb[0][0:8, 16:18]))
        self.v(lambda e: e.tensor_copy(self.bx[1][0:8, 2548:2550], smal[0:8, 238:240]))
        self.v(lambda e: e.tensor_copy(smal[:, 236:237], self.cstb[:, 513:514]))
        self.v(lambda e: e.tensor_scalar(smal[:, 236:237], smal[:, 236:237], 1.0 / 128.0, None, op0=ALU.mult))
        self.dump("router", smal[0:1, 170:247], (1, 77))
        for ee in range(2):
            self._moe_expert(ee, t_sel=238 + ee, t_e=231 + ee, t_wt=233 + ee)
        ga = self.bq[0][:, :].bitcast(F32)[:, 0:1024]
        be = self.bq[1][:, :].bitcast(F32)[:, 0:1024]
        P.dmas("sync", [(ga, IN["gamma"]), (be, IN["beta"])])
        for t in range(16):
            at, lnf = self.repa, self.repb
            m0, r0, v0 = smal[:, 49:50], smal[:, 50:51], smal[:, 51:52]
            P.dma("sync", at[:, :], self.DSC["acc"][128 * t:128 * t + 128, :])
            self.v(lambda e: e.reduce_sum(m0, at[:, :], axis=AXX))
            self.v(lambda e: e.tensor_scalar(m0, m0, 1.0 / D, None, op0=ALU.mult))
            self.v(lambda e: e.tensor_scalar(lnf[:, :], at[:, :], m0, None, op0=ALU.subtract))
            self.s(lambda e: e.activation(at[:, :], lnf[:, :], AF.Square, accum_out=v0))
            self.v(lambda e: e.tensor_scalar(v0, v0, 1.0 / D, None, op0=ALU.mult))
            self.rsqrt(r0, v0)
            self.v(lambda e: e.tensor_scalar(lnf[:, :], lnf[:, :], r0, None, op0=ALU.mult))
            self.v(lambda e: e.tensor_tensor(lnf[:, :], lnf[:, :], ga, op=ALU.mult))
            self.v(lambda e: e.tensor_tensor(lnf[:, :], lnf[:, :], be, op=ALU.add))
            P.dma("sync", at[:, :], self.DSC["y3"][128 * t:128 * t + 128, :])
            self.v(lambda e: e.tensor_tensor(lnf[:, :], lnf[:, :], at[:, :], op=ALU.add))
            P.dma("sync", self.out[128 * t:128 * t + 128, :], lnf[:, :])

    def _w2slot(self, kt2):
        bufs = ([(self.bq[i], o) for i in range(4) for o in (0, 1024)]
                + [(self.bk[i], o) for i in range(4) for o in (0, 1024)]
                + [(self.bv[i], o) for i in range(4) for o in (0, 1024)]
                + [(self.wb[0], o) for o in (0, 1024, 2048, 3072)]
                + [(self.r[16][:, :].bitcast(BF16), o) for o in (0, 1024)]
                + [(self.r[17][:, :].bitcast(BF16), o) for o in (0, 1024)])
        b, o = bufs[kt2]
        return b[:, o:o + 1024]

    def _moe_expert(self, ee, t_sel, t_e, t_wt):
        P, IN, smal, idx = self.P, self.IN, self.smal, self.idx
        selT = self.bk[0][0:8, 2048:2176]
        self.t(lambda e: e.matmul(self.pb[1][0:8, 0:128], smal[0:1, (198 if ee == 0 else 215):(206 if ee == 0 else 223)],
                                  self.ones_row, start=True, stop=True))
        P.op16("gpsimd", lambda e: e.dma_start(self.wb[1][0:8, 0:1024], IN["moe_b2"]))
        self.v(lambda e: e.tensor_copy(selT, self.pb[1][0:8, 0:128]))
        selb = self.bx[1][0:8, 2548 + ee:2549 + ee]
        def mmb2(e, selb=selb):
            e.matmul(self.pb[2][0:1, 0:512], selb, self.wb[1][0:8, 0:512], start=True, stop=True)
            return e.matmul(self.pb[3][0:1, 0:512], selb, self.wb[1][0:8, 512:1024], start=True, stop=True)
        self.t(mmb2)
        self.v(lambda e: (e.tensor_copy(self.bv[0][0:1, 2048:2560], self.pb[2][0:1, 0:512]),
                          e.tensor_copy(self.bv[1][0:1, 2048:2560], self.pb[3][0:1, 0:512]))[-1])
        P.op16("gpsimd", lambda e: e.dma_start(self.wb[1][0:8, 0:4096], IN["moe_b1"]))
        self.v(lambda e, t_e=t_e: e.tensor_scalar(smal[:, 235:236], smal[:, t_e:t_e + 1], 1024.0, None, op0=ALU.mult))
        self.v(lambda e: e.tensor_tensor(smal[:, 235:236], smal[:, 235:236], smal[:, 236:237], op=ALU.add))
        def i1(e):
            last = None
            for kt in range(8):
                last = e.tensor_scalar(smal[:, 72 + kt:73 + kt], smal[:, 235:236], float(128 * kt), None, op0=ALU.add)
            return last
        self.v(i1)
        self.v(lambda e: e.tensor_copy(idx[:, 0:8], smal[:, 72:80]))
        for kt in range(8):
            P.op16("gpsimd", lambda e, kt=kt: e.indirect_dma_start(
                self.wb[2][:, :], None, IN["moe_w1"],
                IndirectOffsetOnAxis(ap=idx[:, kt:kt + 1], axis=0)))
            self.v(lambda e, kt=kt: (
                e.tensor_copy(self.r[2 * kt][:, :].bitcast(BF16), self.wb[2][:, 0:2048]),
                e.tensor_copy(self.r[2 * kt + 1][:, :].bitcast(BF16), self.wb[2][:, 2048:4096]))[-1])
        self.v(lambda e, t_e=t_e: e.tensor_scalar(smal[:, 235:236], smal[:, t_e:t_e + 1], 4096.0, None, op0=ALU.mult))
        self.v(lambda e: e.tensor_tensor(smal[:, 235:236], smal[:, 235:236], smal[:, 236:237], op=ALU.add))
        def i2(e):
            last = None
            for kt in range(32):
                last = e.tensor_scalar(smal[:, 72 + kt:73 + kt], smal[:, 235:236], float(128 * kt), None, op0=ALU.add)
            return last
        self.v(i2)
        self.v(lambda e: e.tensor_copy(idx[:, 0:32], smal[:, 72:104]))
        for kt in range(32):
            P.op16("gpsimd", lambda e, kt=kt: e.indirect_dma_start(
                self._w2slot(kt), None, IN["moe_w2"],
                IndirectOffsetOnAxis(ap=idx[:, kt:kt + 1], axis=0)))
        for t in range(16):
            P.dma("sync", self.bx[0][:, 0:1024].rearrange("p (k n) -> p k n", n=128),
                  self.DSC["ln3F"][t:t + 1, :, :, :].rearrange("a k p n -> (a p) k n"))
            for half in range(2):
                def mmh1(e, half=half):
                    last = None
                    for b in range(4):
                        for kk in range(8):
                            e.matmul(self.pb[b][:, 0:512], self.bx[0][:, 128 * kk:128 * kk + 128],
                                     self.r[2 * kk + half][:, :].bitcast(BF16)[:, 512 * b:512 * b + 512],
                                     start=(kk == 0), stop=False)
                        last = e.matmul(self.pb[b][:, 0:512], selT,
                                        self.wb[1][0:8, 2048 * half + 512 * b:2048 * half + 512 * b + 512],
                                        start=False, stop=True)
                    return last
                self.t(mmh1)
                def gel(e, half=half):
                    last = None
                    for b in range(4):
                        last = e.activation(self.wb[2][:, 2048 * half + 512 * b:2048 * half + 512 * b + 512],
                                            self.pb[b][:, 0:512], AF.Gelu)
                    return last
                self.s(gel)
            for grp in range(8):
                def trs(e, grp=grp):
                    last = None
                    for q in range(4):
                        kt2 = 4 * grp + q
                        last = e.transpose(self.pbt[:, 128 * q:128 * q + 128],
                                          self.wb[2][:, 128 * kt2:128 * kt2 + 128], self.eyeb)
                    return last
                self.t(trs)
                def cpy(e, grp=grp):
                    last = None
                    for q in range(4):
                        kt2 = 4 * grp + q
                        dst = (self.bx[0] if kt2 < 16 else self.bx[1])[:, 128 * (kt2 % 16):128 * (kt2 % 16) + 128]
                        last = e.tensor_copy(dst, self.pbt[:, 128 * q:128 * q + 128])
                    return last
                self.v(cpy)
            def mmo(e):
                last = None
                for nh in range(2):
                    for kt2 in range(32):
                        lhs = (self.bx[0] if kt2 < 16 else self.bx[1])[:, 128 * (kt2 % 16):128 * (kt2 % 16) + 128]
                        e.matmul(self.pb[4 + nh][:, 0:512], lhs,
                                 self._w2slot(kt2)[:, 512 * nh:512 * nh + 512],
                                 start=(kt2 == 0), stop=False)
                    last = e.matmul(self.pb[4 + nh][:, 0:512], self.onesb_row,
                                    self.bv[nh][0:1, 2048:2560], start=False, stop=True)
                return last
            self.t(mmo)
            P.dma("sync", self.repa[:, :], self.DSC["acc"][128 * t:128 * t + 128, :])
            def accu(e, t_wt=t_wt):
                e.scalar_tensor_tensor(self.repa[:, 0:512], self.pb[4][:, 0:512], smal[:, t_wt:t_wt + 1],
                                       self.repa[:, 0:512], op0=ALU.mult, op1=ALU.add)
                return e.scalar_tensor_tensor(self.repa[:, 512:1024], self.pb[5][:, 0:512], smal[:, t_wt:t_wt + 1],
                                              self.repa[:, 512:1024], op0=ALU.mult, op1=ALU.add)
            self.v(accu)
            P.dma("sync", self.DSC["acc"][128 * t:128 * t + 128, :], self.repa[:, :])


def kernel(**inputs):
    shared, per_core = _host_stage(inputs)
    nc, dbg_aps = build_program(debug=DEBUG)
    in_maps = []
    for c in range(NC):
        m = dict(shared)
        m.update(per_core[c])
        in_maps.append({k: np.ascontiguousarray(v, dtype=np.float32) for k, v in m.items()})
    res = run_bass_kernel_spmd(nc, in_maps, core_ids=list(range(NC)))
    kernel.last_results = res.results
    outs = [np.asarray(res.results[c]["out"]) for c in range(NC)]
    return np.concatenate(outs, 0).reshape(1, T, D)
